# version 03
# baseline (speedup 1.0000x reference)
import numpy as np
import ml_dtypes
from contextlib import ExitStack

import concourse.bass as bass
import concourse.mybir as mybir
from concourse.bass_utils import run_bass_kernel_spmd

F32 = mybir.dt.float32
BF16 = mybir.dt.bfloat16
AF = mybir.ActivationFunctionType
ALU = mybir.AluOpType
AX = mybir.AxisListType

BD, SQ, CM = 4, 4096, 1024
NCORE = 8
TC = 2048
LN_EPS = 1e-5
ATTN_EPS = 1e-6
BF = ml_dtypes.bfloat16
ENGS = ("sync", "tensor", "vector", "scalar", "gpsimd")


class Q:
    def __init__(self, name):
        self.name = name
        self.n = 0
        self.items = []
        self._w = []

    def wait(self, q, val):
        if val and val > 0:
            self._w.append((q, val))

    def op(self, fn, inc=1):
        self.items.append((fn, inc, self._w))
        self._w = []
        self.n += inc
        return self.n

    def dma(self, fn):
        return self.op(fn, 16)

    def quiet(self, fn):
        self.items.append((fn, 0, self._w))
        self._w = []
        return self.n


class Sch:
    def __init__(self):
        self.q = {e: Q(e) for e in ENGS}

    def finalize(self, nc, stack):
        sems = {e: stack.enter_context(nc.semaphore(f"sem_{e}")) for e in ENGS}
        block = stack.enter_context(nc.Block())

        def mk_replay(qq):
            def replay(eng):
                for fn, inc, waits in qq.items:
                    for (wq, val) in waits:
                        eng.wait_ge(sems[wq.name], val)
                    inst = fn(eng)
                    if inc:
                        inst.then_inc(sems[qq.name], inc)
            return replay

        for e in ENGS:
            if self.q[e].items:
                getattr(block, e)(mk_replay(self.q[e]))


def _bfa(x):
    return np.ascontiguousarray(np.asarray(x)).astype(BF)


def _f32(x):
    return np.ascontiguousarray(np.asarray(x), dtype=np.float32)


def pack_lhsT(wT, octile=128):
    I, O = wT.shape
    assert I % 128 == 0 and O % octile == 0
    a = wT.reshape(I // 128, 128, O // octile, octile)
    a = np.transpose(a, (2, 1, 0, 3))
    return np.ascontiguousarray(a.reshape(O // octile, 128, (I // 128) * octile))



def build_attn0(dbg=False):
    nc = bass.Bass("TRN2", target_bir_lowering=False)
    xall = nc.dram_tensor("xall", [CM, SQ], F32, kind="ExternalInput").ap()
    fil = nc.dram_tensor("fil", [512, 16], F32, kind="ExternalInput").ap()
    eye = nc.dram_tensor("eye", [128, 128], BF16, kind="ExternalInput").ap()
    onesall = nc.dram_tensor("onesall", [128, 128], BF16, kind="ExternalInput").ap()
    xn_o = nc.dram_tensor("xn_o", [512, SQ], BF16, kind="ExternalOutput").ap()
    sp_o = nc.dram_tensor("sp_o", [512, SQ], BF16, kind="ExternalOutput").ap()

    S = Sch()
    q, t, v, s, g = (S.q[e] for e in ENGS)

    with ExitStack() as stack:
        A16 = stack.enter_context(nc.sbuf_tensor("a16", [128, 76000], BF16))
        A32 = stack.enter_context(nc.sbuf_tensor("a32", [128, 12700], F32))
        pb = [stack.enter_context(nc.psum_tensor(f"pb{i}", [128, 512], F32)) for i in range(7)]
        pbt = stack.enter_context(nc.psum_tensor("pbt", [128, 1024], BF16))

        def b16(off, n):
            return A16[:, off:off + n]

        def f32(off, n):
            return A32[:, off:off + n]

        eye_sb = b16(0, 128)
        ones_sb = b16(128, 128)
        kv_sb = b16(256, 512)
        xsq = b16(768, 4096)
        xoth = b16(4864, 4096)
        qb = [b16(768 + 1024 * i, 1024) for i in range(4)]
        qT = [b16(4864 + 512 * i, 512) for i in range(8)]
        vT = [b16(8960 + 512 * i, 512) for i in range(8)]
        num_sb = b16(13056, 1024)
        attn = [b16(14080 + 1026 * i, 1026) for i in range(4)]
        xn = [b16(18200 + 4100 * i, 4100) for i in range(4)]
        low1 = [b16(34600 + 2052 * i, 2052) for i in range(4)]
        high1 = [b16(34600 + 2052 * (4 + i), 2052) for i in range(4)]
        abc = b16(34600, 4096)
        cbc = b16(34600 + 4096, 4096)
        low2 = [b16(51016 + 1028 * i, 1028) for i in range(4)]
        high2 = [b16(51016 + 1028 * (4 + i), 1028) for i in range(4)]
        syn1 = [b16(59240 + 2052 * i, 2052) for i in range(4)]
        spre0 = b16(67448, 4096)
        spre1 = b16(67448 + 4096, 4096)
        spre = [spre0, spre1]

        xin = f32(0, 4096)
        acc1 = f32(4200, 2048)
        acc2 = f32(4200, 4096)
        fil32 = [f32(8300 + 16 * i, 16) for i in range(4)]
        syn32 = [f32(8364 + 20 * i, 20) for i in range(4)]
        c_lneps = f32(8444, 1)
        c_mhalf = f32(8445, 1)

        srow = A32[0:1, 4200:8296]
        qrow = A32[0:1, 8500:12596]
        tmp_row = A32[0:1, 0:4096]
        a_rowb = A16[0:1, 14080:18176]
        c_rowb = A16[0:1, 67448:71544]
        denf = [A32[0:1, 0:4096], A32[0:1, 8500:12596]]
        den_row = A16[0:1, 67448:75640]

        c_eye = q.dma(lambda e: e.dma_start(out=eye_sb, in_=eye[:]))
        c_ones = q.dma(lambda e: e.dma_start(out=ones_sb, in_=onesall[:]))
        for ct in range(4):
            q.dma(lambda e, ct=ct: e.dma_start(out=fil32[ct][0:128, :], in_=fil[128 * ct:128 * (ct + 1), :]))
        c_fil = q.n

        v.wait(q, c_fil)
        for ct in range(4):
            fl, sy = fil32[ct], syn32[ct]
            for bse, d in ((8, 0), (12, 5)):
                v.quiet(lambda e, fl=fl, sy=sy, b=bse, d=d: e.tensor_copy(sy[:, d:d + 1], fl[:, b:b + 1]))
                v.quiet(lambda e, fl=fl, sy=sy, b=bse, d=d: e.tensor_add(sy[:, d + 1:d + 2], fl[:, b + 1:b + 2], fl[:, b + 2:b + 3]))
                v.quiet(lambda e, fl=fl, sy=sy, b=bse, d=d: e.tensor_copy(sy[:, d + 2:d + 3], fl[:, b + 3:b + 4]))
                v.quiet(lambda e, fl=fl, sy=sy, b=bse, d=d: e.tensor_add(sy[:, d + 3:d + 4], fl[:, b:b + 1], fl[:, b + 1:b + 2]))
                v.quiet(lambda e, fl=fl, sy=sy, b=bse, d=d: e.tensor_add(sy[:, d + 4:d + 5], fl[:, b + 2:b + 3], fl[:, b + 3:b + 4]))
        v.quiet(lambda e: e.memset(srow, 0.0))
        v.quiet(lambda e: e.memset(qrow, 0.0))
        v.quiet(lambda e: e.memset(c_lneps, LN_EPS))
        v.quiet(lambda e: e.memset(c_mhalf, -0.5))
        pads = []
        for ct in range(4):
            pads += [(xn[ct], 0, 1), (xn[ct], 4097, 4100),
                     (low2[ct], 0, 1), (low2[ct], 1025, 1028), (high2[ct], 0, 1), (high2[ct], 1025, 1028),
                     (syn1[ct], 0, 1), (syn1[ct], 2049, 2052)]
        for tile, a, b in pads[:-1]:
            v.quiet(lambda e, tile=tile, a=a, b=b: e.memset(tile[:, a:b], 0.0))
        tile, a, b = pads[-1]
        c_init = v.op(lambda e, tile=tile, a=a, b=b: e.memset(tile[:, a:b], 0.0))

        c_cast = {}
        c_mm_ct = {}
        ev_tok = {}
        q.wait(q, c_fil)
        for ct in range(8):
            q.wait(v, c_cast.get(ct - 1, 0))
            c_x = q.dma(lambda e, ct=ct: e.dma_start(out=xin[0:128, :], in_=xall[128 * ct:128 * (ct + 1), :]))
            bt = xn[ct][:, 1:4097] if ct < 4 else xoth
            v.wait(q, c_x)
            if ct >= 5:
                v.wait(t, c_mm_ct[ct - 1])
            c_cast[ct] = v.op(lambda e, bt=bt: e.tensor_copy(bt, xin[0:128, :]))
            s.wait(v, c_cast[ct])
            if ct >= 1:
                s.wait(t, c_mm_ct[ct - 1])
            c_sqr = s.op(lambda e, bt=bt: e.activation(xsq, bt, AF.Square))
            t.wait(s, c_sqr)
            if ct == 0:
                t.wait(q, c_ones)
                t.wait(v, c_init)
            for j in range(8):
                m = ct * 8 + j
                bs, bq = pb[j % 2], pb[2 + j % 2]
                if m >= 2:
                    t.wait(v, ev_tok[m - 2])
                t.quiet(lambda e, j=j, bt=bt, bs=bs: e.matmul(
                    bs[:, 0:512], ones_sb, bt[:, 512 * j:512 * (j + 1)], start=True, stop=True))
                c_mm = t.op(lambda e, j=j, bq=bq: e.matmul(
                    bq[:, 0:512], ones_sb, xsq[:, 512 * j:512 * (j + 1)], start=True, stop=True))
                v.wait(t, c_mm)
                v.quiet(lambda e, j=j, bs=bs: e.tensor_add(srow[:, 512 * j:512 * (j + 1)], srow[:, 512 * j:512 * (j + 1)], bs[0:1, 0:512]))
                ev_tok[m] = v.op(lambda e, j=j, bq=bq: e.tensor_add(qrow[:, 512 * j:512 * (j + 1)], qrow[:, 512 * j:512 * (j + 1)], bq[0:1, 0:512]))
            c_mm_ct[ct] = t.n

        v.quiet(lambda e: e.tensor_scalar_mul(srow, srow, 1.0 / CM))
        v.quiet(lambda e: e.tensor_scalar_mul(qrow, qrow, 1.0 / CM))
        v.quiet(lambda e: e.tensor_tensor(tmp_row, srow, srow, ALU.mult))
        c_var = v.op(lambda e: e.tensor_tensor(qrow, qrow, tmp_row, ALU.subtract))
        s.wait(v, c_var)
        s.quiet(lambda e: e.activation(qrow, qrow, AF.Ln, bias=c_lneps[0:1, 0:1]))
        c_rstd = s.op(lambda e: e.activation(qrow, qrow, AF.Exp, scale=c_mhalf[0:1, 0:1]))
        v.wait(s, c_rstd)
        v.quiet(lambda e: e.tensor_tensor(tmp_row, srow, qrow, ALU.mult))
        v.quiet(lambda e: e.tensor_copy(a_rowb, qrow))
        c_rowsb = v.op(lambda e: e.tensor_copy(c_rowb, tmp_row))

        t.wait(v, c_rowsb)
        for j in range(8):
            c_ba = t.op(lambda e, j=j: e.matmul(pb[4][:, 0:512], ones_sb[0:1, 0:128], a_rowb[0:1, 512 * j:512 * (j + 1)], start=True, stop=True))
            s.wait(t, c_ba)
            c_ea = s.op(lambda e, j=j: e.copy(abc[:, 512 * j:512 * (j + 1)], pb[4][:, 0:512]))
            t.wait(s, c_ea)
            c_bc2 = t.op(lambda e, j=j: e.matmul(pb[4][:, 0:512], ones_sb[0:1, 0:128], c_rowb[0:1, 512 * j:512 * (j + 1)], start=True, stop=True))
            s.wait(t, c_bc2)
            c_ec = s.op(lambda e, j=j: e.copy(cbc[:, 512 * j:512 * (j + 1)], pb[4][:, 0:512]))
            t.wait(s, c_ec)
        c_bcast = s.n

        v.wait(s, c_bcast)
        for ct in range(4):
            v.quiet(lambda e, ct=ct: e.tensor_tensor(xn[ct][:, 1:4097], xn[ct][:, 1:4097], abc, ALU.mult))
            c_app = v.op(lambda e, ct=ct: e.tensor_tensor(xn[ct][:, 1:4097], xn[ct][:, 1:4097], cbc, ALU.subtract))
            q.wait(v, c_app)
            q.dma(lambda e, ct=ct: e.dma_start(out=xn_o[128 * ct:128 * (ct + 1), :], in_=xn[ct][:, 1:4097]))
        c_xn_done = v.n

        for ct in range(4):
            for tile, a, b in ((low1[ct], 0, 1), (low1[ct], 2049, 2052),
                               (high1[ct], 0, 1), (high1[ct], 2049, 2052),
                               (attn[ct], 0, 1), (attn[ct], 1025, 1026)):
                v.quiet(lambda e, tile=tile, a=a, b=b: e.memset(tile[:, a:b], 0.0))

        def conv(dst, src, filc, n_out, acc):
            for k in range(4):
                sl = src[:, k:k + 2 * n_out:2]
                if k == 0:
                    v.quiet(lambda e, sl=sl, filc=filc: e.tensor_scalar_mul(acc[:, 0:n_out], sl, filc[:, 0:1]))
                elif k < 3:
                    v.quiet(lambda e, sl=sl, filc=filc, k=k: e.scalar_tensor_tensor(acc[:, 0:n_out], sl, filc[:, k:k + 1], acc[:, 0:n_out], ALU.mult, ALU.add))
                else:
                    v.quiet(lambda e, sl=sl, filc=filc, k=k: e.scalar_tensor_tensor(dst[:, 1:1 + n_out], sl, filc[:, k:k + 1], acc[:, 0:n_out], ALU.mult, ALU.add))

        for ct in range(4):
            conv(low1[ct], xn[ct], fil32[ct][0:128, 0:4], 2048, acc1)
            conv(high1[ct], xn[ct], fil32[ct][0:128, 4:8], 2048, acc1)
        for ct in range(4):
            conv(low2[ct], low1[ct], fil32[ct][0:128, 0:4], 1024, acc1)
            conv(high2[ct], low1[ct], fil32[ct][0:128, 4:8], 1024, acc1)
        c_pyr = v.op(lambda e: e.memset(denf[0][0:1, 0:1], 0.0))

        for ct in range(4):
            h_d = low2[ct][:, 1:1025]
            c_qm = v.op(lambda e, h_d=h_d, ct=ct: e.tensor_scalar_min(qb[ct], h_d, 0.0))
            s.wait(v, c_qm)
            c_qe = s.op(lambda e, ct=ct: e.activation(qb[ct], qb[ct], AF.Exp))
            v.wait(s, c_qe)
            v.op(lambda e, h_d=h_d, ct=ct: e.scalar_tensor_tensor(qb[ct], h_d, 0.0, qb[ct], ALU.max, ALU.add))
        c_q = v.n

        t.wait(v, c_q)
        t.wait(q, c_eye)
        c_ev_t = 0
        for src_is_q in (True, False):
            for ct in range(4):
                for pt in range(8):
                    reg = pbt[:, 128 * (pt % 2):128 * (pt % 2) + 128]
                    t.wait(s, c_ev_t)
                    if src_is_q:
                        src = qb[ct][:, 128 * pt:128 * (pt + 1)]
                    else:
                        src = low2[ct][:, 1 + 128 * pt:1 + 128 * (pt + 1)]
                    c_tp = t.op(lambda e, src=src, reg=reg: e.transpose(reg, src, eye_sb))
                    s.wait(t, c_tp)
                    dst = (qT if src_is_q else vT)[pt][:, 128 * ct:128 * (ct + 1)]
                    c_ev_t = s.op(lambda e, dst=dst, reg=reg: e.copy(dst, reg))
        c_trans = c_ev_t

        t.wait(s, c_trans)
        for h in range(8):
            for pt in range(8):
                lhsT = qT[pt][:, 64 * h:64 * (h + 1)]
                rhs = vT[pt][:, 64 * h:64 * (h + 1)]
                out = pb[4][64 * (h % 2):64 * (h % 2) + 64, 64 * (h // 2):64 * (h // 2) + 64]
                last = (pt == 7 and h == 7)
                fn = lambda e, lhsT=lhsT, rhs=rhs, out=out, pt=pt: e.matmul(
                    out, lhsT, rhs, start=(pt == 0), stop=(pt == 7))
                c_kv = t.op(fn) if last else t.quiet(fn)
        s.wait(t, c_kv)
        c_kvsb = s.op(lambda e: e.copy(kv_sb, pb[4][:, 0:512]))
        if dbg:
            dkv = nc.dram_tensor("d_kv", [128, 512], BF16, kind="ExternalOutput").ap()
            dqt = nc.dram_tensor("d_qt0", [128, 512], BF16, kind="ExternalOutput").ap()
            dvt = nc.dram_tensor("d_vt0", [128, 512], BF16, kind="ExternalOutput").ap()
            q.wait(s, c_kvsb)
            q.dma(lambda e: e.dma_start(out=dkv[:], in_=kv_sb))
            q.dma(lambda e: e.dma_start(out=dqt[:], in_=qT[0]))
            q.dma(lambda e: e.dma_start(out=dvt[:], in_=vT[0]))

        v.wait(s, c_kvsb)
        c_red = {}
        for ct in range(4):
            c_red[ct] = v.op(lambda e, ct=ct: e.tensor_reduce(A32[:, 12600 + ct:12601 + ct], qb[ct], AX.X, ALU.add))
        c_z = 0
        for ct in range(4):
            v.wait(v, c_red[ct])
            c_z = v.op(lambda e, ct=ct: e.tensor_copy(A16[:, 13056 + ct:13057 + ct], A32[:, 12600 + ct:12601 + ct]))

        t.wait(v, c_z)
        for h in range(8):
            ct, par = h // 2, h % 2
            for ch in range(2):
                lhsT = A16[64 * par:64 * par + 64, 13056 + ct:13057 + ct]
                rhs = A16[64 * par:64 * par + 64, 768 + 1024 * ct + 512 * ch:768 + 1024 * ct + 512 * (ch + 1)]
                c_dmm = t.op(lambda e, lhsT=lhsT, rhs=rhs: e.matmul(
                    pb[5][0:1, 0:512], lhsT, rhs, start=True, stop=True))
                s.wait(t, c_dmm)
                dst = denf[h // 4][0:1, 1024 * (h % 4) + 512 * ch:1024 * (h % 4) + 512 * (ch + 1)]
                c_dev = s.op(lambda e, dst=dst: e.copy(dst, pb[5][0:1, 0:512]))
                t.wait(s, c_dev)

        v.wait(s, s.n)
        for r in range(2):
            ca = v.op(lambda e, r=r: e.tensor_scalar_add(denf[r], denf[r], ATTN_EPS))
            v.wait(v, ca)
            cb = v.op(lambda e, r=r: e.reciprocal(denf[r], denf[r]))
            v.wait(v, cb)
            v.op(lambda e, r=r: e.tensor_copy(den_row[0:1, 4096 * r:4096 * (r + 1)], denf[r]))
        c_recip = v.n
        if dbg:
            dden = nc.dram_tensor("d_den", [1, 8192], BF16, kind="ExternalOutput").ap()
            dz = nc.dram_tensor("d_z", [128, 4], BF16, kind="ExternalOutput").ap()
            q.wait(v, c_recip)
            q.dma(lambda e: e.dma_start(out=dden[:], in_=den_row[0:1, 0:8192]))
            q.dma(lambda e: e.dma_start(out=dz[:], in_=num_sb[:, 0:4]))

        t.wait(v, c_recip)
        c_nev = 0
        for h in range(8):
            ct, par = h // 2, h % 2
            for ch in range(2):
                lhsT = kv_sb[64 * par:64 * par + 64, 64 * (h // 2):64 * (h // 2) + 64]
                rhs = qb[ct][64 * par:64 * par + 64, 512 * ch:512 * (ch + 1)]
                t.wait(v, c_nev)
                t.quiet(lambda e, lhsT=lhsT, rhs=rhs, par=par: e.matmul(
                    pb[5][64 * par:64 * par + 64, 0:512], lhsT, rhs, start=True, stop=True))
                c_nmm = t.op(lambda e, h=h, ch=ch, par=par: e.matmul(
                    pb[6][64 * par:64 * par + 64, 0:512], ones_sb[0:1, 0:64],
                    den_row[0:1, 1024 * h + 512 * ch:1024 * h + 512 * (ch + 1)], start=True, stop=True))
                s.wait(t, c_nmm)
                c_ns = s.op(lambda e, par=par: e.copy(num_sb[64 * par:64 * par + 64, 512:1024], pb[5][64 * par:64 * par + 64, 0:512]))
                v.wait(s, c_ns)
                c_nev = v.op(lambda e, ct=ct, par=par, ch=ch: e.tensor_tensor(
                    attn[ct][64 * par:64 * par + 64, 1 + 512 * ch:1 + 512 * (ch + 1)],
                    num_sb[64 * par:64 * par + 64, 512:1024],
                    pb[6][64 * par:64 * par + 64, 0:512], ALU.mult))
        c_attn = c_nev

        def synth(dst, A, H, sy, n2, extra=None):
            aE, aO = acc2[:, 0:n2], acc2[:, n2:2 * n2]
            v.quiet(lambda e, A=A, sy=sy, aE=aE: e.tensor_scalar_mul(aE, A[:, 0:n2], sy[:, 0:1]))
            v.quiet(lambda e, A=A, sy=sy, aE=aE: e.scalar_tensor_tensor(aE, A[:, 1:1 + n2], sy[:, 1:2], aE, ALU.mult, ALU.add))
            v.quiet(lambda e, A=A, sy=sy, aE=aE: e.scalar_tensor_tensor(aE, A[:, 2:2 + n2], sy[:, 2:3], aE, ALU.mult, ALU.add))
            v.quiet(lambda e, H=H, sy=sy, aE=aE: e.scalar_tensor_tensor(aE, H[:, 0:n2], sy[:, 5:6], aE, ALU.mult, ALU.add))
            v.quiet(lambda e, H=H, sy=sy, aE=aE: e.scalar_tensor_tensor(aE, H[:, 1:1 + n2], sy[:, 6:7], aE, ALU.mult, ALU.add))
            v.quiet(lambda e, A=A, sy=sy, aO=aO: e.tensor_scalar_mul(aO, A[:, 1:1 + n2], sy[:, 3:4]))
            v.quiet(lambda e, A=A, sy=sy, aO=aO: e.scalar_tensor_tensor(aO, A[:, 2:2 + n2], sy[:, 4:5], aO, ALU.mult, ALU.add))
            v.quiet(lambda e, H=H, sy=sy, aO=aO: e.scalar_tensor_tensor(aO, H[:, 1:1 + n2], sy[:, 8:9], aO, ALU.mult, ALU.add))
            if extra is None:
                v.quiet(lambda e, H=H, sy=sy, aE=aE, dst=dst, n2=n2: e.scalar_tensor_tensor(
                    dst[:, 1:1 + 2 * n2:2], H[:, 2:2 + n2], sy[:, 7:8], aE, ALU.mult, ALU.add))
                return v.op(lambda e, H=H, sy=sy, aO=aO, dst=dst, n2=n2: e.scalar_tensor_tensor(
                    dst[:, 2:2 + 2 * n2:2], H[:, 2:2 + n2], sy[:, 9:10], aO, ALU.mult, ALU.add))
            else:
                v.quiet(lambda e, H=H, sy=sy, aE=aE: e.scalar_tensor_tensor(aE, H[:, 2:2 + n2], sy[:, 7:8], aE, ALU.mult, ALU.add))
                v.quiet(lambda e, H=H, sy=sy, aO=aO: e.scalar_tensor_tensor(aO, H[:, 2:2 + n2], sy[:, 9:10], aO, ALU.mult, ALU.add))
                v.quiet(lambda e, dst=dst, aE=aE, extra=extra, n2=n2: e.tensor_tensor(
                    dst[:, 0:2 * n2:2], aE, extra[:, 1:1 + 2 * n2:2], ALU.add))
                return v.op(lambda e, dst=dst, aO=aO, extra=extra, n2=n2: e.tensor_tensor(
                    dst[:, 1:2 * n2:2], aO, extra[:, 2:2 + 2 * n2:2], ALU.add))

        v.wait(s, s.n)
        for ct in range(4):
            synth(syn1[ct], attn[ct], high2[ct], syn32[ct], 1024)
        c_sp_dma = {}
        for ct in range(4):
            par = ct % 2
            if ct >= 2:
                v.wait(q, c_sp_dma[ct - 2])
            c_sp = synth(spre[par], syn1[ct], high1[ct], syn32[ct], 2048, extra=xn[ct])
            q.wait(v, c_sp)
            c_sp_dma[ct] = q.dma(lambda e, ct=ct, par=par: e.dma_start(
                out=sp_o[128 * ct:128 * (ct + 1), :], in_=spre[par]))
        if dbg:
            q.wait(v, v.n)
            q.wait(s, s.n)
            for name, tile in (("d_low1", low1[0][:, 1:2049]), ("d_high1", high1[0][:, 1:2049]),
                               ("d_low2", low2[0][:, 1:1025]), ("d_high2", high2[0][:, 1:1025]),
                               ("d_q", qb[0]), ("d_attn", attn[0][:, 1:1025]),
                               ("d_syn1", syn1[0][:, 1:2049])):
                dt = nc.dram_tensor(name, list(tile.shape), BF16, kind="ExternalOutput").ap()
                q.dma(lambda e, dt=dt, tile=tile: e.dma_start(out=dt[:], in_=tile))

        g.wait(q, q.n)
        g.op(lambda e: e.memset(acc1[0:1, 0:1], 0.0))

        S.finalize(nc, stack)
    return nc


def prep_attn0(inp):
    x = _f32(inp["L0_attn_x"])
    filT = np.concatenate(
        [_f32(inp[k]).T for k in ("L0_attn_h_filter", "L0_attn_g_filter",
                                  "L0_attn_h_synth", "L0_attn_g_synth")], axis=1)
    eye = np.eye(128, dtype=np.float32).astype(BF)
    ones = np.ones((128, 128), dtype=BF)
    maps = []
    for c in range(NCORE):
        b, hc = c // 2, c % 2
        xt = x[b].T
        xall = np.ascontiguousarray(np.concatenate(
            [xt[512 * hc:512 * (hc + 1)], xt[512 * (1 - hc):512 * (2 - hc)]], axis=0))
        maps.append({"xall": xall,
                     "fil": np.ascontiguousarray(filT[512 * hc:512 * (hc + 1)]),
                     "eye": eye, "onesall": ones})
    return maps


def run_attn0(inp, dbg=False):
    nc = build_attn0(dbg)
    res = run_bass_kernel_spmd(nc, prep_attn0(inp), core_ids=list(range(NCORE)))
    xn = np.empty((BD, CM, SQ), dtype=BF)
    sp = np.empty((BD, CM, SQ), dtype=BF)
    for c in range(NCORE):
        b, hc = c // 2, c % 2
        xn[b, 512 * hc:512 * (hc + 1)] = np.asarray(res.results[c]["xn_o"])
        sp[b, 512 * hc:512 * (hc + 1)] = np.asarray(res.results[c]["sp_o"])
    return xn, sp, res



def build_ffn0():
    nc = bass.Bass("TRN2", target_bir_lowering=False)
    x_i = nc.dram_tensor("x_i", [CM, TC], F32, kind="ExternalInput").ap()
    xn_i = nc.dram_tensor("xn_i", [CM, TC], BF16, kind="ExternalInput").ap()
    sp_i = nc.dram_tensor("sp_i", [CM, TC], BF16, kind="ExternalInput").ap()
    gb_i = nc.dram_tensor("gb_i", [CM, 2], F32, kind="ExternalInput").ap()
    scal_i = nc.dram_tensor("scal_i", [128, 136], F32, kind="ExternalInput").ap()
    onesall = nc.dram_tensor("onesall", [128, 128], BF16, kind="ExternalInput").ap()
    w1s = nc.dram_tensor("w1s", [32, 128, 1024], F32, kind="ExternalInput").ap()
    w2s = nc.dram_tensor("w2s", [32, 128, 1024], F32, kind="ExternalInput").ap()
    h1s = nc.dram_tensor("h1s", [32, 128, 4096], F32, kind="ExternalInput").ap()
    h2s = nc.dram_tensor("h2s", [32, 128, 4096], F32, kind="ExternalInput").ap()
    w3s = nc.dram_tensor("w3s", [8, 128, 4096], F32, kind="ExternalInput").ap()
    x1_o = nc.dram_tensor("x1_o", [CM, TC], F32, kind="ExternalOutput").ap()

    S = Sch()
    q, t, v, s, g = (S.q[e] for e in ENGS)
    NG, G = 4, 512

    with ExitStack() as stack:
        A16 = stack.enter_context(nc.sbuf_tensor("a16", [128, 84200], BF16))
        A32 = stack.enter_context(nc.sbuf_tensor("a32", [128, 8360], F32))
        pb = [stack.enter_context(nc.psum_tensor(f"pb{i}", [128, 512], F32)) for i in range(8)]

        def b16(off, n):
            return A16[:, off:off + n]

        xn = [b16(2048 * i, 2048) for i in range(8)]
        sb = [b16(16384 + 2048 * i, 2048) for i in range(8)]
        t1 = [b16(32768 + 512 * j, 512) for j in range(32)]
        t2 = [b16(49152 + 512 * j, 512) for j in range(32)]
        gbuf = [b16(65536 + 512 * i, 512) for i in range(8)]
        wcast = [b16(69632 + 4096 * i, 4096) for i in range(2)]
        abc = b16(77824, 2048)
        cbc = b16(79872, 2048)
        xsq = b16(81920, 2048)
        ones_sb = b16(83968, 128)
        a_rowb = A16[0:1, 32768:34816]
        c_rowb = A16[0:1, 34816:36864]

        wraw = A32[:, 0:4096]
        xbuf = A32[:, 4096:8192]
        scal = A32[:, 8192:8328]
        c_lneps = A32[:, 8328:8329]
        c_mhalf = A32[:, 8329:8330]
        gbc = A32[:, 8332:8348]
        srow = A32[0:1, 4096:6144]
        qrow = A32[0:1, 6144:8192]
        tmp_row = A32[0:1, 0:2048]

        q.dma(lambda e: e.dma_start(out=ones_sb, in_=onesall[:]))
        q.dma(lambda e: e.dma_start(out=scal, in_=scal_i[:]))
        for ct in range(8):
            q.dma(lambda e, ct=ct: e.dma_start(out=gbc[:, 2 * ct:2 * ct + 2], in_=gb_i[128 * ct:128 * (ct + 1), :]))
        for ct in range(8):
            q.dma(lambda e, ct=ct: e.dma_start(out=xn[ct], in_=xn_i[128 * ct:128 * (ct + 1), :]))
        for ct in range(8):
            q.dma(lambda e, ct=ct: e.dma_start(out=sb[ct], in_=sp_i[128 * ct:128 * (ct + 1), :]))
        c_in = q.n
        q.wait(q, c_in)

        v.wait(q, c_in)
        v.quiet(lambda e: e.memset(srow, 0.0))
        v.quiet(lambda e: e.memset(qrow, 0.0))
        v.quiet(lambda e: e.memset(c_lneps, LN_EPS))
        c_init = v.op(lambda e: e.memset(c_mhalf, -0.5))

        ev_tok = {}
        c_sq_prev = 0
        for ct in range(8):
            if ct == 0:
                s.wait(v, c_init)
            s.wait(t, c_sq_prev)
            c_sqr = s.op(lambda e, ct=ct: e.activation(xsq, sb[ct], AF.Square))
            t.wait(s, c_sqr)
            if ct == 0:
                t.wait(v, c_init)
            for j in range(4):
                m = ct * 4 + j
                bs_, bq_ = pb[j % 2], pb[2 + j % 2]
                if m >= 2:
                    t.wait(v, ev_tok[m - 2])
                t.quiet(lambda e, j=j, ct=ct, bs_=bs_: e.matmul(
                    bs_[:, 0:512], ones_sb, sb[ct][:, 512 * j:512 * (j + 1)], start=True, stop=True))
                c_mm = t.op(lambda e, j=j, bq_=bq_: e.matmul(
                    bq_[:, 0:512], ones_sb, xsq[:, 512 * j:512 * (j + 1)], start=True, stop=True))
                v.wait(t, c_mm)
                v.quiet(lambda e, j=j, bs_=bs_: e.tensor_add(srow[:, 512 * j:512 * (j + 1)], srow[:, 512 * j:512 * (j + 1)], bs_[0:1, 0:512]))
                ev_tok[m] = v.op(lambda e, j=j, bq_=bq_: e.tensor_add(qrow[:, 512 * j:512 * (j + 1)], qrow[:, 512 * j:512 * (j + 1)], bq_[0:1, 0:512]))
            c_sq_prev = t.n

        v.quiet(lambda e: e.tensor_scalar_mul(srow, srow, 1.0 / CM))
        v.quiet(lambda e: e.tensor_scalar_mul(qrow, qrow, 1.0 / CM))
        v.quiet(lambda e: e.tensor_tensor(tmp_row, srow, srow, ALU.mult))
        c_var = v.op(lambda e: e.tensor_tensor(qrow, qrow, tmp_row, ALU.subtract))
        s.wait(v, c_var)
        s.quiet(lambda e: e.activation(qrow, qrow, AF.Ln, bias=c_lneps[0:1, 0:1]))
        c_rstd = s.op(lambda e: e.activation(qrow, qrow, AF.Exp, scale=c_mhalf[0:1, 0:1]))
        v.wait(s, c_rstd)
        v.quiet(lambda e: e.tensor_tensor(tmp_row, srow, qrow, ALU.mult))
        v.quiet(lambda e: e.tensor_copy(a_rowb, qrow))
        c_rowsb = v.op(lambda e: e.tensor_copy(c_rowb, tmp_row))
        t.wait(v, c_rowsb)
        for j in range(4):
            for src, dst in ((a_rowb, abc), (c_rowb, cbc)):
                c_b1 = t.op(lambda e, j=j, src=src: e.matmul(
                    pb[4][:, 0:512], ones_sb[0:1, 0:128], src[0:1, 512 * j:512 * (j + 1)], start=True, stop=True))
                s.wait(t, c_b1)
                c_e1 = s.op(lambda e, j=j, dst=dst: e.copy(dst[:, 512 * j:512 * (j + 1)], pb[4][:, 0:512]))
                t.wait(s, c_e1)
        c_bcast = s.n

        v.wait(s, c_bcast)
        for ct in range(8):
            v.quiet(lambda e, ct=ct: e.tensor_tensor(sb[ct], sb[ct], abc, ALU.mult))
            v.quiet(lambda e, ct=ct: e.tensor_tensor(sb[ct], sb[ct], cbc, ALU.subtract))
            v.quiet(lambda e, ct=ct: e.tensor_scalar(
                sb[ct], sb[ct], gbc[:, 2 * ct:2 * ct + 1], gbc[:, 2 * ct + 1:2 * ct + 2], ALU.mult, ALU.add))
        c_sdone = v.op(lambda e: e.memset(tmp_row[0:1, 0:1], 0.0))

        c_comb_prev = c_sdone
        c_store_prev = 0
        c_prev_mm = 0
        c_cast_carry = 0
        for gi in range(NG):
            gs = G * gi

            tok_w = []
            ev1 = []
            for j in range(32):
                q.wait(v, c_cast_carry)
                c_w1 = q.dma(lambda e, j=j: e.dma_start(out=wraw[:, 0:1024], in_=w1s[j]))
                v.wait(q, c_w1)
                if tok_w:
                    v.wait(t, tok_w[-1])
                elif j == 0:
                    v.wait(t, c_prev_mm)
                c_c1 = v.op(lambda e: e.tensor_copy(wcast[0][:, 0:1024], wraw[:, 0:1024]))
                q.wait(v, c_c1)
                c_w2 = q.dma(lambda e, j=j: e.dma_start(out=wraw[:, 0:1024], in_=w2s[j]))
                v.wait(q, c_w2)
                c_c2 = v.op(lambda e: e.tensor_copy(wcast[1][:, 0:1024], wraw[:, 0:1024]))
                c_cast_carry = c_c2
                t.wait(v, c_c2)
                if gi == 0 and j == 0:
                    t.wait(q, c_in)
                    t.wait(v, c_sdone)
                if ev1:
                    t.wait(v, ev1[-1])
                else:
                    t.wait(v, c_comb_prev)
                for k in range(8):
                    t.quiet(lambda e, k=k, gs=gs: e.matmul(
                        pb[4][:, 0:G], wcast[0][:, 128 * k:128 * (k + 1)], xn[k][:, gs:gs + G],
                        start=(k == 0), stop=(k == 7)))
                for k in range(8):
                    fn = lambda e, k=k, gs=gs: e.matmul(
                        pb[5][:, 0:G], wcast[1][:, 128 * k:128 * (k + 1)], xn[k][:, gs:gs + G],
                        start=(k == 0), stop=(k == 7))
                    tok_w.append(t.op(fn)) if k == 7 else t.quiet(fn)
                s.wait(t, tok_w[-1])
                s.quiet(lambda e, j=j: e.activation(t1[j], pb[4][:, 0:G], AF.Silu, scale=scal[:, j:j + 1]))
                c_ev2 = s.op(lambda e, j=j: e.activation(xsq[:, 0:G], pb[5][:, 0:G], AF.Copy, scale=scal[:, 32 + j:33 + j]))
                v.wait(s, c_ev2)
                ev1.append(v.op(lambda e, j=j: e.tensor_tensor(t1[j], t1[j], xsq[:, 0:G], ALU.mult)))

            def mk_hidden(wdram, scal_off, src_list, dst_list, first_wait):
                nonlocal c_cast_carry
                toks, evs = [], []
                for j in range(32):
                    par = j % 2
                    q.wait(v, c_cast_carry)
                    c_w = q.dma(lambda e, j=j, wdram=wdram: e.dma_start(out=wraw[:, 0:4096], in_=wdram[j]))
                    v.wait(q, c_w)
                    if len(toks) >= 2:
                        v.wait(t, toks[-2])
                    elif j == 0:
                        v.wait(t, tok_w[-1] if wdram is h1s else hid_prev_mm[0])
                    c_cast = v.op(lambda e, par=par: e.tensor_copy(wcast[par][:, 0:4096], wraw[:, 0:4096]))
                    c_cast_carry = c_cast
                    t.wait(v, c_cast)
                    if j == 0:
                        eng, tokv = first_wait
                        t.wait(eng, tokv)
                    bank = pb[4 + par]
                    if len(evs) >= 2:
                        t.wait(s, evs[-2])
                    for k in range(32):
                        fn = lambda e, k=k, par=par, bank=bank: e.matmul(
                            bank[:, 0:G], wcast[par][:, 128 * k:128 * (k + 1)], src_list[k][:, 0:G],
                            start=(k == 0), stop=(k == 31))
                        toks.append(t.op(fn)) if k == 31 else t.quiet(fn)
                    s.wait(t, toks[-1])
                    evs.append(s.op(lambda e, j=j, bank=bank: e.activation(
                        dst_list[j], bank[:, 0:G], AF.Gelu, scale=scal[:, scal_off + j:scal_off + j + 1])))
                hid_prev_mm[0] = toks[-1]
                return evs

            hid_prev_mm = [0]
            evs_h1 = mk_hidden(h1s, 64, t1, t2, (v, ev1[-1]))
            evs_h2 = mk_hidden(h2s, 96, t2, t1, (s, evs_h1[-1]))

            toks3, evs3 = [], []
            for j in range(8):
                par = j % 2
                q.wait(v, c_cast_carry)
                c_w = q.dma(lambda e, j=j: e.dma_start(out=wraw[:, 0:4096], in_=w3s[j]))
                v.wait(q, c_w)
                if len(toks3) >= 2:
                    v.wait(t, toks3[-2])
                elif j == 0:
                    v.wait(t, hid_prev_mm[0])
                c_cast = v.op(lambda e, par=par: e.tensor_copy(wcast[par][:, 0:4096], wraw[:, 0:4096]))
                c_cast_carry = c_cast
                t.wait(v, c_cast)
                if j == 0:
                    t.wait(s, evs_h2[-1])
                bank = pb[4 + par]
                if len(evs3) >= 2:
                    t.wait(s, evs3[-2])
                for k in range(32):
                    fn = lambda e, k=k, par=par, bank=bank: e.matmul(
                        bank[:, 0:G], wcast[par][:, 128 * k:128 * (k + 1)], t1[k][:, 0:G],
                        start=(k == 0), stop=(k == 31))
                    toks3.append(t.op(fn)) if k == 31 else t.quiet(fn)
                s.wait(t, toks3[-1])
                evs3.append(s.op(lambda e, j=j, bank=bank: e.activation(
                    gbuf[j], bank[:, 0:G], AF.Silu, scale=scal[:, 128 + j:129 + j])))
            c_prev_mm = toks3[-1]

            for ct in range(8):
                g.wait(g, c_store_prev)
                if gi == 0 and ct == 0:
                    g.wait(v, c_sdone)
                c_x = g.dma(lambda e, ct=ct, gs=gs: e.dma_start(
                    out=xbuf[:, 0:G], in_=x_i[128 * ct:128 * (ct + 1), gs:gs + G]))
                v.wait(g, c_x)
                v.wait(s, evs3[ct])
                c_tm = v.op(lambda e, ct=ct, gs=gs: e.tensor_tensor(xsq[:, 0:G], sb[ct][:, gs:gs + G], gbuf[ct], ALU.mult))
                v.wait(v, c_tm)
                c_comb_prev = v.op(lambda e: e.tensor_tensor(xbuf[:, 0:G], xbuf[:, 0:G], xsq[:, 0:G], ALU.add))
                g.wait(v, c_comb_prev)
                c_store_prev = g.dma(lambda e, ct=ct, gs=gs: e.dma_start(
                    out=x1_o[128 * ct:128 * (ct + 1), gs:gs + G], in_=xbuf[:, 0:G]))

        g.wait(q, q.n)
        g.wait(g, g.n)
        g.op(lambda e: e.memset(A32[0:1, 0:1], 0.0))

        S.finalize(nc, stack)
    return nc


def prep_ffn0(inp, xn_full, sp_full):
    x = _f32(inp["L0_attn_x"])
    gb = np.stack([_f32(inp["L0_attn_norm_gamma"]), _f32(inp["L0_attn_norm_beta"])], axis=1)
    w1p = pack_lhsT(_f32(inp["L0_ffn_w1_w"]).T)
    w2p = pack_lhsT(_f32(inp["L0_ffn_w2_w"]).T)
    h1p = pack_lhsT(_f32(inp["L0_ffn_h1_w"]).T)
    h2p = pack_lhsT(_f32(inp["L0_ffn_h2_w"]).T)
    w3p = pack_lhsT(_f32(inp["L0_ffn_w3_w"]).T)
    scal = np.zeros((128, 136), np.float32)
    scal[:, 0:32] = _f32(inp["L0_ffn_w1_s"]).reshape(32, 128).T
    scal[:, 32:64] = _f32(inp["L0_ffn_w2_s"]).reshape(32, 128).T
    scal[:, 64:96] = _f32(inp["L0_ffn_h1_s"]).reshape(32, 128).T
    scal[:, 96:128] = _f32(inp["L0_ffn_h2_s"]).reshape(32, 128).T
    scal[:, 128:136] = _f32(inp["L0_ffn_w3_s"]).reshape(8, 128).T
    ones = np.ones((128, 128), dtype=BF)
    maps = []
    for c in range(NCORE):
        b, th = c // 2, c % 2
        sl = slice(TC * th, TC * (th + 1))
        maps.append({
            "x_i": np.ascontiguousarray(x[b].T[:, sl]),
            "xn_i": np.ascontiguousarray(xn_full[b][:, sl]),
            "sp_i": np.ascontiguousarray(sp_full[b][:, sl]),
            "gb_i": gb, "scal_i": scal, "onesall": ones,
            "w1s": w1p, "w2s": w2p, "h1s": h1p, "h2s": h2p, "w3s": w3p,
        })
    return maps


def run_ffn0(inp, xn_full, sp_full):
    nc = build_ffn0()
    res = run_bass_kernel_spmd(nc, prep_ffn0(inp, xn_full, sp_full), core_ids=list(range(NCORE)))
    x1 = np.empty((BD, CM, SQ), dtype=np.float32)
    for c in range(NCORE):
        b, th = c // 2, c % 2
        x1[b][:, TC * th:TC * (th + 1)] = np.asarray(res.results[c]["x1_o"])
    return x1



def build_pre1():
    nc = bass.Bass("TRN2", target_bir_lowering=False)
    x1_i = nc.dram_tensor("x1_i", [CM, TC], F32, kind="ExternalInput").ap()
    gb_i = nc.dram_tensor("gb_i", [CM, 2], F32, kind="ExternalInput").ap()
    wds = nc.dram_tensor("wds", [8, 128, 1024], F32, kind="ExternalInput").ap()
    bd_i = nc.dram_tensor("bd_i", [CM, 1], F32, kind="ExternalInput").ap()
    wbcs = nc.dram_tensor("wbcs", [1, 128, 256], F32, kind="ExternalInput").ap()
    bbc_i = nc.dram_tensor("bbc_i", [32, 1], F32, kind="ExternalInput").ap()
    xn1_o = nc.dram_tensor("xn1_o", [CM, TC], BF16, kind="ExternalOutput").ap()
    zn_o = nc.dram_tensor("zn_o", [CM, TC], BF16, kind="ExternalOutput").ap()
    dl_o = nc.dram_tensor("dl_o", [CM, TC], F32, kind="ExternalOutput").ap()
    xd_o = nc.dram_tensor("xd_o", [CM, TC], F32, kind="ExternalOutput").ap()
    bc_o = nc.dram_tensor("bc_o", [32, TC], F32, kind="ExternalOutput").ap()

    S = Sch()
    q, t, v, s, g = (S.q[e] for e in ENGS)

    with ExitStack() as stack:
        A16 = stack.enter_context(nc.sbuf_tensor("a16", [128, 20800], BF16))
        A32 = stack.enter_context(nc.sbuf_tensor("a32", [128, 29900], F32))
        pb = [stack.enter_context(nc.psum_tensor(f"pb{i}", [128, 512], F32)) for i in range(8)]

        def b16(off, n):
            return A16[:, off:off + n]

        zn = [b16(2048 * i, 2048) for i in range(8)]
        xscr = b16(16384, 2048)
        wcast = [b16(18432 + 1024 * i, 1024) for i in range(2)]

        x1 = [A32[:, 2048 * i:2048 * (i + 1)] for i in range(8)]
        wraw = A32[:, 16384:17408]
        dscr = A32[:, 17408:19456]
        dscr2 = A32[:, 19456:21504]
        abc32 = A32[:, 21504:23552]
        cbc32 = A32[:, 23552:25600]
        ones32 = A32[:, 25600:25728]
        srow = A32[0:1, 25728:27776]
        qrow = A32[0:1, 27776:29824]
        tmp_row = A32[0:1, 17408:19456]
        gbc = A32[:, 29824:29840]
        bd_col = A32[:, 29840:29848]
        bbc_col = A32[:, 29848:29849]
        c_lneps = A32[:, 29852:29853]
        c_mhalf = A32[:, 29853:29854]

        for ct in range(8):
            q.dma(lambda e, ct=ct: e.dma_start(out=gbc[:, 2 * ct:2 * ct + 2], in_=gb_i[128 * ct:128 * (ct + 1), :]))
        for j in range(8):
            q.dma(lambda e, j=j: e.dma_start(out=bd_col[:, j:j + 1], in_=bd_i[128 * j:128 * (j + 1), :]))
        q.dma(lambda e: e.dma_start(out=bbc_col[0:32, 0:1], in_=bbc_i[:]))
        for ct in range(8):
            q.dma(lambda e, ct=ct: e.dma_start(out=x1[ct], in_=x1_i[128 * ct:128 * (ct + 1), :]))
        c_in = q.n
        q.wait(q, c_in)

        v.wait(q, c_in)
        v.quiet(lambda e: e.memset(c_lneps, LN_EPS))
        v.quiet(lambda e: e.memset(c_mhalf, -0.5))
        c_init = v.op(lambda e: e.memset(ones32, 1.0))

        def ln_pass_f32(tiles, use_gb, export, out_dram, first_tok):
            v.quiet(lambda e: e.memset(srow, 0.0))
            c_z0 = v.op(lambda e: e.memset(qrow, 0.0))
            ev_tok = {}
            c_sq_prev = 0
            for ct in range(8):
                if ct == 0:
                    s.wait(v, max(c_z0, first_tok))
                s.wait(t, c_sq_prev)
                c_sqr = s.op(lambda e, ct=ct: e.activation(dscr, tiles[ct], AF.Square))
                t.wait(s, c_sqr)
                if ct == 0:
                    t.wait(v, c_init)
                for j in range(4):
                    m = ct * 4 + j
                    bs_, bq_ = pb[j % 2], pb[2 + j % 2]
                    if m >= 2:
                        t.wait(v, ev_tok[m - 2])
                    t.quiet(lambda e, j=j, ct=ct, bs_=bs_: e.matmul(
                        bs_[:, 0:512], ones32, tiles[ct][:, 512 * j:512 * (j + 1)], start=True, stop=True))
                    c_mm = t.op(lambda e, j=j, bq_=bq_: e.matmul(
                        bq_[:, 0:512], ones32, dscr[:, 512 * j:512 * (j + 1)], start=True, stop=True))
                    v.wait(t, c_mm)
                    v.quiet(lambda e, j=j, bs_=bs_: e.tensor_add(srow[:, 512 * j:512 * (j + 1)], srow[:, 512 * j:512 * (j + 1)], bs_[0:1, 0:512]))
                    ev_tok[m] = v.op(lambda e, j=j, bq_=bq_: e.tensor_add(qrow[:, 512 * j:512 * (j + 1)], qrow[:, 512 * j:512 * (j + 1)], bq_[0:1, 0:512]))
                c_sq_prev = t.n
            v.quiet(lambda e: e.tensor_scalar_mul(srow, srow, 1.0 / CM))
            v.quiet(lambda e: e.tensor_scalar_mul(qrow, qrow, 1.0 / CM))
            v.quiet(lambda e: e.tensor_tensor(tmp_row, srow, srow, ALU.mult))
            c_var = v.op(lambda e: e.tensor_tensor(qrow, qrow, tmp_row, ALU.subtract))
            s.wait(v, c_var)
            s.quiet(lambda e: e.activation(qrow, qrow, AF.Ln, bias=c_lneps[0:1, 0:1]))
            c_rstd = s.op(lambda e: e.activation(qrow, qrow, AF.Exp, scale=c_mhalf[0:1, 0:1]))
            v.wait(s, c_rstd)
            c_crow = v.op(lambda e: e.tensor_tensor(tmp_row, srow, qrow, ALU.mult))
            t.wait(v, c_crow)
            for j in range(4):
                for src, dst in ((qrow, abc32), (tmp_row, cbc32)):
                    c_b1 = t.op(lambda e, j=j, src=src: e.matmul(
                        pb[4][:, 0:512], ones32[0:1, 0:128], src[0:1, 512 * j:512 * (j + 1)], start=True, stop=True))
                    s.wait(t, c_b1)
                    c_e1 = s.op(lambda e, j=j, dst=dst: e.copy(dst[:, 512 * j:512 * (j + 1)], pb[4][:, 0:512]))
                    t.wait(s, c_e1)
            c_bcast = s.n
            v.wait(s, c_bcast)
            c_exp_dma = 0
            last = 0
            for ct in range(8):
                v.quiet(lambda e, ct=ct: e.tensor_tensor(tiles[ct], tiles[ct], abc32, ALU.mult))
                last = v.op(lambda e, ct=ct: e.tensor_tensor(tiles[ct], tiles[ct], cbc32, ALU.subtract))
                if use_gb:
                    last = v.op(lambda e, ct=ct: e.tensor_scalar(
                        tiles[ct], tiles[ct], gbc[:, 2 * ct:2 * ct + 1], gbc[:, 2 * ct + 1:2 * ct + 2], ALU.mult, ALU.add))
                dstt = zn[ct] if export == "zn" else xscr
                if export == "x" and c_exp_dma:
                    v.wait(g, c_exp_dma)
                c_cp = v.op(lambda e, ct=ct, dstt=dstt: e.tensor_copy(dstt, tiles[ct]))
                g.wait(v, c_cp)
                c_exp_dma = g.dma(lambda e, ct=ct, dstt=dstt: e.dma_start(
                    out=out_dram[128 * ct:128 * (ct + 1), :], in_=dstt))
            return last, c_exp_dma

        c_xn1, c_dma1 = ln_pass_f32(x1, False, "x", xn1_o, c_init)
        c_zn, c_dma2 = ln_pass_f32(x1, True, "zn", zn_o, 0)
        zn32 = x1

        tok_mm = []
        c_cast = 0
        c_store = {}
        for j in range(8):
            par = j % 2
            q.wait(v, c_cast)
            c_w = q.dma(lambda e, j=j: e.dma_start(out=wraw, in_=wds[j]))
            v.wait(q, c_w)
            if len(tok_mm) >= 2:
                v.wait(t, tok_mm[-2])
            if j == 0:
                v.wait(v, c_zn)
            c_cast = v.op(lambda e, par=par: e.tensor_copy(wcast[par][:, 0:1024], wraw))
            t.wait(v, c_cast)
            for ch in range(4):
                bank = pb[4 + (2 * j + ch) % 2]
                if j * 4 + ch >= 2:
                    t.wait(v, c_store.get(j * 4 + ch - 2, 0))
                for k in range(8):
                    fn = lambda e, k=k, par=par, ch=ch, bank=bank: e.matmul(
                        bank[:, 0:512], wcast[par][:, 128 * k:128 * (k + 1)], zn[k][:, 512 * ch:512 * (ch + 1)],
                        start=(k == 0), stop=(k == 7))
                    tok_mm.append(t.op(fn)) if k == 7 else t.quiet(fn)
                v.wait(t, tok_mm[-1])
                v.wait(g, c_store.get(("st", j * 4 + ch - 1), 0))
                c_u = v.op(lambda e, j=j, bank=bank: e.tensor_scalar_add(dscr[:, 0:512], bank[:, 0:512], bd_col[:, j:j + 1]))
                s.wait(v, c_u)
                s.quiet(lambda e: e.activation(dscr2[:, 0:512], dscr[:, 0:512], AF.Abs))
                s.quiet(lambda e: e.activation(dscr2[:, 0:512], dscr2[:, 0:512], AF.Exp, scale=-1.0))
                c_ln = s.op(lambda e: e.activation(dscr2[:, 0:512], dscr2[:, 0:512], AF.Ln, bias=1.0))
                v.wait(s, c_ln)
                c_d = v.op(lambda e: e.scalar_tensor_tensor(dscr[:, 0:512], dscr[:, 0:512], 0.0, dscr2[:, 0:512], ALU.max, ALU.add))
                g.wait(v, c_d)
                g.dma(lambda e, j=j, ch=ch: e.dma_start(out=dl_o[128 * j:128 * (j + 1), 512 * ch:512 * (ch + 1)], in_=dscr[:, 0:512]))
                v.wait(v, c_d)
                c_xd = v.op(lambda e, j=j, ch=ch: e.tensor_tensor(
                    dscr2[:, 0:512], dscr[:, 0:512], zn32[j][:, 512 * ch:512 * (ch + 1)], ALU.mult))
                g.wait(v, c_xd)
                c_store[("st", j * 4 + ch)] = g.dma(lambda e, j=j, ch=ch: e.dma_start(
                    out=xd_o[128 * j:128 * (j + 1), 512 * ch:512 * (ch + 1)], in_=dscr2[:, 0:512]))
                c_store[j * 4 + ch] = c_xd

        c_laststore = g.n
        q.wait(v, c_cast)
        c_wbc = q.dma(lambda e: e.dma_start(out=wraw[:, 0:256], in_=wbcs[0]))
        v.wait(g, c_laststore)
        for ch in range(4):
            bank = pb[6]
            if ch >= 1:
                t.wait(v, c_bcev)
            else:
                t.wait(q, c_wbc)
            for k in range(8):
                fn = lambda e, k=k, ch=ch, bank=bank: e.matmul(
                    bank[0:32, 0:512], wraw[:, 32 * k:32 * (k + 1)], zn32[k][:, 512 * ch:512 * (ch + 1)],
                    start=(k == 0), stop=(k == 7))
                c_bcmm = t.op(fn) if k == 7 else t.quiet(fn)
            v.wait(t, c_bcmm)
            if ch >= 1:
                v.wait(g, c_bcst)
            c_bcev = v.op(lambda e, ch=ch: e.tensor_scalar_add(dscr[0:32, 0:512], bank[0:32, 0:512], bbc_col[0:32, 0:1]))
            g.wait(v, c_bcev)
            c_bcst = g.dma(lambda e, ch=ch: e.dma_start(out=bc_o[:, 512 * ch:512 * (ch + 1)], in_=dscr[0:32, 0:512]))

        g.wait(q, q.n)
        g.wait(g, g.n)
        g.op(lambda e: e.memset(A32[0:1, 0:1], 0.0))

        S.finalize(nc, stack)
    return nc


def prep_pre1(inp, x1_full):
    gb = np.stack([_f32(inp["L1_attn_ln1_g"]), _f32(inp["L1_attn_ln1_b"])], axis=1)
    wdp = pack_lhsT(_f32(inp["L1_attn_Wd"]).T)
    wbcp = pack_lhsT(_f32(inp["L1_attn_Wbc"]).T, octile=32)
    bd = _f32(inp["L1_attn_bd"]).reshape(CM, 1)
    bbc = _f32(inp["L1_attn_bbc"]).reshape(32, 1)
    ones = np.ones((128, 128), dtype=BF)
    maps = []
    for c in range(NCORE):
        b, th = c // 2, c % 2
        sl = slice(TC * th, TC * (th + 1))
        maps.append({
            "x1_i": np.ascontiguousarray(x1_full[b][:, sl]),
            "gb_i": gb, "wds": wdp, "bd_i": bd, "wbcs": wbcp, "bbc_i": bbc,
        })
    return maps


def run_pre1(inp, x1_full):
    nc = build_pre1()
    res = run_bass_kernel_spmd(nc, prep_pre1(inp, x1_full), core_ids=list(range(NCORE)))
    out = {}
    for name, dt in (("xn1", BF), ("zn", BF), ("dl", np.float32), ("xd", np.float32)):
        arr = np.empty((BD, CM, SQ), dtype=dt)
        for c in range(NCORE):
            b, th = c // 2, c % 2
            arr[b][:, TC * th:TC * (th + 1)] = np.asarray(res.results[c][name + "_o"])
        out[name] = arr
    bc = np.empty((BD, 32, SQ), dtype=np.float32)
    for c in range(NCORE):
        b, th = c // 2, c % 2
        bc[b][:, TC * th:TC * (th + 1)] = np.asarray(res.results[c]["bc_o"])
    out["bc"] = bc
    return out



def build_scan1():
    nc = bass.Bass("TRN2", target_bir_lowering=False)
    dl_i = nc.dram_tensor("dl_i", [512, SQ], F32, kind="ExternalInput").ap()
    xd_i = nc.dram_tensor("xd_i", [512, SQ], F32, kind="ExternalInput").ap()
    bc_i = nc.dram_tensor("bc_i", [32, SQ], F32, kind="ExternalInput").ap()
    alog_i = nc.dram_tensor("alog_i", [512, 16], F32, kind="ExternalInput").ap()
    esel_i = nc.dram_tensor("esel_i", [32, 4096], F32, kind="ExternalInput").ap()
    y_o = nc.dram_tensor("y_o", [512, SQ], F32, kind="ExternalOutput").ap()

    S = Sch()
    q, t, v, s, g = (S.q[e] for e in ENGS)

    with ExitStack() as stack:
        A32 = stack.enter_context(nc.sbuf_tensor("a32", [128, 37100], F32))
        pb = [stack.enter_context(nc.psum_tensor(f"pb{i}", [128, 512], F32)) for i in range(4)]

        dl = A32[:, 0:4096]
        xd = A32[:, 4096:8192]
        dab = A32[:, 8192:12288]
        dbx = A32[:, 12288:16384]
        hbuf = A32[:, 16384:20480]
        tmp = A32[:, 20480:24576]
        ybuf = A32[:, 24576:28672]
        alog = [A32[:, 28672 + 16 * i:28688 + 16 * i] for i in range(4)]
        aneg = [A32[:, 28736 + 16 * i:28752 + 16 * i] for i in range(4)]

        bcb = A32[:, 28900:32996]
        esel = A32[:, 32996:37092]

        q.dma(lambda e: e.dma_start(out=esel[0:32, 0:4096], in_=esel_i[:]))
        c_bc = q.dma(lambda e: e.dma_start(out=bcb[0:32, 0:4096], in_=bc_i[:]))
        for ct in range(4):
            q.dma(lambda e, ct=ct: e.dma_start(out=alog[ct], in_=alog_i[128 * ct:128 * (ct + 1), :]))
        c_al = q.n
        q.wait(q, c_al)
        v.wait(q, c_al)
        c_bcb = v.op(lambda e: e.memset(ybuf[0:1, 0:1], 0.0))
        s.wait(q, c_al)
        for ct in range(4):
            s.quiet(lambda e, ct=ct: e.activation(aneg[ct], alog[ct], AF.Exp))
        c_an = s.op(lambda e: e.activation(aneg[0], alog[0], AF.Exp))
        v.wait(s, c_an)
        for ct in range(4):
            v.quiet(lambda e, ct=ct: e.tensor_scalar_mul(aneg[ct], aneg[ct], -1.0))
        c_a = v.op(lambda e: e.memset(ybuf[0:1, 0:1], 0.0))

        c_dA_prev = 0
        c_scan_prev = 0
        c_ydma_prev = 0
        c_dl_used = (0, 0)
        for ct in range(4):
            q.wait(s, c_dl_used[0])
            q.wait(v, c_dl_used[1])
            c_dl = q.dma(lambda e, ct=ct: e.dma_start(out=dl, in_=dl_i[128 * ct:128 * (ct + 1), :]))
            c_xd = q.dma(lambda e, ct=ct: e.dma_start(out=xd, in_=xd_i[128 * ct:128 * (ct + 1), :]))
            for n in range(16):
                s.wait(q, c_xd)
                if c_scan_prev:
                    s.wait(v, c_scan_prev)
                c_dA = s.op(lambda e, ct=ct, n=n: e.activation(dab, dl, AF.Exp, scale=aneg[ct][:, n:n + 1]))
                tok_b = {}
                for jj in range(8):
                    bank = pb[jj % 2]
                    if jj >= 2:
                        t.wait(v, tok_b[jj - 2])
                    elif c_scan_prev:
                        t.wait(v, c_scan_prev)
                    if ct == 0 and n == 0 and jj == 0:
                        t.wait(v, c_bcb)
                    c_mb = t.op(lambda e, n=n, jj=jj, bank=bank: e.matmul(
                        bank[:, 0:512], esel[0:32, 128 * n:128 * (n + 1)], bcb[0:32, 512 * jj:512 * (jj + 1)],
                        start=True, stop=True))
                    v.wait(t, c_mb)
                    v.wait(q, c_xd)
                    tok_b[jj] = v.op(lambda e, jj=jj, bank=bank: e.tensor_tensor(
                        dbx[:, 512 * jj:512 * (jj + 1)], xd[:, 512 * jj:512 * (jj + 1)], bank[:, 0:512], ALU.mult))
                v.wait(s, c_dA)
                c_scan = v.op(lambda e: e.tensor_tensor_scan(hbuf, dab, dbx, 0.0, ALU.mult, ALU.add))
                tok_c = {}
                for jj in range(8):
                    bank = pb[2 + jj % 2]
                    if jj >= 2:
                        t.wait(v, tok_c[jj - 2])
                    c_mc = t.op(lambda e, n=n, jj=jj, bank=bank: e.matmul(
                        bank[:, 0:512], esel[0:32, 128 * (16 + n):128 * (17 + n)], bcb[0:32, 512 * jj:512 * (jj + 1)],
                        start=True, stop=True))
                    v.wait(t, c_mc)
                    if n == 0 and ct >= 0:
                        if c_ydma_prev:
                            v.wait(q, c_ydma_prev)
                        dst = ybuf[:, 512 * jj:512 * (jj + 1)]
                    else:
                        dst = tmp[:, 512 * jj:512 * (jj + 1)]
                    tok_c[jj] = v.op(lambda e, jj=jj, bank=bank, dst=dst: e.tensor_tensor(
                        dst, hbuf[:, 512 * jj:512 * (jj + 1)], bank[:, 0:512], ALU.mult))
                if n > 0:
                    c_y = v.op(lambda e: e.tensor_tensor(ybuf, ybuf, tmp, ALU.add))
                c_scan_prev = c_scan
                c_dA_prev = c_dA
            q.wait(v, v.n)
            c_ydma_prev = q.dma(lambda e, ct=ct: e.dma_start(out=y_o[128 * ct:128 * (ct + 1), :], in_=ybuf))
            q.wait(q, c_ydma_prev)
            c_dl_used = (c_dA_prev, v.n)

        g.wait(q, q.n)
        g.op(lambda e: e.memset(A32[0:1, 0:1], 0.0))
        S.finalize(nc, stack)
    return nc


def prep_scan1(pre, inp):
    alog = _f32(inp["L1_attn_A_log"])
    esel = np.zeros((32, 4096), dtype=np.float32)
    for n in range(32):
        esel[n, 128 * n:128 * (n + 1)] = 1
    maps = []
    for c in range(NCORE):
        b, chalf = c // 2, c % 2
        rows = slice(512 * chalf, 512 * (chalf + 1))
        maps.append({
            "dl_i": np.ascontiguousarray(pre["dl"][b][rows]),
            "xd_i": np.ascontiguousarray(pre["xd"][b][rows]),
            "bc_i": np.ascontiguousarray(pre["bc"][b]),
            "alog_i": np.ascontiguousarray(alog[rows]),
            "esel_i": esel,
        })
    return maps


def run_scan1(pre, inp):
    nc = build_scan1()
    res = run_bass_kernel_spmd(nc, prep_scan1(pre, inp), core_ids=list(range(NCORE)))
    y = np.empty((BD, CM, SQ), dtype=np.float32)
    for c in range(NCORE):
        b, chalf = c // 2, c % 2
        y[b][512 * chalf:512 * (chalf + 1)] = np.asarray(res.results[c]["y_o"])
    return y



def build_post1(dbg=False):
    nc = bass.Bass("TRN2", target_bir_lowering=False)
    xn1_i = nc.dram_tensor("xn1_i", [CM, TC], BF16, kind="ExternalInput").ap()
    zn_i = nc.dram_tensor("zn_i", [CM, TC], BF16, kind="ExternalInput").ap()
    y_i = nc.dram_tensor("y_i", [CM, TC], F32, kind="ExternalInput").ap()
    dd_i = nc.dram_tensor("dd_i", [CM, 1], F32, kind="ExternalInput").ap()
    gb_i = nc.dram_tensor("gb_i", [CM, 2], F32, kind="ExternalInput").ap()
    b1_i = nc.dram_tensor("b1_i", [128, 32], F32, kind="ExternalInput").ap()
    b2_i = nc.dram_tensor("b2_i", [128, 8], F32, kind="ExternalInput").ap()
    w1s = nc.dram_tensor("w1s", [32, 128, 1024], F32, kind="ExternalInput").ap()
    w2s = nc.dram_tensor("w2s", [8, 128, 4096], F32, kind="ExternalInput").ap()
    onesall = nc.dram_tensor("onesall", [128, 128], BF16, kind="ExternalInput").ap()
    s1_o = nc.dram_tensor("s1_o", [CM, TC], BF16, kind="ExternalOutput").ap()
    ssm32_h = nc.dram_tensor("ssm32_h", [CM, TC], F32, kind="Internal").ap()

    S = Sch()
    q, t, v, s, g = (S.q[e] for e in ENGS)
    G = 1024

    with ExitStack() as stack:
        A16 = stack.enter_context(nc.sbuf_tensor("a16", [128, 80600], BF16))
        A32 = stack.enter_context(nc.sbuf_tensor("a32", [128, 7000], F32))
        pb = [stack.enter_context(nc.psum_tensor(f"pb{i}", [128, 512], F32)) for i in range(8)]

        def b16(off, n):
            return A16[:, off:off + n]

        xn1 = [b16(2048 * i, 2048) for i in range(8)]
        zn = [b16(16384 + 2048 * i, 2048) for i in range(8)]
        hff = [b16(1024 * j, 1024) for j in range(32)]
        ssm = [b16(32768 + 2048 * i, 2048) for i in range(8)]
        nrm = [b16(49152 + 2048 * i, 2048) for i in range(8)]
        wcast = [b16(65536 + 4096 * i, 4096) for i in range(2)]
        abc = b16(73728, 2048)
        cbc = b16(75776, 2048)
        xsq = b16(77824, 2048)
        ones_sb = b16(79872, 128)
        sst = b16(80000, 512)
        a_rowb = A16[0:1, 49152:51200]
        c_rowb = A16[0:1, 51200:53248]

        wraw = A32[:, 0:4096]
        yscr = A32[:, 4096:6144]
        dcol = A32[:, 6144:6145]
        gbc = A32[:, 6148:6164]
        b1c = A32[:, 6164:6196]
        b2c = A32[:, 6196:6204]
        c_lneps = A32[:, 6204:6205]
        c_mhalf = A32[:, 6205:6206]
        srow = A32[0:1, 0:2048]
        qrow = A32[0:1, 2048:4096]
        tmp_row = A32[0:1, 4096:6144]

        q.dma(lambda e: e.dma_start(out=ones_sb, in_=onesall[:]))
        q.dma(lambda e: e.dma_start(out=b1c, in_=b1_i[:]))
        q.dma(lambda e: e.dma_start(out=b2c, in_=b2_i[:]))
        for ct in range(8):
            q.dma(lambda e, ct=ct: e.dma_start(out=gbc[:, 2 * ct:2 * ct + 2], in_=gb_i[128 * ct:128 * (ct + 1), :]))
        for ct in range(8):
            q.dma(lambda e, ct=ct: e.dma_start(out=xn1[ct], in_=xn1_i[128 * ct:128 * (ct + 1), :]))
        for ct in range(8):
            q.dma(lambda e, ct=ct: e.dma_start(out=zn[ct], in_=zn_i[128 * ct:128 * (ct + 1), :]))
        c_in = q.n
        q.wait(q, c_in)

        v.wait(q, c_in)
        v.quiet(lambda e: e.memset(c_lneps, LN_EPS))
        c_init = v.op(lambda e: e.memset(c_mhalf, -0.5))

        ddc = A32[:, 6208:6216]
        for ct in range(8):
            q.dma(lambda e, ct=ct: e.dma_start(out=ddc[:, ct:ct + 1], in_=dd_i[128 * ct:128 * (ct + 1), :]))
        c_dd = q.n
        c_y_used = 0
        q.wait(q, c_dd)
        for ct in range(8):
            q.wait(v, c_y_used)
            c_y = q.dma(lambda e, ct=ct: e.dma_start(out=yscr, in_=y_i[128 * ct:128 * (ct + 1), :]))
            v.wait(q, c_y)
            if ct == 0:
                v.wait(q, c_dd)
            v.quiet(lambda e, ct=ct: e.scalar_tensor_tensor(yscr, zn[ct], ddc[:, ct:ct + 1], yscr, ALU.mult, ALU.add))
            c_f32 = v.op(lambda e, ct=ct: e.tensor_tensor(yscr, yscr, xn1[ct], ALU.add))
            v.wait(v, c_f32)
            c_y_used = v.op(lambda e, ct=ct: e.tensor_copy(ssm[ct], yscr))
        c_ssm = c_y_used

        v.quiet(lambda e: e.memset(srow, 0.0))
        c_z0 = v.op(lambda e: e.memset(qrow, 0.0))
        ev_tok = {}
        c_sq_prev = 0
        for ct in range(8):
            if ct == 0:
                s.wait(v, c_z0)
            s.wait(t, c_sq_prev)
            c_sqr = s.op(lambda e, ct=ct: e.activation(xsq, ssm[ct], AF.Square))
            t.wait(s, c_sqr)
            for j in range(4):
                m = ct * 4 + j
                bs_, bq_ = pb[j % 2], pb[2 + j % 2]
                if m >= 2:
                    t.wait(v, ev_tok[m - 2])
                t.quiet(lambda e, j=j, ct=ct, bs_=bs_: e.matmul(
                    bs_[:, 0:512], ones_sb, ssm[ct][:, 512 * j:512 * (j + 1)], start=True, stop=True))
                c_mm = t.op(lambda e, j=j, bq_=bq_: e.matmul(
                    bq_[:, 0:512], ones_sb, xsq[:, 512 * j:512 * (j + 1)], start=True, stop=True))
                v.wait(t, c_mm)
                v.quiet(lambda e, j=j, bs_=bs_: e.tensor_add(srow[:, 512 * j:512 * (j + 1)], srow[:, 512 * j:512 * (j + 1)], bs_[0:1, 0:512]))
                ev_tok[m] = v.op(lambda e, j=j, bq_=bq_: e.tensor_add(qrow[:, 512 * j:512 * (j + 1)], qrow[:, 512 * j:512 * (j + 1)], bq_[0:1, 0:512]))
            c_sq_prev = t.n
        v.quiet(lambda e: e.tensor_scalar_mul(srow, srow, 1.0 / CM))
        v.quiet(lambda e: e.tensor_scalar_mul(qrow, qrow, 1.0 / CM))
        v.quiet(lambda e: e.tensor_tensor(tmp_row, srow, srow, ALU.mult))
        c_var = v.op(lambda e: e.tensor_tensor(qrow, qrow, tmp_row, ALU.subtract))
        s.wait(v, c_var)
        s.quiet(lambda e: e.activation(qrow, qrow, AF.Ln, bias=c_lneps[0:1, 0:1]))
        c_rstd = s.op(lambda e: e.activation(qrow, qrow, AF.Exp, scale=c_mhalf[0:1, 0:1]))
        v.wait(s, c_rstd)
        v.quiet(lambda e: e.tensor_tensor(tmp_row, srow, qrow, ALU.mult))
        v.quiet(lambda e: e.tensor_copy(a_rowb, qrow))
        c_rowsb = v.op(lambda e: e.tensor_copy(c_rowb, tmp_row))
        t.wait(v, c_rowsb)
        for j in range(4):
            for src, dst in ((a_rowb, abc), (c_rowb, cbc)):
                c_b1 = t.op(lambda e, j=j, src=src: e.matmul(
                    pb[4][:, 0:512], ones_sb[0:1, 0:128], src[0:1, 512 * j:512 * (j + 1)], start=True, stop=True))
                s.wait(t, c_b1)
                c_e1 = s.op(lambda e, j=j, dst=dst: e.copy(dst[:, 512 * j:512 * (j + 1)], pb[4][:, 0:512]))
                t.wait(s, c_e1)
        c_bcast = s.n
        v.wait(s, c_bcast)
        c_nrm = 0
        for ct in range(8):
            v.quiet(lambda e, ct=ct: e.tensor_tensor(nrm[ct], ssm[ct], abc, ALU.mult))
            v.quiet(lambda e, ct=ct: e.tensor_tensor(nrm[ct], nrm[ct], cbc, ALU.subtract))
            c_nrm = v.op(lambda e, ct=ct: e.tensor_scalar(
                nrm[ct], nrm[ct], gbc[:, 2 * ct:2 * ct + 1], gbc[:, 2 * ct + 1:2 * ct + 2], ALU.mult, ALU.add))

        c_cast = 0
        G2 = 512
        hff2 = [b16(32768 + 512 * j, 512) for j in range(32)]
        c_store_prev = 0
        c_s1_prev = 0
        c_prev_phase_mm = 0
        for gi in range(4):
            gs = G2 * gi
            toks, evs = [], []
            for j in range(32):
                par = j % 2
                q.wait(v, c_cast)
                if gi == 0 and j == 0:
                    q.wait(v, c_rowsb)
                c_w = q.dma(lambda e, j=j: e.dma_start(out=wraw[:, 0:1024], in_=w1s[j]))
                v.wait(q, c_w)
                if len(toks) >= 2:
                    v.wait(t, toks[-2])
                elif j == 0:
                    v.wait(t, c_prev_phase_mm)
                c_cast = v.op(lambda e, par=par: e.tensor_copy(wcast[par][:, 0:1024], wraw[:, 0:1024]))
                t.wait(v, c_cast)
                if gi == 0 and j == 0:
                    t.wait(v, c_nrm)
                bank = pb[4 + par]
                if len(evs) >= 2:
                    t.wait(s, evs[-2])
                elif gi > 0:
                    t.wait(v, c_s1_prev)
                for k in range(8):
                    fn = lambda e, k=k, par=par, gs=gs, bank=bank: e.matmul(
                        bank[:, 0:512], wcast[par][:, 128 * k:128 * (k + 1)],
                        nrm[k][:, gs:gs + G2], start=(k == 0), stop=(k == 7))
                    toks.append(t.op(fn)) if k == 7 else t.quiet(fn)
                s.wait(t, toks[-1])
                if gi == 0 and j == 0:
                    s.wait(v, c_nrm)
                evs.append(s.op(lambda e, j=j, bank=bank: e.activation(
                    hff2[j], bank[:, 0:512], AF.Gelu, bias=b1c[:, j:j + 1])))

            toks2 = []
            for hoc in range(8):
                par = hoc % 2
                q.wait(v, c_cast)
                c_w = q.dma(lambda e, hoc=hoc: e.dma_start(out=wraw[:, 0:4096], in_=w2s[hoc]))
                v.wait(q, c_w)
                if len(toks2) >= 2:
                    v.wait(t, toks2[-2])
                elif hoc == 0:
                    v.wait(t, toks[-1])
                c_cast = v.op(lambda e, par=par: e.tensor_copy(wcast[par][:, 0:4096], wraw[:, 0:4096]))
                t.wait(v, c_cast)
                if hoc == 0:
                    t.wait(s, evs[-1])
                bank = pb[4 + par]
                if hoc >= 2:
                    t.wait(v, c_s1_prev)
                for k in range(32):
                    fn = lambda e, k=k, par=par, bank=bank: e.matmul(
                        bank[:, 0:512], wcast[par][:, 128 * k:128 * (k + 1)],
                        hff2[k], start=(k == 0), stop=(k == 31))
                    toks2.append(t.op(fn)) if k == 31 else t.quiet(fn)
                g.wait(v, c_s1_prev if c_s1_prev else c_nrm)
                c_yld = g.dma(lambda e, hoc=hoc, gs=gs: e.dma_start(
                    out=yscr[:, 0:512], in_=y_i[128 * hoc:128 * (hoc + 1), gs:gs + G2]))
                v.wait(g, c_yld)
                v.quiet(lambda e, hoc=hoc, gs=gs: e.scalar_tensor_tensor(
                    yscr[:, 0:512], zn[hoc][:, gs:gs + G2], ddc[:, hoc:hoc + 1], yscr[:, 0:512], ALU.mult, ALU.add))
                c_sx = v.op(lambda e, hoc=hoc, gs=gs: e.tensor_tensor(
                    yscr[:, 0:512], yscr[:, 0:512], xn1[hoc][:, gs:gs + G2], ALU.add))
                v.wait(t, toks2[-1])
                v.wait(g, c_store_prev)
                c_s1 = v.op(lambda e, hoc=hoc, gs=gs, bank=bank: e.scalar_tensor_tensor(
                    sst, bank[:, 0:512], b2c[:, hoc:hoc + 1], yscr[:, 0:512], ALU.add, ALU.add))
                c_s1_prev = c_s1
                g.wait(v, c_s1)
                c_store_prev = g.dma(lambda e, hoc=hoc, gs=gs: e.dma_start(
                    out=s1_o[128 * hoc:128 * (hoc + 1), gs:gs + G2], in_=sst))
            c_prev_phase_mm = toks2[-1]

        g.wait(q, q.n)
        g.wait(g, g.n)
        g.op(lambda e: e.memset(A32[0:1, 0:1], 0.0))
        S.finalize(nc, stack)
    return nc


def prep_post1(inp, pre, y_full):
    gb = np.stack([_f32(inp["L1_attn_ln2_g"]), _f32(inp["L1_attn_ln2_b"])], axis=1)
    w1p = pack_lhsT(_f32(inp["L1_attn_W1"]).T)
    w2p = pack_lhsT(_f32(inp["L1_attn_W2"]).T)
    b1 = _f32(inp["L1_attn_b1"]).reshape(32, 128).T
    b2 = _f32(inp["L1_attn_b2"]).reshape(8, 128).T
    dd = _f32(inp["L1_attn_D"]).reshape(CM, 1)
    ones = np.ones((128, 128), dtype=BF)
    maps = []
    for c in range(NCORE):
        b, th = c // 2, c % 2
        sl = slice(TC * th, TC * (th + 1))
        maps.append({
            "xn1_i": np.ascontiguousarray(pre["xn1"][b][:, sl]),
            "zn_i": np.ascontiguousarray(pre["zn"][b][:, sl]),
            "y_i": np.ascontiguousarray(y_full[b][:, sl]),
            "dd_i": dd, "gb_i": gb, "b1_i": b1, "b2_i": b2,
            "w1s": w1p, "w2s": w2p, "onesall": ones,
        })
    return maps


def run_post1(inp, pre, y_full, dbg=False):
    nc = build_post1(dbg)
    res = run_bass_kernel_spmd(nc, prep_post1(inp, pre, y_full), core_ids=list(range(NCORE)))
    s1 = np.empty((BD, CM, SQ), dtype=BF)
    for c in range(NCORE):
        b, th = c // 2, c % 2
        s1[b][:, TC * th:TC * (th + 1)] = np.asarray(res.results[c]["s1_o"])
    if dbg:
        return s1, res
    return s1



def build_moe1():
    nc = bass.Bass("TRN2", target_bir_lowering=False)
    xn1_i = nc.dram_tensor("xn1_i", [CM, TC], BF16, kind="ExternalInput").ap()
    s1_i = nc.dram_tensor("s1_i", [CM, TC], BF16, kind="ExternalInput").ap()
    x1_i = nc.dram_tensor("x1_i", [CM, TC], F32, kind="ExternalInput").ap()
    nw_i = nc.dram_tensor("nw_i", [CM, 1], F32, kind="ExternalInput").ap()
    gw_i = nc.dram_tensor("gw_i", [CM, 8], BF16, kind="ExternalInput").ap()
    gb_i = nc.dram_tensor("gb_i", [1, 8], BF16, kind="ExternalInput").ap()
    m1s = nc.dram_tensor("m1s", [8, 16, 128, 1024], BF16, kind="ExternalInput").ap()
    b1s = nc.dram_tensor("b1s", [8, 128, 16], F32, kind="ExternalInput").ap()
    m2s = nc.dram_tensor("m2s", [8, 8, 128, 1024], BF16, kind="ExternalInput").ap()
    b2lh = nc.dram_tensor("b2lh", [8, 8, 128], BF16, kind="ExternalInput").ap()
    eye = nc.dram_tensor("eye", [128, 128], BF16, kind="ExternalInput").ap()
    esel8_i = nc.dram_tensor("esel8_i", [8, 1024], BF16, kind="ExternalInput").ap()
    onesall = nc.dram_tensor("onesall", [128, 128], BF16, kind="ExternalInput").ap()
    out_o = nc.dram_tensor("out_o", [CM, TC], F32, kind="ExternalOutput").ap()

    S = Sch()
    q, t, v, s, g = (S.q[e] for e in ENGS)
    NCH, CH = 4, 512

    with ExitStack() as stack:
        A16 = stack.enter_context(nc.sbuf_tensor("a16", [128, 69500], BF16))
        A32 = stack.enter_context(nc.sbuf_tensor("a32", [128, 17100], F32))
        pb = [stack.enter_context(nc.psum_tensor(f"pb{i}", [128, 512], F32)) for i in range(7)]
        pbt = stack.enter_context(nc.psum_tensor("pbt", [128, 1024], BF16))

        def b16(off, n):
            return A16[:, off:off + n]

        xn1 = [b16(2048 * i, 2048) for i in range(8)]
        nf = [b16(16384 + 2048 * i, 2048) for i in range(8)]
        s1 = [b16(32768 + 2048 * i, 2048) for i in range(8)]
        act = [b16(49152 + 512 * i, 512) for i in range(8)]
        wg_b = [b16(53248 + 1024 * i, 1024) for i in range(2)]
        wl_b = [b16(55296 + 1024 * i, 1024) for i in range(2)]
        w2_b = [b16(57344 + 1024 * i, 1024) for i in range(2)]
        gwc = b16(59392, 64)
        comb = b16(59456, 2048)
        b2a = b16(61504, 1024)
        lgt = b16(62528, 8)
        combT = b16(62536, 8)
        sndT = b16(62544, 8)
        arow = A16[0:1, 62552:64600]
        gbrow = b16(64600, 8)
        ones_sb = b16(64608, 128)
        eye_sb = b16(64736, 128)
        sgt = b16(64864, 512)
        lint = b16(65376, 512)
        sst = b16(65888, 512)
        mot = b16(66400, 2048)
        esel8 = b16(68448, 1024)

        macc = [A32[:, 2048 * i:2048 * (i + 1)] for i in range(8)]
        b1all = A32[:, 16384:16512]
        nwcs = A32[:, 16512:16520]
        c_rmseps = A32[:, 16520:16521]
        c_mhalf = A32[:, 16521:16522]
        c_1702 = A32[:, 16522:16523]
        m1col = A32[:, 16524:16525]
        m2col = A32[:, 16525:16526]
        fnd = A32[:, 16526:16527]
        eqc = A32[:, 16527:16528]
        w1col = A32[:, 16528:16529]
        w2col = A32[:, 16529:16530]
        sc1 = A32[:, 16530:16531]
        xst = A32[:, 16536:17048]
        srow = A32[0:1, 0:2048]

        q.dma(lambda e: e.dma_start(out=ones_sb, in_=onesall[:]))
        q.dma(lambda e: e.dma_start(out=eye_sb, in_=eye[:]))
        q.dma(lambda e: e.dma_start(out=esel8[0:8, :], in_=esel8_i[:]))
        q.dma(lambda e: e.dma_start(out=gbrow[0:1, 0:8], in_=gb_i[:]))
        for kc in range(8):
            q.dma(lambda e, kc=kc: e.dma_start(out=gwc[:, 8 * kc:8 * (kc + 1)], in_=gw_i[128 * kc:128 * (kc + 1), :]))
        for ct in range(8):
            q.dma(lambda e, ct=ct: e.dma_start(out=nwcs[:, ct:ct + 1], in_=nw_i[128 * ct:128 * (ct + 1), :]))
        for ex in range(8):
            q.dma(lambda e, ex=ex: e.dma_start(out=b1all[:, 16 * ex:16 * (ex + 1)], in_=b1s[ex]))
        for hoc in range(8):
            q.dma(lambda e, hoc=hoc: e.dma_start(out=b2a[0:8, 128 * hoc:128 * (hoc + 1)], in_=b2lh[hoc]))
        for ct in range(8):
            q.dma(lambda e, ct=ct: e.dma_start(out=xn1[ct], in_=xn1_i[128 * ct:128 * (ct + 1), :]))
        for ct in range(8):
            q.dma(lambda e, ct=ct: e.dma_start(out=s1[ct], in_=s1_i[128 * ct:128 * (ct + 1), :]))
        c_in = q.n
        q.wait(q, c_in)

        def b1c(ex, jj):
            return b1all[:, 16 * ex + jj:16 * ex + jj + 1]

        v.wait(q, c_in)
        v.quiet(lambda e: e.memset(c_rmseps, LN_EPS))
        v.quiet(lambda e: e.memset(c_mhalf, -0.5))
        v.quiet(lambda e: e.memset(c_1702, 1.702))
        v.quiet(lambda e: e.memset(srow, 0.0))
        c_init = v.op(lambda e: e.memset(fnd, 0.0))

        xsq16 = b16(66400, 2048)
        ev_tok = {}
        c_sq_prev = 0
        for ct in range(8):
            if ct == 0:
                s.wait(v, c_init)
            s.wait(t, c_sq_prev)
            c_sqr = s.op(lambda e, ct=ct: e.activation(xsq16, xn1[ct], AF.Square))
            t.wait(s, c_sqr)
            for j in range(4):
                m = ct * 4 + j
                bq_ = pb[m % 2]
                if m >= 2:
                    t.wait(v, ev_tok[m - 2])
                c_mm = t.op(lambda e, j=j, bq_=bq_: e.matmul(
                    bq_[:, 0:512], ones_sb, xsq16[:, 512 * j:512 * (j + 1)], start=True, stop=True))
                v.wait(t, c_mm)
                ev_tok[m] = v.op(lambda e, j=j, bq_=bq_: e.tensor_add(srow[:, 512 * j:512 * (j + 1)], srow[:, 512 * j:512 * (j + 1)], bq_[0:1, 0:512]))
            c_sq_prev = t.n
        ones32m = A32[:, 2048:2176]
        c_sr = v.op(lambda e: e.tensor_scalar_mul(srow, srow, 1.0 / CM))
        c_o32 = v.op(lambda e: e.memset(ones32m, 1.0))
        s.wait(v, c_sr)
        s.quiet(lambda e: e.activation(srow, srow, AF.Ln, bias=c_rmseps[0:1, 0:1]))
        c_rstd = s.op(lambda e: e.activation(srow, srow, AF.Exp, scale=c_mhalf[0:1, 0:1]))
        t.wait(s, c_rstd)
        t.wait(v, c_o32)
        c_nf = 0
        for j in range(4):
            c_b1 = t.op(lambda e, j=j: e.matmul(
                pb[4][:, 0:512], ones32m[0:1, 0:128], srow[0:1, 512 * j:512 * (j + 1)], start=True, stop=True))
            v.wait(t, c_b1)
            for ct in range(8):
                c_nf = v.op(lambda e, ct=ct, j=j: e.scalar_tensor_tensor(
                    nf[ct][:, 512 * j:512 * (j + 1)], xn1[ct][:, 512 * j:512 * (j + 1)],
                    nwcs[:, ct:ct + 1], pb[4][:, 0:512], ALU.mult, ALU.mult))
            t.wait(v, c_nf)

        def vs(fn):
            tok = v.op(fn)
            v.wait(v, tok)
            return tok

        c_combT = 0
        for tt in range(16):
            if tt == 0:
                t.wait(v, c_nf)
            for kc in range(8):
                fn = lambda e, kc=kc, tt=tt: e.matmul(
                    pb[5][:, 0:8], nf[kc][:, 128 * tt:128 * (tt + 1)], gwc[:, 8 * kc:8 * (kc + 1)],
                    start=(kc == 0), stop=False)
                t.quiet(fn)
            c_lg = t.op(lambda e: e.matmul(pb[5][:, 0:8], ones_sb[0:1, 0:128], gbrow[0:1, 0:8], start=False, stop=True))
            v.wait(t, c_lg)
            vs(lambda e: e.tensor_copy(lgt[:, 0:8], pb[5][:, 0:8]))
            c_lgdone = v.n
            vs(lambda e: e.tensor_reduce(m1col, lgt[:, 0:8], AX.X, ALU.max))
            vs(lambda e: e.memset(fnd, 0.0))
            for ee in range(8):
                vs(lambda e, ee=ee: e.tensor_tensor(eqc, lgt[:, ee:ee + 1], m1col, ALU.is_ge))
                vs(lambda e, ee=ee: e.scalar_tensor_tensor(eqc, fnd, -1.0, eqc, ALU.mult, ALU.add))
                vs(lambda e, ee=ee: e.tensor_scalar_max(eqc, eqc, 0.0))
                vs(lambda e, ee=ee: e.tensor_copy(combT[:, ee:ee + 1], eqc))
                vs(lambda e, ee=ee: e.tensor_add(fnd, fnd, eqc))
            vs(lambda e: e.scalar_tensor_tensor(sgt[:, 0:8], combT[:, 0:8], -60000.0, lgt[:, 0:8], ALU.mult, ALU.add))
            vs(lambda e: e.tensor_reduce(m2col, sgt[:, 0:8], AX.X, ALU.max))
            vs(lambda e: e.memset(fnd, 0.0))
            for ee in range(8):
                vs(lambda e, ee=ee: e.tensor_tensor(eqc, sgt[:, ee:ee + 1], m2col, ALU.is_ge))
                vs(lambda e, ee=ee: e.scalar_tensor_tensor(eqc, fnd, -1.0, eqc, ALU.mult, ALU.add))
                vs(lambda e, ee=ee: e.tensor_scalar_max(eqc, eqc, 0.0))
                vs(lambda e, ee=ee: e.tensor_copy(sndT[:, ee:ee + 1], eqc))
                vs(lambda e, ee=ee: e.tensor_add(fnd, fnd, eqc))
            c_d12 = vs(lambda e: e.tensor_tensor(sc1, m1col, m2col, ALU.subtract))
            s.wait(v, c_d12)
            c_w1 = s.op(lambda e: e.activation(w1col, sc1, AF.Sigmoid))
            v.wait(s, c_w1)
            vs(lambda e: e.tensor_scalar(w2col, w1col, -1.0, 1.0, ALU.mult, ALU.add))
            for ee in range(8):
                vs(lambda e, ee=ee: e.tensor_scalar_mul(eqc, sndT[:, ee:ee + 1], w2col[:, 0:1]))
                vs(lambda e, ee=ee: e.scalar_tensor_tensor(combT[:, ee:ee + 1], combT[:, ee:ee + 1], w1col[:, 0:1], eqc, ALU.mult, ALU.add))
            c_cT = v.n
            t.wait(v, c_cT)
            c_tp = t.op(lambda e, tt=tt: e.transpose(pbt[0:8, 0:128], combT[:, 0:8], eye_sb))
            s.wait(t, c_tp)
            c_combT = s.op(lambda e, tt=tt: e.copy(comb[0:8, 128 * tt:128 * (tt + 1)], pbt[0:8, 0:128]))
            t.wait(s, c_combT)
        c_comb = c_combT

        c_z = 0
        for ct in range(8):
            c_z = v.op(lambda e, ct=ct: e.memset(macc[ct], 0.0))
        wtoks = {"g": {}, "l": {}, "2": {}}
        c_acc_tok = c_z
        c_cb = 0
        for chx in range(NCH):
            cs_ = CH * chx
            for ex in range(8):
                if chx == 0 and ex == 0:
                    t.wait(s, c_comb)
                t.wait(v, c_acc_tok)
                c_cb = t.op(lambda e, ex=ex, cs_=cs_: e.matmul(
                    pb[6][:, 0:512], esel8[0:8, 128 * ex:128 * (ex + 1)], comb[0:8, cs_:cs_ + 512],
                    start=True, stop=True))
                c_ep = 0
                for j in range(8):
                    par = j % 2
                    q.wait(t, wtoks["g"].get(par, 0))
                    if "c_wprev" in locals():
                        q.wait(q, c_wprev)
                    c_wg = q.dma(lambda e, ex=ex, j=j, par=par: e.dma_start(out=wg_b[par], in_=m1s[ex, j]))
                    q.wait(t, wtoks["l"].get(par, 0))
                    q.wait(q, c_wg)
                    c_wl = q.dma(lambda e, ex=ex, j=j, par=par: e.dma_start(out=wl_b[par], in_=m1s[ex, 8 + j]))
                    c_wprev = c_wl
                    t.wait(q, c_wl)
                    t.wait(v, c_ep if c_ep else c_acc_tok)
                    for k in range(8):
                        fn = lambda e, k=k, par=par, cs_=cs_: e.matmul(
                            pb[4][:, 0:512], wg_b[par][:, 128 * k:128 * (k + 1)], nf[k][:, cs_:cs_ + 512],
                            start=(k == 0), stop=(k == 7))
                        wtoks["g"][par] = t.op(fn) if k == 7 else t.quiet(fn)
                    for k in range(8):
                        fn = lambda e, k=k, par=par, cs_=cs_: e.matmul(
                            pb[5][:, 0:512], wl_b[par][:, 128 * k:128 * (k + 1)], nf[k][:, cs_:cs_ + 512],
                            start=(k == 0), stop=(k == 7))
                        wtoks["l"][par] = t.op(fn) if k == 7 else t.quiet(fn)
                    c_p1 = wtoks["l"][par]
                    v.wait(t, c_p1)
                    c_gm = v.op(lambda e, ex=ex, j=j: e.tensor_scalar(
                        sgt, pb[4][:, 0:512], b1c(ex, j), 7.0, ALU.add, ALU.min))
                    s.wait(v, c_gm)
                    c_sg = s.op(lambda e: e.activation(lint, sgt, AF.Sigmoid, scale=c_1702))
                    c_lin = v.op(lambda e, ex=ex, j=j: e.tensor_scalar(
                        sst, pb[5][:, 0:512], b1c(ex, 8 + j), 7.0, ALU.add, ALU.min))
                    v.wait(v, c_lin)
                    v.quiet(lambda e: e.tensor_scalar(sst, sst, -7.0, 1.0, ALU.max, ALU.add))
                    v.wait(s, c_sg)
                    v.quiet(lambda e: e.tensor_tensor(sgt, sgt, lint, ALU.mult))
                    c_ep = v.op(lambda e, j=j: e.tensor_tensor(act[j], sgt, sst, ALU.mult))
                v.wait(t, c_cb)
                c_aw = 0
                for j in range(8):
                    c_aw = v.op(lambda e, j=j: e.tensor_tensor(act[j], act[j], pb[6][:, 0:512], ALU.mult))
                for hoc in range(8):
                    par = hoc % 2
                    q.wait(t, wtoks["2"].get(par, 0))
                    q.wait(q, c_wprev)
                    c_w2d = q.dma(lambda e, ex=ex, hoc=hoc, par=par: e.dma_start(out=w2_b[par], in_=m2s[ex, hoc]))
                    c_wprev = c_w2d
                    t.wait(q, c_w2d)
                    t.wait(v, c_aw if hoc == 0 else c_acc_tok)
                    for k in range(8):
                        fn = lambda e, k=k, par=par: e.matmul(
                            pb[4][:, 0:512], w2_b[par][:, 128 * k:128 * (k + 1)], act[k],
                            start=(k == 0), stop=(k == 7))
                        wtoks["2"][par] = t.op(fn) if k == 7 else t.quiet(fn)
                    c_p2 = wtoks["2"][par]
                    v.wait(t, c_p2)
                    c_acc_tok = v.op(lambda e, hoc=hoc, cs_=cs_: e.tensor_tensor(
                        macc[hoc][:, cs_:cs_ + 512], macc[hoc][:, cs_:cs_ + 512], pb[4][:, 0:512], ALU.add))
            for hoc in range(8):
                t.wait(v, c_acc_tok)
                for k in range(1):
                    c_bt = t.op(lambda e, hoc=hoc, cs_=cs_: e.matmul(
                        pb[4][:, 0:512], b2a[0:8, 128 * hoc:128 * (hoc + 1)], comb[0:8, cs_:cs_ + 512],
                        start=True, stop=True))
                v.wait(t, c_bt)
                c_acc_tok = v.op(lambda e, hoc=hoc, cs_=cs_: e.tensor_tensor(
                    macc[hoc][:, cs_:cs_ + 512], macc[hoc][:, cs_:cs_ + 512], pb[4][:, 0:512], ALU.add))

        c_xst_used = 0
        c_ostore = 0
        for ct in range(8):
            v.wait(v, c_acc_tok)
            c_mo = v.op(lambda e, ct=ct: e.tensor_tensor(mot, macc[ct], xn1[ct], ALU.add))
            s.wait(v, c_mo)
            c_g1 = s.op(lambda e: e.activation(mot, mot, AF.Silu))
            v.wait(s, c_g1)
            c_sg1 = v.op(lambda e, ct=ct: e.tensor_tensor(mot, mot, s1[ct], ALU.mult))
            for ch in range(4):
                q.wait(v, c_xst_used)
                q.wait(q, c_ostore)
                c_x1 = q.dma(lambda e, ct=ct, ch=ch: e.dma_start(
                    out=xst, in_=x1_i[128 * ct:128 * (ct + 1), 512 * ch:512 * (ch + 1)]))
                v.wait(q, c_x1)
                v.wait(v, c_sg1)
                if c_ostore:
                    v.wait(q, c_ostore)
                c_xst_used = v.op(lambda e, ch=ch: e.tensor_tensor(xst, xst, mot[:, 512 * ch:512 * (ch + 1)], ALU.add))
                q.wait(v, c_xst_used)
                c_ostore = q.dma(lambda e, ct=ct, ch=ch: e.dma_start(
                    out=out_o[128 * ct:128 * (ct + 1), 512 * ch:512 * (ch + 1)], in_=xst))

        g.wait(q, q.n)
        g.op(lambda e: e.memset(A32[0:1, 0:1], 0.0))
        S.finalize(nc, stack)
    return nc


def prep_moe1(inp, pre, s1_full, x1_full):
    gwT = np.ascontiguousarray(np.asarray(inp["L1_ffn_gate_weight"]).T)
    gb = np.ascontiguousarray(np.asarray(inp["L1_ffn_gate_bias"]).reshape(1, 8))
    m1w = np.asarray(inp["L1_ffn_mlp1_weight"])
    m2w = np.asarray(inp["L1_ffn_mlp2_weight"])
    b1w = np.asarray(inp["L1_ffn_mlp1_bias"]).astype(np.float32)
    b2w = np.asarray(inp["L1_ffn_mlp2_bias"])
    m1p = np.empty((8, 16, 128, 1024), dtype=BF)
    b1p = np.empty((8, 128, 16), dtype=np.float32)
    m2p = np.empty((8, 8, 128, 1024), dtype=BF)
    for ex in range(8):
        gT = np.ascontiguousarray(m1w[ex, 0::2, :].T)
        lT = np.ascontiguousarray(m1w[ex, 1::2, :].T)
        m1p[ex, 0:8] = pack_lhsT(gT)
        m1p[ex, 8:16] = pack_lhsT(lT)
        b1p[ex, :, 0:8] = b1w[ex, 0::2].reshape(8, 128).T
        b1p[ex, :, 8:16] = b1w[ex, 1::2].reshape(8, 128).T
        m2p[ex] = pack_lhsT(np.ascontiguousarray(m2w[ex].T))
    b2p = np.ascontiguousarray(np.transpose(b2w.reshape(8, 8, 128), (1, 0, 2)))
    nw = _f32(inp["L1_ffn_norm_weight"]).reshape(CM, 1)
    eyeb = np.eye(128, dtype=np.float32).astype(BF)
    ones = np.ones((128, 128), dtype=BF)
    esel8 = np.zeros((8, 1024), dtype=BF)
    for ee in range(8):
        esel8[ee, 128 * ee:128 * (ee + 1)] = 1
    maps = []
    for c in range(NCORE):
        b, th = c // 2, c % 2
        sl = slice(TC * th, TC * (th + 1))
        maps.append({
            "xn1_i": np.ascontiguousarray(pre["xn1"][b][:, sl]),
            "s1_i": np.ascontiguousarray(s1_full[b][:, sl]),
            "x1_i": np.ascontiguousarray(x1_full[b][:, sl]),
            "nw_i": nw, "gw_i": gwT, "gb_i": gb,
            "m1s": m1p, "b1s": b1p, "m2s": m2p, "b2lh": b2p,
            "eye": eyeb, "onesall": ones, "esel8_i": esel8,
        })
    return maps


def run_moe1(inp, pre, s1_full, x1_full):
    nc = build_moe1()
    res = run_bass_kernel_spmd(nc, prep_moe1(inp, pre, s1_full, x1_full), core_ids=list(range(NCORE)))
    out = np.empty((BD, CM, SQ), dtype=np.float32)
    for c in range(NCORE):
        b, th = c // 2, c % 2
        out[b][:, TC * th:TC * (th + 1)] = np.asarray(res.results[c]["out_o"])
    return out



def kernel(**inputs):
    xn, sp, _ = run_attn0(inputs)
    x1 = run_ffn0(inputs, xn, sp)
    pre = run_pre1(inputs, x1)
    y = run_scan1(pre, inputs)
    s1 = run_post1(inputs, pre, y)
    out = run_moe1(inputs, pre, s1, x1)
    return np.ascontiguousarray(np.transpose(out, (0, 2, 1)))


# version 04 (final)
# speedup vs baseline: 1.0253x; 1.0253x over previous
import numpy as np
import ml_dtypes
from contextlib import ExitStack

import concourse.bass as bass
import concourse.mybir as mybir
from concourse.bass_utils import run_bass_kernel_spmd

F32 = mybir.dt.float32
BF16 = mybir.dt.bfloat16
AF = mybir.ActivationFunctionType
ALU = mybir.AluOpType
AX = mybir.AxisListType

BD, SQ, CM = 4, 4096, 1024
NCORE = 8
TC = 2048
LN_EPS = 1e-5
ATTN_EPS = 1e-6
BF = ml_dtypes.bfloat16
ENGS = ("sync", "tensor", "vector", "scalar", "gpsimd")


class Q:
    def __init__(self, name):
        self.name = name
        self.n = 0
        self.items = []
        self._w = []

    def wait(self, q, val):
        if val and val > 0:
            self._w.append((q, val))

    def op(self, fn, inc=1):
        self.items.append((fn, inc, self._w))
        self._w = []
        self.n += inc
        return self.n

    def dma(self, fn):
        return self.op(fn, 16)

    def quiet(self, fn):
        self.items.append((fn, 0, self._w))
        self._w = []
        return self.n


class Sch:
    def __init__(self):
        self.q = {e: Q(e) for e in ENGS}

    def finalize(self, nc, stack):
        sems = {e: stack.enter_context(nc.semaphore(f"sem_{e}")) for e in ENGS}
        block = stack.enter_context(nc.Block())

        def mk_replay(qq):
            def replay(eng):
                for fn, inc, waits in qq.items:
                    for (wq, val) in waits:
                        eng.wait_ge(sems[wq.name], val)
                    inst = fn(eng)
                    if inc:
                        inst.then_inc(sems[qq.name], inc)
            return replay

        for e in ENGS:
            if self.q[e].items:
                getattr(block, e)(mk_replay(self.q[e]))


def _bfa(x):
    return np.ascontiguousarray(np.asarray(x)).astype(BF)


def _f32(x):
    return np.ascontiguousarray(np.asarray(x), dtype=np.float32)


def pack_lhsT(wT, octile=128):
    I, O = wT.shape
    assert I % 128 == 0 and O % octile == 0
    a = wT.reshape(I // 128, 128, O // octile, octile)
    a = np.transpose(a, (2, 1, 0, 3))
    return np.ascontiguousarray(a.reshape(O // octile, 128, (I // 128) * octile))



def build_attn0(dbg=False):
    nc = bass.Bass("TRN2", target_bir_lowering=False)
    xall = nc.dram_tensor("xall", [CM, SQ], F32, kind="ExternalInput").ap()
    fil = nc.dram_tensor("fil", [512, 16], F32, kind="ExternalInput").ap()
    eye = nc.dram_tensor("eye", [128, 128], BF16, kind="ExternalInput").ap()
    onesall = nc.dram_tensor("onesall", [128, 128], BF16, kind="ExternalInput").ap()
    xn_o = nc.dram_tensor("xn_o", [512, SQ], BF16, kind="ExternalOutput").ap()
    sp_o = nc.dram_tensor("sp_o", [512, SQ], BF16, kind="ExternalOutput").ap()

    S = Sch()
    q, t, v, s, g = (S.q[e] for e in ENGS)

    with ExitStack() as stack:
        A16 = stack.enter_context(nc.sbuf_tensor("a16", [128, 76000], BF16))
        A32 = stack.enter_context(nc.sbuf_tensor("a32", [128, 12700], F32))
        pb = [stack.enter_context(nc.psum_tensor(f"pb{i}", [128, 512], F32)) for i in range(7)]
        pbt = stack.enter_context(nc.psum_tensor("pbt", [128, 1024], BF16))

        def b16(off, n):
            return A16[:, off:off + n]

        def f32(off, n):
            return A32[:, off:off + n]

        eye_sb = b16(0, 128)
        ones_sb = b16(128, 128)
        kv_sb = b16(256, 512)
        xsq = b16(768, 4096)
        xoth = b16(4864, 4096)
        qb = [b16(768 + 1024 * i, 1024) for i in range(4)]
        qT = [b16(4864 + 512 * i, 512) for i in range(8)]
        vT = [b16(8960 + 512 * i, 512) for i in range(8)]
        num_sb = b16(13056, 1024)
        attn = [b16(14080 + 1026 * i, 1026) for i in range(4)]
        xn = [b16(18200 + 4100 * i, 4100) for i in range(4)]
        low1 = [b16(34600 + 2052 * i, 2052) for i in range(4)]
        high1 = [b16(34600 + 2052 * (4 + i), 2052) for i in range(4)]
        abc = b16(34600, 4096)
        cbc = b16(34600 + 4096, 4096)
        low2 = [b16(51016 + 1028 * i, 1028) for i in range(4)]
        high2 = [b16(51016 + 1028 * (4 + i), 1028) for i in range(4)]
        syn1 = [b16(59240 + 2052 * i, 2052) for i in range(4)]
        spre0 = b16(67448, 4096)
        spre1 = b16(67448 + 4096, 4096)
        spre = [spre0, spre1]

        xin = f32(0, 4096)
        acc1 = f32(4200, 2048)
        acc2 = f32(4200, 4096)
        fil32 = [f32(8300 + 16 * i, 16) for i in range(4)]
        syn32 = [f32(8364 + 20 * i, 20) for i in range(4)]
        c_lneps = f32(8444, 1)
        c_mhalf = f32(8445, 1)

        srow = A32[0:1, 4200:8296]
        qrow = A32[0:1, 8500:12596]
        tmp_row = A32[0:1, 0:4096]
        a_rowb = A16[0:1, 14080:18176]
        c_rowb = A16[0:1, 67448:71544]
        denf = [A32[0:1, 0:4096], A32[0:1, 8500:12596]]
        den_row = A16[0:1, 67448:75640]

        c_eye = q.dma(lambda e: e.dma_start(out=eye_sb, in_=eye[:]))
        c_ones = q.dma(lambda e: e.dma_start(out=ones_sb, in_=onesall[:]))
        for ct in range(4):
            q.dma(lambda e, ct=ct: e.dma_start(out=fil32[ct][0:128, :], in_=fil[128 * ct:128 * (ct + 1), :]))
        c_fil = q.n

        v.wait(q, c_fil)
        for ct in range(4):
            fl, sy = fil32[ct], syn32[ct]
            for bse, d in ((8, 0), (12, 5)):
                v.quiet(lambda e, fl=fl, sy=sy, b=bse, d=d: e.tensor_copy(sy[:, d:d + 1], fl[:, b:b + 1]))
                v.quiet(lambda e, fl=fl, sy=sy, b=bse, d=d: e.tensor_add(sy[:, d + 1:d + 2], fl[:, b + 1:b + 2], fl[:, b + 2:b + 3]))
                v.quiet(lambda e, fl=fl, sy=sy, b=bse, d=d: e.tensor_copy(sy[:, d + 2:d + 3], fl[:, b + 3:b + 4]))
                v.quiet(lambda e, fl=fl, sy=sy, b=bse, d=d: e.tensor_add(sy[:, d + 3:d + 4], fl[:, b:b + 1], fl[:, b + 1:b + 2]))
                v.quiet(lambda e, fl=fl, sy=sy, b=bse, d=d: e.tensor_add(sy[:, d + 4:d + 5], fl[:, b + 2:b + 3], fl[:, b + 3:b + 4]))
        v.quiet(lambda e: e.memset(srow, 0.0))
        v.quiet(lambda e: e.memset(qrow, 0.0))
        v.quiet(lambda e: e.memset(c_lneps, LN_EPS))
        v.quiet(lambda e: e.memset(c_mhalf, -0.5))
        pads = []
        for ct in range(4):
            pads += [(xn[ct], 0, 1), (xn[ct], 4097, 4100),
                     (low2[ct], 0, 1), (low2[ct], 1025, 1028), (high2[ct], 0, 1), (high2[ct], 1025, 1028),
                     (syn1[ct], 0, 1), (syn1[ct], 2049, 2052)]
        for tile, a, b in pads[:-1]:
            v.quiet(lambda e, tile=tile, a=a, b=b: e.memset(tile[:, a:b], 0.0))
        tile, a, b = pads[-1]
        c_init = v.op(lambda e, tile=tile, a=a, b=b: e.memset(tile[:, a:b], 0.0))

        c_cast = {}
        c_mm_ct = {}
        ev_tok = {}
        q.wait(q, c_fil)
        for ct in range(8):
            q.wait(v, c_cast.get(ct - 1, 0))
            c_x = q.dma(lambda e, ct=ct: e.dma_start(out=xin[0:128, :], in_=xall[128 * ct:128 * (ct + 1), :]))
            bt = xn[ct][:, 1:4097] if ct < 4 else xoth
            v.wait(q, c_x)
            if ct >= 5:
                v.wait(t, c_mm_ct[ct - 1])
            c_cast[ct] = v.op(lambda e, bt=bt: e.tensor_copy(bt, xin[0:128, :]))
            s.wait(v, c_cast[ct])
            if ct >= 1:
                s.wait(t, c_mm_ct[ct - 1])
            c_sqr = s.op(lambda e, bt=bt: e.activation(xsq, bt, AF.Square))
            t.wait(s, c_sqr)
            if ct == 0:
                t.wait(q, c_ones)
                t.wait(v, c_init)
            for j in range(8):
                m = ct * 8 + j
                bs, bq = pb[j % 2], pb[2 + j % 2]
                if m >= 2:
                    t.wait(v, ev_tok[m - 2])
                t.quiet(lambda e, j=j, bt=bt, bs=bs: e.matmul(
                    bs[:, 0:512], ones_sb, bt[:, 512 * j:512 * (j + 1)], start=True, stop=True))
                c_mm = t.op(lambda e, j=j, bq=bq: e.matmul(
                    bq[:, 0:512], ones_sb, xsq[:, 512 * j:512 * (j + 1)], start=True, stop=True))
                v.wait(t, c_mm)
                v.quiet(lambda e, j=j, bs=bs: e.tensor_add(srow[:, 512 * j:512 * (j + 1)], srow[:, 512 * j:512 * (j + 1)], bs[0:1, 0:512]))
                ev_tok[m] = v.op(lambda e, j=j, bq=bq: e.tensor_add(qrow[:, 512 * j:512 * (j + 1)], qrow[:, 512 * j:512 * (j + 1)], bq[0:1, 0:512]))
            c_mm_ct[ct] = t.n

        v.quiet(lambda e: e.tensor_scalar_mul(srow, srow, 1.0 / CM))
        v.quiet(lambda e: e.tensor_scalar_mul(qrow, qrow, 1.0 / CM))
        v.quiet(lambda e: e.tensor_tensor(tmp_row, srow, srow, ALU.mult))
        c_var = v.op(lambda e: e.tensor_tensor(qrow, qrow, tmp_row, ALU.subtract))
        s.wait(v, c_var)
        s.quiet(lambda e: e.activation(qrow, qrow, AF.Ln, bias=c_lneps[0:1, 0:1]))
        c_rstd = s.op(lambda e: e.activation(qrow, qrow, AF.Exp, scale=c_mhalf[0:1, 0:1]))
        v.wait(s, c_rstd)
        v.quiet(lambda e: e.tensor_tensor(tmp_row, srow, qrow, ALU.mult))
        v.quiet(lambda e: e.tensor_copy(a_rowb, qrow))
        c_rowsb = v.op(lambda e: e.tensor_copy(c_rowb, tmp_row))

        t.wait(v, c_rowsb)
        for j in range(8):
            c_ba = t.op(lambda e, j=j: e.matmul(pb[4][:, 0:512], ones_sb[0:1, 0:128], a_rowb[0:1, 512 * j:512 * (j + 1)], start=True, stop=True))
            s.wait(t, c_ba)
            c_ea = s.op(lambda e, j=j: e.copy(abc[:, 512 * j:512 * (j + 1)], pb[4][:, 0:512]))
            t.wait(s, c_ea)
            c_bc2 = t.op(lambda e, j=j: e.matmul(pb[4][:, 0:512], ones_sb[0:1, 0:128], c_rowb[0:1, 512 * j:512 * (j + 1)], start=True, stop=True))
            s.wait(t, c_bc2)
            c_ec = s.op(lambda e, j=j: e.copy(cbc[:, 512 * j:512 * (j + 1)], pb[4][:, 0:512]))
            t.wait(s, c_ec)
        c_bcast = s.n

        v.wait(s, c_bcast)
        for ct in range(4):
            v.quiet(lambda e, ct=ct: e.tensor_tensor(xn[ct][:, 1:4097], xn[ct][:, 1:4097], abc, ALU.mult))
            c_app = v.op(lambda e, ct=ct: e.tensor_tensor(xn[ct][:, 1:4097], xn[ct][:, 1:4097], cbc, ALU.subtract))
            q.wait(v, c_app)
            q.dma(lambda e, ct=ct: e.dma_start(out=xn_o[128 * ct:128 * (ct + 1), :], in_=xn[ct][:, 1:4097]))
        c_xn_done = v.n

        for ct in range(4):
            for tile, a, b in ((low1[ct], 0, 1), (low1[ct], 2049, 2052),
                               (high1[ct], 0, 1), (high1[ct], 2049, 2052),
                               (attn[ct], 0, 1), (attn[ct], 1025, 1026)):
                v.quiet(lambda e, tile=tile, a=a, b=b: e.memset(tile[:, a:b], 0.0))

        def conv(dst, src, filc, n_out, acc):
            for k in range(4):
                sl = src[:, k:k + 2 * n_out:2]
                if k == 0:
                    v.quiet(lambda e, sl=sl, filc=filc: e.tensor_scalar_mul(acc[:, 0:n_out], sl, filc[:, 0:1]))
                elif k < 3:
                    v.quiet(lambda e, sl=sl, filc=filc, k=k: e.scalar_tensor_tensor(acc[:, 0:n_out], sl, filc[:, k:k + 1], acc[:, 0:n_out], ALU.mult, ALU.add))
                else:
                    v.quiet(lambda e, sl=sl, filc=filc, k=k: e.scalar_tensor_tensor(dst[:, 1:1 + n_out], sl, filc[:, k:k + 1], acc[:, 0:n_out], ALU.mult, ALU.add))

        for ct in range(4):
            conv(low1[ct], xn[ct], fil32[ct][0:128, 0:4], 2048, acc1)
            conv(high1[ct], xn[ct], fil32[ct][0:128, 4:8], 2048, acc1)
        for ct in range(4):
            conv(low2[ct], low1[ct], fil32[ct][0:128, 0:4], 1024, acc1)
            conv(high2[ct], low1[ct], fil32[ct][0:128, 4:8], 1024, acc1)
        c_pyr = v.op(lambda e: e.memset(denf[0][0:1, 0:1], 0.0))

        for ct in range(4):
            h_d = low2[ct][:, 1:1025]
            c_qm = v.op(lambda e, h_d=h_d, ct=ct: e.tensor_scalar_min(qb[ct], h_d, 0.0))
            s.wait(v, c_qm)
            c_qe = s.op(lambda e, ct=ct: e.activation(qb[ct], qb[ct], AF.Exp))
            v.wait(s, c_qe)
            v.op(lambda e, h_d=h_d, ct=ct: e.scalar_tensor_tensor(qb[ct], h_d, 0.0, qb[ct], ALU.max, ALU.add))
        c_q = v.n

        t.wait(v, c_q)
        t.wait(q, c_eye)
        c_ev_t = 0
        for src_is_q in (True, False):
            for ct in range(4):
                for pt in range(8):
                    reg = pbt[:, 128 * (pt % 2):128 * (pt % 2) + 128]
                    t.wait(s, c_ev_t)
                    if src_is_q:
                        src = qb[ct][:, 128 * pt:128 * (pt + 1)]
                    else:
                        src = low2[ct][:, 1 + 128 * pt:1 + 128 * (pt + 1)]
                    c_tp = t.op(lambda e, src=src, reg=reg: e.transpose(reg, src, eye_sb))
                    s.wait(t, c_tp)
                    dst = (qT if src_is_q else vT)[pt][:, 128 * ct:128 * (ct + 1)]
                    c_ev_t = s.op(lambda e, dst=dst, reg=reg: e.copy(dst, reg))
        c_trans = c_ev_t

        t.wait(s, c_trans)
        for h in range(8):
            for pt in range(8):
                lhsT = qT[pt][:, 64 * h:64 * (h + 1)]
                rhs = vT[pt][:, 64 * h:64 * (h + 1)]
                out = pb[4][64 * (h % 2):64 * (h % 2) + 64, 64 * (h // 2):64 * (h // 2) + 64]
                last = (pt == 7 and h == 7)
                fn = lambda e, lhsT=lhsT, rhs=rhs, out=out, pt=pt: e.matmul(
                    out, lhsT, rhs, start=(pt == 0), stop=(pt == 7))
                c_kv = t.op(fn) if last else t.quiet(fn)
        s.wait(t, c_kv)
        c_kvsb = s.op(lambda e: e.copy(kv_sb, pb[4][:, 0:512]))
        if dbg:
            dkv = nc.dram_tensor("d_kv", [128, 512], BF16, kind="ExternalOutput").ap()
            dqt = nc.dram_tensor("d_qt0", [128, 512], BF16, kind="ExternalOutput").ap()
            dvt = nc.dram_tensor("d_vt0", [128, 512], BF16, kind="ExternalOutput").ap()
            q.wait(s, c_kvsb)
            q.dma(lambda e: e.dma_start(out=dkv[:], in_=kv_sb))
            q.dma(lambda e: e.dma_start(out=dqt[:], in_=qT[0]))
            q.dma(lambda e: e.dma_start(out=dvt[:], in_=vT[0]))

        v.wait(s, c_kvsb)
        c_red = {}
        for ct in range(4):
            c_red[ct] = v.op(lambda e, ct=ct: e.tensor_reduce(A32[:, 12600 + ct:12601 + ct], qb[ct], AX.X, ALU.add))
        c_z = 0
        for ct in range(4):
            v.wait(v, c_red[ct])
            c_z = v.op(lambda e, ct=ct: e.tensor_copy(A16[:, 13056 + ct:13057 + ct], A32[:, 12600 + ct:12601 + ct]))

        t.wait(v, c_z)
        for h in range(8):
            ct, par = h // 2, h % 2
            for ch in range(2):
                lhsT = A16[64 * par:64 * par + 64, 13056 + ct:13057 + ct]
                rhs = A16[64 * par:64 * par + 64, 768 + 1024 * ct + 512 * ch:768 + 1024 * ct + 512 * (ch + 1)]
                c_dmm = t.op(lambda e, lhsT=lhsT, rhs=rhs: e.matmul(
                    pb[5][0:1, 0:512], lhsT, rhs, start=True, stop=True))
                s.wait(t, c_dmm)
                dst = denf[h // 4][0:1, 1024 * (h % 4) + 512 * ch:1024 * (h % 4) + 512 * (ch + 1)]
                c_dev = s.op(lambda e, dst=dst: e.copy(dst, pb[5][0:1, 0:512]))
                t.wait(s, c_dev)

        v.wait(s, s.n)
        for r in range(2):
            ca = v.op(lambda e, r=r: e.tensor_scalar_add(denf[r], denf[r], ATTN_EPS))
            v.wait(v, ca)
            cb = v.op(lambda e, r=r: e.reciprocal(denf[r], denf[r]))
            v.wait(v, cb)
            v.op(lambda e, r=r: e.tensor_copy(den_row[0:1, 4096 * r:4096 * (r + 1)], denf[r]))
        c_recip = v.n
        if dbg:
            dden = nc.dram_tensor("d_den", [1, 8192], BF16, kind="ExternalOutput").ap()
            dz = nc.dram_tensor("d_z", [128, 4], BF16, kind="ExternalOutput").ap()
            q.wait(v, c_recip)
            q.dma(lambda e: e.dma_start(out=dden[:], in_=den_row[0:1, 0:8192]))
            q.dma(lambda e: e.dma_start(out=dz[:], in_=num_sb[:, 0:4]))

        t.wait(v, c_recip)
        c_nev = 0
        for h in range(8):
            ct, par = h // 2, h % 2
            for ch in range(2):
                lhsT = kv_sb[64 * par:64 * par + 64, 64 * (h // 2):64 * (h // 2) + 64]
                rhs = qb[ct][64 * par:64 * par + 64, 512 * ch:512 * (ch + 1)]
                t.wait(v, c_nev)
                t.quiet(lambda e, lhsT=lhsT, rhs=rhs, par=par: e.matmul(
                    pb[5][64 * par:64 * par + 64, 0:512], lhsT, rhs, start=True, stop=True))
                c_nmm = t.op(lambda e, h=h, ch=ch, par=par: e.matmul(
                    pb[6][64 * par:64 * par + 64, 0:512], ones_sb[0:1, 0:64],
                    den_row[0:1, 1024 * h + 512 * ch:1024 * h + 512 * (ch + 1)], start=True, stop=True))
                s.wait(t, c_nmm)
                c_ns = s.op(lambda e, par=par: e.copy(num_sb[64 * par:64 * par + 64, 512:1024], pb[5][64 * par:64 * par + 64, 0:512]))
                v.wait(s, c_ns)
                c_nev = v.op(lambda e, ct=ct, par=par, ch=ch: e.tensor_tensor(
                    attn[ct][64 * par:64 * par + 64, 1 + 512 * ch:1 + 512 * (ch + 1)],
                    num_sb[64 * par:64 * par + 64, 512:1024],
                    pb[6][64 * par:64 * par + 64, 0:512], ALU.mult))
        c_attn = c_nev

        def synth(dst, A, H, sy, n2, extra=None):
            aE, aO = acc2[:, 0:n2], acc2[:, n2:2 * n2]
            v.quiet(lambda e, A=A, sy=sy, aE=aE: e.tensor_scalar_mul(aE, A[:, 0:n2], sy[:, 0:1]))
            v.quiet(lambda e, A=A, sy=sy, aE=aE: e.scalar_tensor_tensor(aE, A[:, 1:1 + n2], sy[:, 1:2], aE, ALU.mult, ALU.add))
            v.quiet(lambda e, A=A, sy=sy, aE=aE: e.scalar_tensor_tensor(aE, A[:, 2:2 + n2], sy[:, 2:3], aE, ALU.mult, ALU.add))
            v.quiet(lambda e, H=H, sy=sy, aE=aE: e.scalar_tensor_tensor(aE, H[:, 0:n2], sy[:, 5:6], aE, ALU.mult, ALU.add))
            v.quiet(lambda e, H=H, sy=sy, aE=aE: e.scalar_tensor_tensor(aE, H[:, 1:1 + n2], sy[:, 6:7], aE, ALU.mult, ALU.add))
            v.quiet(lambda e, A=A, sy=sy, aO=aO: e.tensor_scalar_mul(aO, A[:, 1:1 + n2], sy[:, 3:4]))
            v.quiet(lambda e, A=A, sy=sy, aO=aO: e.scalar_tensor_tensor(aO, A[:, 2:2 + n2], sy[:, 4:5], aO, ALU.mult, ALU.add))
            v.quiet(lambda e, H=H, sy=sy, aO=aO: e.scalar_tensor_tensor(aO, H[:, 1:1 + n2], sy[:, 8:9], aO, ALU.mult, ALU.add))
            if extra is None:
                v.quiet(lambda e, H=H, sy=sy, aE=aE, dst=dst, n2=n2: e.scalar_tensor_tensor(
                    dst[:, 1:1 + 2 * n2:2], H[:, 2:2 + n2], sy[:, 7:8], aE, ALU.mult, ALU.add))
                return v.op(lambda e, H=H, sy=sy, aO=aO, dst=dst, n2=n2: e.scalar_tensor_tensor(
                    dst[:, 2:2 + 2 * n2:2], H[:, 2:2 + n2], sy[:, 9:10], aO, ALU.mult, ALU.add))
            else:
                v.quiet(lambda e, H=H, sy=sy, aE=aE: e.scalar_tensor_tensor(aE, H[:, 2:2 + n2], sy[:, 7:8], aE, ALU.mult, ALU.add))
                v.quiet(lambda e, H=H, sy=sy, aO=aO: e.scalar_tensor_tensor(aO, H[:, 2:2 + n2], sy[:, 9:10], aO, ALU.mult, ALU.add))
                v.quiet(lambda e, dst=dst, aE=aE, extra=extra, n2=n2: e.tensor_tensor(
                    dst[:, 0:2 * n2:2], aE, extra[:, 1:1 + 2 * n2:2], ALU.add))
                return v.op(lambda e, dst=dst, aO=aO, extra=extra, n2=n2: e.tensor_tensor(
                    dst[:, 1:2 * n2:2], aO, extra[:, 2:2 + 2 * n2:2], ALU.add))

        v.wait(s, s.n)
        for ct in range(4):
            synth(syn1[ct], attn[ct], high2[ct], syn32[ct], 1024)
        c_sp_dma = {}
        for ct in range(4):
            par = ct % 2
            if ct >= 2:
                v.wait(q, c_sp_dma[ct - 2])
            c_sp = synth(spre[par], syn1[ct], high1[ct], syn32[ct], 2048, extra=xn[ct])
            q.wait(v, c_sp)
            c_sp_dma[ct] = q.dma(lambda e, ct=ct, par=par: e.dma_start(
                out=sp_o[128 * ct:128 * (ct + 1), :], in_=spre[par]))
        if dbg:
            q.wait(v, v.n)
            q.wait(s, s.n)
            for name, tile in (("d_low1", low1[0][:, 1:2049]), ("d_high1", high1[0][:, 1:2049]),
                               ("d_low2", low2[0][:, 1:1025]), ("d_high2", high2[0][:, 1:1025]),
                               ("d_q", qb[0]), ("d_attn", attn[0][:, 1:1025]),
                               ("d_syn1", syn1[0][:, 1:2049])):
                dt = nc.dram_tensor(name, list(tile.shape), BF16, kind="ExternalOutput").ap()
                q.dma(lambda e, dt=dt, tile=tile: e.dma_start(out=dt[:], in_=tile))

        g.wait(q, q.n)
        g.op(lambda e: e.memset(acc1[0:1, 0:1], 0.0))

        S.finalize(nc, stack)
    return nc


def prep_attn0(inp):
    x = _f32(inp["L0_attn_x"])
    filT = np.concatenate(
        [_f32(inp[k]).T for k in ("L0_attn_h_filter", "L0_attn_g_filter",
                                  "L0_attn_h_synth", "L0_attn_g_synth")], axis=1)
    eye = np.eye(128, dtype=np.float32).astype(BF)
    ones = np.ones((128, 128), dtype=BF)
    maps = []
    for c in range(NCORE):
        b, hc = c // 2, c % 2
        xt = x[b].T
        xall = np.ascontiguousarray(np.concatenate(
            [xt[512 * hc:512 * (hc + 1)], xt[512 * (1 - hc):512 * (2 - hc)]], axis=0))
        maps.append({"xall": xall,
                     "fil": np.ascontiguousarray(filT[512 * hc:512 * (hc + 1)]),
                     "eye": eye, "onesall": ones})
    return maps


def run_attn0(inp, dbg=False):
    nc = build_attn0(dbg)
    res = run_bass_kernel_spmd(nc, prep_attn0(inp), core_ids=list(range(NCORE)))
    xn = np.empty((BD, CM, SQ), dtype=BF)
    sp = np.empty((BD, CM, SQ), dtype=BF)
    for c in range(NCORE):
        b, hc = c // 2, c % 2
        xn[b, 512 * hc:512 * (hc + 1)] = np.asarray(res.results[c]["xn_o"])
        sp[b, 512 * hc:512 * (hc + 1)] = np.asarray(res.results[c]["sp_o"])
    return xn, sp, res



def build_ffn0():
    nc = bass.Bass("TRN2", target_bir_lowering=False)
    x_i = nc.dram_tensor("x_i", [CM, TC], F32, kind="ExternalInput").ap()
    xn_i = nc.dram_tensor("xn_i", [CM, TC], BF16, kind="ExternalInput").ap()
    sp_i = nc.dram_tensor("sp_i", [CM, TC], BF16, kind="ExternalInput").ap()
    gb_i = nc.dram_tensor("gb_i", [CM, 2], F32, kind="ExternalInput").ap()
    scal_i = nc.dram_tensor("scal_i", [128, 136], F32, kind="ExternalInput").ap()
    onesall = nc.dram_tensor("onesall", [128, 128], BF16, kind="ExternalInput").ap()
    w1s = nc.dram_tensor("w1s", [32, 128, 1024], F32, kind="ExternalInput").ap()
    w2s = nc.dram_tensor("w2s", [32, 128, 1024], F32, kind="ExternalInput").ap()
    h1s = nc.dram_tensor("h1s", [32, 128, 4096], F32, kind="ExternalInput").ap()
    h2s = nc.dram_tensor("h2s", [32, 128, 4096], F32, kind="ExternalInput").ap()
    w3s = nc.dram_tensor("w3s", [8, 128, 4096], F32, kind="ExternalInput").ap()
    x1_o = nc.dram_tensor("x1_o", [CM, TC], F32, kind="ExternalOutput").ap()

    S = Sch()
    q, t, v, s, g = (S.q[e] for e in ENGS)
    NG, G = 4, 512

    with ExitStack() as stack:
        A16 = stack.enter_context(nc.sbuf_tensor("a16", [128, 84200], BF16))
        A32 = stack.enter_context(nc.sbuf_tensor("a32", [128, 8360], F32))
        pb = [stack.enter_context(nc.psum_tensor(f"pb{i}", [128, 512], F32)) for i in range(8)]

        def b16(off, n):
            return A16[:, off:off + n]

        xn = [b16(2048 * i, 2048) for i in range(8)]
        sb = [b16(16384 + 2048 * i, 2048) for i in range(8)]
        t1 = [b16(32768 + 512 * j, 512) for j in range(32)]
        t2 = [b16(49152 + 512 * j, 512) for j in range(32)]
        gbuf = [b16(65536 + 512 * i, 512) for i in range(8)]
        wcast = [b16(69632 + 4096 * i, 4096) for i in range(2)]
        abc = b16(77824, 2048)
        cbc = b16(79872, 2048)
        xsq = b16(81920, 2048)
        ones_sb = b16(83968, 128)
        a_rowb = A16[0:1, 32768:34816]
        c_rowb = A16[0:1, 34816:36864]

        wraw = A32[:, 0:4096]
        xbuf = A32[:, 4096:8192]
        scal = A32[:, 8192:8328]
        c_lneps = A32[:, 8328:8329]
        c_mhalf = A32[:, 8329:8330]
        gbc = A32[:, 8332:8348]
        srow = A32[0:1, 4096:6144]
        qrow = A32[0:1, 6144:8192]
        tmp_row = A32[0:1, 0:2048]

        q.dma(lambda e: e.dma_start(out=ones_sb, in_=onesall[:]))
        q.dma(lambda e: e.dma_start(out=scal, in_=scal_i[:]))
        for ct in range(8):
            q.dma(lambda e, ct=ct: e.dma_start(out=gbc[:, 2 * ct:2 * ct + 2], in_=gb_i[128 * ct:128 * (ct + 1), :]))
        for ct in range(8):
            q.dma(lambda e, ct=ct: e.dma_start(out=xn[ct], in_=xn_i[128 * ct:128 * (ct + 1), :]))
        for ct in range(8):
            q.dma(lambda e, ct=ct: e.dma_start(out=sb[ct], in_=sp_i[128 * ct:128 * (ct + 1), :]))
        c_in = q.n
        q.wait(q, c_in)

        v.wait(q, c_in)
        v.quiet(lambda e: e.memset(srow, 0.0))
        v.quiet(lambda e: e.memset(qrow, 0.0))
        v.quiet(lambda e: e.memset(c_lneps, LN_EPS))
        c_init = v.op(lambda e: e.memset(c_mhalf, -0.5))

        ev_tok = {}
        c_sq_prev = 0
        for ct in range(8):
            if ct == 0:
                s.wait(v, c_init)
            s.wait(t, c_sq_prev)
            c_sqr = s.op(lambda e, ct=ct: e.activation(xsq, sb[ct], AF.Square))
            t.wait(s, c_sqr)
            if ct == 0:
                t.wait(v, c_init)
            for j in range(4):
                m = ct * 4 + j
                bs_, bq_ = pb[j % 2], pb[2 + j % 2]
                if m >= 2:
                    t.wait(v, ev_tok[m - 2])
                t.quiet(lambda e, j=j, ct=ct, bs_=bs_: e.matmul(
                    bs_[:, 0:512], ones_sb, sb[ct][:, 512 * j:512 * (j + 1)], start=True, stop=True))
                c_mm = t.op(lambda e, j=j, bq_=bq_: e.matmul(
                    bq_[:, 0:512], ones_sb, xsq[:, 512 * j:512 * (j + 1)], start=True, stop=True))
                v.wait(t, c_mm)
                v.quiet(lambda e, j=j, bs_=bs_: e.tensor_add(srow[:, 512 * j:512 * (j + 1)], srow[:, 512 * j:512 * (j + 1)], bs_[0:1, 0:512]))
                ev_tok[m] = v.op(lambda e, j=j, bq_=bq_: e.tensor_add(qrow[:, 512 * j:512 * (j + 1)], qrow[:, 512 * j:512 * (j + 1)], bq_[0:1, 0:512]))
            c_sq_prev = t.n

        v.quiet(lambda e: e.tensor_scalar_mul(srow, srow, 1.0 / CM))
        v.quiet(lambda e: e.tensor_scalar_mul(qrow, qrow, 1.0 / CM))
        v.quiet(lambda e: e.tensor_tensor(tmp_row, srow, srow, ALU.mult))
        c_var = v.op(lambda e: e.tensor_tensor(qrow, qrow, tmp_row, ALU.subtract))
        s.wait(v, c_var)
        s.quiet(lambda e: e.activation(qrow, qrow, AF.Ln, bias=c_lneps[0:1, 0:1]))
        c_rstd = s.op(lambda e: e.activation(qrow, qrow, AF.Exp, scale=c_mhalf[0:1, 0:1]))
        v.wait(s, c_rstd)
        v.quiet(lambda e: e.tensor_tensor(tmp_row, srow, qrow, ALU.mult))
        v.quiet(lambda e: e.tensor_copy(a_rowb, qrow))
        c_rowsb = v.op(lambda e: e.tensor_copy(c_rowb, tmp_row))
        t.wait(v, c_rowsb)
        for j in range(4):
            for src, dst in ((a_rowb, abc), (c_rowb, cbc)):
                c_b1 = t.op(lambda e, j=j, src=src: e.matmul(
                    pb[4][:, 0:512], ones_sb[0:1, 0:128], src[0:1, 512 * j:512 * (j + 1)], start=True, stop=True))
                s.wait(t, c_b1)
                c_e1 = s.op(lambda e, j=j, dst=dst: e.copy(dst[:, 512 * j:512 * (j + 1)], pb[4][:, 0:512]))
                t.wait(s, c_e1)
        c_bcast = s.n

        v.wait(s, c_bcast)
        for ct in range(8):
            v.quiet(lambda e, ct=ct: e.tensor_tensor(sb[ct], sb[ct], abc, ALU.mult))
            v.quiet(lambda e, ct=ct: e.tensor_tensor(sb[ct], sb[ct], cbc, ALU.subtract))
            v.quiet(lambda e, ct=ct: e.tensor_scalar(
                sb[ct], sb[ct], gbc[:, 2 * ct:2 * ct + 1], gbc[:, 2 * ct + 1:2 * ct + 2], ALU.mult, ALU.add))
        c_sdone = v.op(lambda e: e.memset(tmp_row[0:1, 0:1], 0.0))

        c_comb_prev = c_sdone
        c_store_prev = 0
        c_prev_mm = 0
        c_cast_carry = 0
        for gi in range(NG):
            gs = G * gi

            tok_w = []
            ev1 = []
            for j in range(32):
                q.wait(v, c_cast_carry)
                c_w1 = q.dma(lambda e, j=j: e.dma_start(out=wraw[:, 0:1024], in_=w1s[j]))
                v.wait(q, c_w1)
                if tok_w:
                    v.wait(t, tok_w[-1])
                elif j == 0:
                    v.wait(t, c_prev_mm)
                c_c1 = v.op(lambda e: e.tensor_copy(wcast[0][:, 0:1024], wraw[:, 0:1024]))
                q.wait(v, c_c1)
                c_w2 = q.dma(lambda e, j=j: e.dma_start(out=wraw[:, 0:1024], in_=w2s[j]))
                v.wait(q, c_w2)
                c_c2 = v.op(lambda e: e.tensor_copy(wcast[1][:, 0:1024], wraw[:, 0:1024]))
                c_cast_carry = c_c2
                t.wait(v, c_c2)
                if gi == 0 and j == 0:
                    t.wait(q, c_in)
                    t.wait(v, c_sdone)
                if ev1:
                    t.wait(v, ev1[-1])
                else:
                    t.wait(v, c_comb_prev)
                for k in range(8):
                    t.quiet(lambda e, k=k, gs=gs: e.matmul(
                        pb[4][:, 0:G], wcast[0][:, 128 * k:128 * (k + 1)], xn[k][:, gs:gs + G],
                        start=(k == 0), stop=(k == 7)))
                for k in range(8):
                    fn = lambda e, k=k, gs=gs: e.matmul(
                        pb[5][:, 0:G], wcast[1][:, 128 * k:128 * (k + 1)], xn[k][:, gs:gs + G],
                        start=(k == 0), stop=(k == 7))
                    tok_w.append(t.op(fn)) if k == 7 else t.quiet(fn)
                s.wait(t, tok_w[-1])
                s.quiet(lambda e, j=j: e.activation(t1[j], pb[4][:, 0:G], AF.Silu, scale=scal[:, j:j + 1]))
                c_ev2 = s.op(lambda e, j=j: e.activation(xsq[:, 0:G], pb[5][:, 0:G], AF.Copy, scale=scal[:, 32 + j:33 + j]))
                v.wait(s, c_ev2)
                ev1.append(v.op(lambda e, j=j: e.tensor_tensor(t1[j], t1[j], xsq[:, 0:G], ALU.mult)))

            def mk_hidden(wdram, scal_off, src_list, dst_list, first_wait):
                nonlocal c_cast_carry
                toks, evs = [], []
                for j in range(32):
                    par = j % 2
                    q.wait(v, c_cast_carry)
                    c_w = q.dma(lambda e, j=j, wdram=wdram: e.dma_start(out=wraw[:, 0:4096], in_=wdram[j]))
                    v.wait(q, c_w)
                    if len(toks) >= 2:
                        v.wait(t, toks[-2])
                    elif j == 0:
                        v.wait(t, tok_w[-1] if wdram is h1s else hid_prev_mm[0])
                    c_cast = v.op(lambda e, par=par: e.tensor_copy(wcast[par][:, 0:4096], wraw[:, 0:4096]))
                    c_cast_carry = c_cast
                    t.wait(v, c_cast)
                    if j == 0:
                        eng, tokv = first_wait
                        t.wait(eng, tokv)
                    bank = pb[4 + par]
                    if len(evs) >= 2:
                        t.wait(s, evs[-2])
                    for k in range(32):
                        fn = lambda e, k=k, par=par, bank=bank: e.matmul(
                            bank[:, 0:G], wcast[par][:, 128 * k:128 * (k + 1)], src_list[k][:, 0:G],
                            start=(k == 0), stop=(k == 31))
                        toks.append(t.op(fn)) if k == 31 else t.quiet(fn)
                    s.wait(t, toks[-1])
                    evs.append(s.op(lambda e, j=j, bank=bank: e.activation(
                        dst_list[j], bank[:, 0:G], AF.Gelu, scale=scal[:, scal_off + j:scal_off + j + 1])))
                hid_prev_mm[0] = toks[-1]
                return evs

            hid_prev_mm = [0]
            evs_h1 = mk_hidden(h1s, 64, t1, t2, (v, ev1[-1]))
            evs_h2 = mk_hidden(h2s, 96, t2, t1, (s, evs_h1[-1]))

            toks3, evs3 = [], []
            for j in range(8):
                par = j % 2
                q.wait(v, c_cast_carry)
                c_w = q.dma(lambda e, j=j: e.dma_start(out=wraw[:, 0:4096], in_=w3s[j]))
                v.wait(q, c_w)
                if len(toks3) >= 2:
                    v.wait(t, toks3[-2])
                elif j == 0:
                    v.wait(t, hid_prev_mm[0])
                c_cast = v.op(lambda e, par=par: e.tensor_copy(wcast[par][:, 0:4096], wraw[:, 0:4096]))
                c_cast_carry = c_cast
                t.wait(v, c_cast)
                if j == 0:
                    t.wait(s, evs_h2[-1])
                bank = pb[4 + par]
                if len(evs3) >= 2:
                    t.wait(s, evs3[-2])
                for k in range(32):
                    fn = lambda e, k=k, par=par, bank=bank: e.matmul(
                        bank[:, 0:G], wcast[par][:, 128 * k:128 * (k + 1)], t1[k][:, 0:G],
                        start=(k == 0), stop=(k == 31))
                    toks3.append(t.op(fn)) if k == 31 else t.quiet(fn)
                s.wait(t, toks3[-1])
                evs3.append(s.op(lambda e, j=j, bank=bank: e.activation(
                    gbuf[j], bank[:, 0:G], AF.Silu, scale=scal[:, 128 + j:129 + j])))
            c_prev_mm = toks3[-1]

            for ct in range(8):
                g.wait(g, c_store_prev)
                if gi == 0 and ct == 0:
                    g.wait(v, c_sdone)
                c_x = g.dma(lambda e, ct=ct, gs=gs: e.dma_start(
                    out=xbuf[:, 0:G], in_=x_i[128 * ct:128 * (ct + 1), gs:gs + G]))
                v.wait(g, c_x)
                v.wait(s, evs3[ct])
                c_tm = v.op(lambda e, ct=ct, gs=gs: e.tensor_tensor(xsq[:, 0:G], sb[ct][:, gs:gs + G], gbuf[ct], ALU.mult))
                v.wait(v, c_tm)
                c_comb_prev = v.op(lambda e: e.tensor_tensor(xbuf[:, 0:G], xbuf[:, 0:G], xsq[:, 0:G], ALU.add))
                g.wait(v, c_comb_prev)
                c_store_prev = g.dma(lambda e, ct=ct, gs=gs: e.dma_start(
                    out=x1_o[128 * ct:128 * (ct + 1), gs:gs + G], in_=xbuf[:, 0:G]))

        g.wait(q, q.n)
        g.wait(g, g.n)
        g.op(lambda e: e.memset(A32[0:1, 0:1], 0.0))

        S.finalize(nc, stack)
    return nc


def prep_ffn0(inp, xn_full, sp_full):
    x = _f32(inp["L0_attn_x"])
    gb = np.stack([_f32(inp["L0_attn_norm_gamma"]), _f32(inp["L0_attn_norm_beta"])], axis=1)
    w1p = pack_lhsT(_f32(inp["L0_ffn_w1_w"]).T)
    w2p = pack_lhsT(_f32(inp["L0_ffn_w2_w"]).T)
    h1p = pack_lhsT(_f32(inp["L0_ffn_h1_w"]).T)
    h2p = pack_lhsT(_f32(inp["L0_ffn_h2_w"]).T)
    w3p = pack_lhsT(_f32(inp["L0_ffn_w3_w"]).T)
    scal = np.zeros((128, 136), np.float32)
    scal[:, 0:32] = _f32(inp["L0_ffn_w1_s"]).reshape(32, 128).T
    scal[:, 32:64] = _f32(inp["L0_ffn_w2_s"]).reshape(32, 128).T
    scal[:, 64:96] = _f32(inp["L0_ffn_h1_s"]).reshape(32, 128).T
    scal[:, 96:128] = _f32(inp["L0_ffn_h2_s"]).reshape(32, 128).T
    scal[:, 128:136] = _f32(inp["L0_ffn_w3_s"]).reshape(8, 128).T
    ones = np.ones((128, 128), dtype=BF)
    maps = []
    for c in range(NCORE):
        b, th = c // 2, c % 2
        sl = slice(TC * th, TC * (th + 1))
        maps.append({
            "x_i": np.ascontiguousarray(x[b].T[:, sl]),
            "xn_i": np.ascontiguousarray(xn_full[b][:, sl]),
            "sp_i": np.ascontiguousarray(sp_full[b][:, sl]),
            "gb_i": gb, "scal_i": scal, "onesall": ones,
            "w1s": w1p, "w2s": w2p, "h1s": h1p, "h2s": h2p, "w3s": w3p,
        })
    return maps


def run_ffn0(inp, xn_full, sp_full):
    nc = build_ffn0()
    res = run_bass_kernel_spmd(nc, prep_ffn0(inp, xn_full, sp_full), core_ids=list(range(NCORE)))
    x1 = np.empty((BD, CM, SQ), dtype=np.float32)
    for c in range(NCORE):
        b, th = c // 2, c % 2
        x1[b][:, TC * th:TC * (th + 1)] = np.asarray(res.results[c]["x1_o"])
    return x1



def build_pre1():
    nc = bass.Bass("TRN2", target_bir_lowering=False)
    x1_i = nc.dram_tensor("x1_i", [CM, TC], F32, kind="ExternalInput").ap()
    gb_i = nc.dram_tensor("gb_i", [CM, 2], F32, kind="ExternalInput").ap()
    wds = nc.dram_tensor("wds", [8, 128, 1024], F32, kind="ExternalInput").ap()
    bd_i = nc.dram_tensor("bd_i", [CM, 1], F32, kind="ExternalInput").ap()
    wbcs = nc.dram_tensor("wbcs", [1, 128, 256], F32, kind="ExternalInput").ap()
    bbc_i = nc.dram_tensor("bbc_i", [32, 1], F32, kind="ExternalInput").ap()
    xn1_o = nc.dram_tensor("xn1_o", [CM, TC], BF16, kind="ExternalOutput").ap()
    zn_o = nc.dram_tensor("zn_o", [CM, TC], BF16, kind="ExternalOutput").ap()
    dl_o = nc.dram_tensor("dl_o", [CM, TC], F32, kind="ExternalOutput").ap()
    xd_o = nc.dram_tensor("xd_o", [CM, TC], F32, kind="ExternalOutput").ap()
    bc_o = nc.dram_tensor("bc_o", [32, TC], F32, kind="ExternalOutput").ap()

    S = Sch()
    q, t, v, s, g = (S.q[e] for e in ENGS)

    with ExitStack() as stack:
        A16 = stack.enter_context(nc.sbuf_tensor("a16", [128, 20800], BF16))
        A32 = stack.enter_context(nc.sbuf_tensor("a32", [128, 29900], F32))
        pb = [stack.enter_context(nc.psum_tensor(f"pb{i}", [128, 512], F32)) for i in range(8)]

        def b16(off, n):
            return A16[:, off:off + n]

        zn = [b16(2048 * i, 2048) for i in range(8)]
        xscr = b16(16384, 2048)
        wcast = [b16(18432 + 1024 * i, 1024) for i in range(2)]

        x1 = [A32[:, 2048 * i:2048 * (i + 1)] for i in range(8)]
        wraw = A32[:, 16384:17408]
        dscr = A32[:, 17408:19456]
        dscr2 = A32[:, 19456:21504]
        abc32 = A32[:, 21504:23552]
        cbc32 = A32[:, 23552:25600]
        ones32 = A32[:, 25600:25728]
        srow = A32[0:1, 25728:27776]
        qrow = A32[0:1, 27776:29824]
        tmp_row = A32[0:1, 17408:19456]
        gbc = A32[:, 29824:29840]
        bd_col = A32[:, 29840:29848]
        bbc_col = A32[:, 29848:29849]
        c_lneps = A32[:, 29852:29853]
        c_mhalf = A32[:, 29853:29854]

        for ct in range(8):
            q.dma(lambda e, ct=ct: e.dma_start(out=gbc[:, 2 * ct:2 * ct + 2], in_=gb_i[128 * ct:128 * (ct + 1), :]))
        for j in range(8):
            q.dma(lambda e, j=j: e.dma_start(out=bd_col[:, j:j + 1], in_=bd_i[128 * j:128 * (j + 1), :]))
        q.dma(lambda e: e.dma_start(out=bbc_col[0:32, 0:1], in_=bbc_i[:]))
        for ct in range(8):
            q.dma(lambda e, ct=ct: e.dma_start(out=x1[ct], in_=x1_i[128 * ct:128 * (ct + 1), :]))
        c_in = q.n
        q.wait(q, c_in)

        v.wait(q, c_in)
        v.quiet(lambda e: e.memset(c_lneps, LN_EPS))
        v.quiet(lambda e: e.memset(c_mhalf, -0.5))
        c_init = v.op(lambda e: e.memset(ones32, 1.0))

        def ln_pass_f32(tiles, use_gb, export, out_dram, first_tok):
            v.quiet(lambda e: e.memset(srow, 0.0))
            c_z0 = v.op(lambda e: e.memset(qrow, 0.0))
            ev_tok = {}
            c_sq_prev = 0
            for ct in range(8):
                if ct == 0:
                    s.wait(v, max(c_z0, first_tok))
                s.wait(t, c_sq_prev)
                c_sqr = s.op(lambda e, ct=ct: e.activation(dscr, tiles[ct], AF.Square))
                t.wait(s, c_sqr)
                if ct == 0:
                    t.wait(v, c_init)
                for j in range(4):
                    m = ct * 4 + j
                    bs_, bq_ = pb[j % 2], pb[2 + j % 2]
                    if m >= 2:
                        t.wait(v, ev_tok[m - 2])
                    t.quiet(lambda e, j=j, ct=ct, bs_=bs_: e.matmul(
                        bs_[:, 0:512], ones32, tiles[ct][:, 512 * j:512 * (j + 1)], start=True, stop=True))
                    c_mm = t.op(lambda e, j=j, bq_=bq_: e.matmul(
                        bq_[:, 0:512], ones32, dscr[:, 512 * j:512 * (j + 1)], start=True, stop=True))
                    v.wait(t, c_mm)
                    v.quiet(lambda e, j=j, bs_=bs_: e.tensor_add(srow[:, 512 * j:512 * (j + 1)], srow[:, 512 * j:512 * (j + 1)], bs_[0:1, 0:512]))
                    ev_tok[m] = v.op(lambda e, j=j, bq_=bq_: e.tensor_add(qrow[:, 512 * j:512 * (j + 1)], qrow[:, 512 * j:512 * (j + 1)], bq_[0:1, 0:512]))
                c_sq_prev = t.n
            v.quiet(lambda e: e.tensor_scalar_mul(srow, srow, 1.0 / CM))
            v.quiet(lambda e: e.tensor_scalar_mul(qrow, qrow, 1.0 / CM))
            v.quiet(lambda e: e.tensor_tensor(tmp_row, srow, srow, ALU.mult))
            c_var = v.op(lambda e: e.tensor_tensor(qrow, qrow, tmp_row, ALU.subtract))
            s.wait(v, c_var)
            s.quiet(lambda e: e.activation(qrow, qrow, AF.Ln, bias=c_lneps[0:1, 0:1]))
            c_rstd = s.op(lambda e: e.activation(qrow, qrow, AF.Exp, scale=c_mhalf[0:1, 0:1]))
            v.wait(s, c_rstd)
            c_crow = v.op(lambda e: e.tensor_tensor(tmp_row, srow, qrow, ALU.mult))
            t.wait(v, c_crow)
            for j in range(4):
                for src, dst in ((qrow, abc32), (tmp_row, cbc32)):
                    c_b1 = t.op(lambda e, j=j, src=src: e.matmul(
                        pb[4][:, 0:512], ones32[0:1, 0:128], src[0:1, 512 * j:512 * (j + 1)], start=True, stop=True))
                    s.wait(t, c_b1)
                    c_e1 = s.op(lambda e, j=j, dst=dst: e.copy(dst[:, 512 * j:512 * (j + 1)], pb[4][:, 0:512]))
                    t.wait(s, c_e1)
            c_bcast = s.n
            v.wait(s, c_bcast)
            c_exp_dma = 0
            last = 0
            for ct in range(8):
                v.quiet(lambda e, ct=ct: e.tensor_tensor(tiles[ct], tiles[ct], abc32, ALU.mult))
                last = v.op(lambda e, ct=ct: e.tensor_tensor(tiles[ct], tiles[ct], cbc32, ALU.subtract))
                if use_gb:
                    last = v.op(lambda e, ct=ct: e.tensor_scalar(
                        tiles[ct], tiles[ct], gbc[:, 2 * ct:2 * ct + 1], gbc[:, 2 * ct + 1:2 * ct + 2], ALU.mult, ALU.add))
                dstt = zn[ct] if export == "zn" else xscr
                if export == "x" and c_exp_dma:
                    v.wait(g, c_exp_dma)
                c_cp = v.op(lambda e, ct=ct, dstt=dstt: e.tensor_copy(dstt, tiles[ct]))
                g.wait(v, c_cp)
                c_exp_dma = g.dma(lambda e, ct=ct, dstt=dstt: e.dma_start(
                    out=out_dram[128 * ct:128 * (ct + 1), :], in_=dstt))
            return last, c_exp_dma

        c_xn1, c_dma1 = ln_pass_f32(x1, False, "x", xn1_o, c_init)
        c_zn, c_dma2 = ln_pass_f32(x1, True, "zn", zn_o, 0)
        zn32 = x1

        tok_mm = []
        c_cast = 0
        c_store = {}
        for j in range(8):
            par = j % 2
            q.wait(v, c_cast)
            c_w = q.dma(lambda e, j=j: e.dma_start(out=wraw, in_=wds[j]))
            v.wait(q, c_w)
            if len(tok_mm) >= 2:
                v.wait(t, tok_mm[-2])
            if j == 0:
                v.wait(v, c_zn)
            c_cast = v.op(lambda e, par=par: e.tensor_copy(wcast[par][:, 0:1024], wraw))
            t.wait(v, c_cast)
            for ch in range(4):
                bank = pb[4 + (2 * j + ch) % 2]
                if j * 4 + ch >= 2:
                    t.wait(v, c_store.get(j * 4 + ch - 2, 0))
                for k in range(8):
                    fn = lambda e, k=k, par=par, ch=ch, bank=bank: e.matmul(
                        bank[:, 0:512], wcast[par][:, 128 * k:128 * (k + 1)], zn[k][:, 512 * ch:512 * (ch + 1)],
                        start=(k == 0), stop=(k == 7))
                    tok_mm.append(t.op(fn)) if k == 7 else t.quiet(fn)
                v.wait(t, tok_mm[-1])
                v.wait(g, c_store.get(("st", j * 4 + ch - 1), 0))
                c_u = v.op(lambda e, j=j, bank=bank: e.tensor_scalar_add(dscr[:, 0:512], bank[:, 0:512], bd_col[:, j:j + 1]))
                s.wait(v, c_u)
                s.quiet(lambda e: e.activation(dscr2[:, 0:512], dscr[:, 0:512], AF.Abs))
                s.quiet(lambda e: e.activation(dscr2[:, 0:512], dscr2[:, 0:512], AF.Exp, scale=-1.0))
                c_ln = s.op(lambda e: e.activation(dscr2[:, 0:512], dscr2[:, 0:512], AF.Ln, bias=1.0))
                v.wait(s, c_ln)
                c_d = v.op(lambda e: e.scalar_tensor_tensor(dscr[:, 0:512], dscr[:, 0:512], 0.0, dscr2[:, 0:512], ALU.max, ALU.add))
                g.wait(v, c_d)
                g.dma(lambda e, j=j, ch=ch: e.dma_start(out=dl_o[128 * j:128 * (j + 1), 512 * ch:512 * (ch + 1)], in_=dscr[:, 0:512]))
                v.wait(v, c_d)
                c_xd = v.op(lambda e, j=j, ch=ch: e.tensor_tensor(
                    dscr2[:, 0:512], dscr[:, 0:512], zn32[j][:, 512 * ch:512 * (ch + 1)], ALU.mult))
                g.wait(v, c_xd)
                c_store[("st", j * 4 + ch)] = g.dma(lambda e, j=j, ch=ch: e.dma_start(
                    out=xd_o[128 * j:128 * (j + 1), 512 * ch:512 * (ch + 1)], in_=dscr2[:, 0:512]))
                c_store[j * 4 + ch] = c_xd

        c_laststore = g.n
        q.wait(v, c_cast)
        c_wbc = q.dma(lambda e: e.dma_start(out=wraw[:, 0:256], in_=wbcs[0]))
        v.wait(g, c_laststore)
        for ch in range(4):
            bank = pb[6]
            if ch >= 1:
                t.wait(v, c_bcev)
            else:
                t.wait(q, c_wbc)
            for k in range(8):
                fn = lambda e, k=k, ch=ch, bank=bank: e.matmul(
                    bank[0:32, 0:512], wraw[:, 32 * k:32 * (k + 1)], zn32[k][:, 512 * ch:512 * (ch + 1)],
                    start=(k == 0), stop=(k == 7))
                c_bcmm = t.op(fn) if k == 7 else t.quiet(fn)
            v.wait(t, c_bcmm)
            if ch >= 1:
                v.wait(g, c_bcst)
            c_bcev = v.op(lambda e, ch=ch: e.tensor_scalar_add(dscr[0:32, 0:512], bank[0:32, 0:512], bbc_col[0:32, 0:1]))
            g.wait(v, c_bcev)
            c_bcst = g.dma(lambda e, ch=ch: e.dma_start(out=bc_o[:, 512 * ch:512 * (ch + 1)], in_=dscr[0:32, 0:512]))

        g.wait(q, q.n)
        g.wait(g, g.n)
        g.op(lambda e: e.memset(A32[0:1, 0:1], 0.0))

        S.finalize(nc, stack)
    return nc


def prep_pre1(inp, x1_full):
    gb = np.stack([_f32(inp["L1_attn_ln1_g"]), _f32(inp["L1_attn_ln1_b"])], axis=1)
    wdp = pack_lhsT(_f32(inp["L1_attn_Wd"]).T)
    wbcp = pack_lhsT(_f32(inp["L1_attn_Wbc"]).T, octile=32)
    bd = _f32(inp["L1_attn_bd"]).reshape(CM, 1)
    bbc = _f32(inp["L1_attn_bbc"]).reshape(32, 1)
    ones = np.ones((128, 128), dtype=BF)
    maps = []
    for c in range(NCORE):
        b, th = c // 2, c % 2
        sl = slice(TC * th, TC * (th + 1))
        maps.append({
            "x1_i": np.ascontiguousarray(x1_full[b][:, sl]),
            "gb_i": gb, "wds": wdp, "bd_i": bd, "wbcs": wbcp, "bbc_i": bbc,
        })
    return maps


def run_pre1(inp, x1_full):
    nc = build_pre1()
    res = run_bass_kernel_spmd(nc, prep_pre1(inp, x1_full), core_ids=list(range(NCORE)))
    out = {}
    for name, dt in (("xn1", BF), ("zn", BF), ("dl", np.float32), ("xd", np.float32)):
        arr = np.empty((BD, CM, SQ), dtype=dt)
        for c in range(NCORE):
            b, th = c // 2, c % 2
            arr[b][:, TC * th:TC * (th + 1)] = np.asarray(res.results[c][name + "_o"])
        out[name] = arr
    bc = np.empty((BD, 32, SQ), dtype=np.float32)
    for c in range(NCORE):
        b, th = c // 2, c % 2
        bc[b][:, TC * th:TC * (th + 1)] = np.asarray(res.results[c]["bc_o"])
    out["bc"] = bc
    return out



def build_scan1():
    nc = bass.Bass("TRN2", target_bir_lowering=False)
    dl_i = nc.dram_tensor("dl_i", [512, SQ], F32, kind="ExternalInput").ap()
    xd_i = nc.dram_tensor("xd_i", [512, SQ], F32, kind="ExternalInput").ap()
    bc_i = nc.dram_tensor("bc_i", [32, SQ], F32, kind="ExternalInput").ap()
    alog_i = nc.dram_tensor("alog_i", [512, 16], F32, kind="ExternalInput").ap()
    esel_i = nc.dram_tensor("esel_i", [32, 4096], F32, kind="ExternalInput").ap()
    y_o = nc.dram_tensor("y_o", [512, SQ], F32, kind="ExternalOutput").ap()

    S = Sch()
    q, t, v, s, g = (S.q[e] for e in ENGS)

    with ExitStack() as stack:
        A32 = stack.enter_context(nc.sbuf_tensor("a32", [128, 37100], F32))
        pb = [stack.enter_context(nc.psum_tensor(f"pb{i}", [128, 512], F32)) for i in range(4)]

        dl = A32[:, 0:4096]
        xd = A32[:, 4096:8192]
        dab = A32[:, 8192:12288]
        dbx = A32[:, 12288:16384]
        hbuf = A32[:, 16384:20480]
        tmp = A32[:, 20480:24576]
        ybuf = A32[:, 24576:28672]
        alog = [A32[:, 28672 + 16 * i:28688 + 16 * i] for i in range(4)]
        aneg = [A32[:, 28736 + 16 * i:28752 + 16 * i] for i in range(4)]

        bcb = A32[:, 28900:32996]
        esel = A32[:, 32996:37092]

        q.dma(lambda e: e.dma_start(out=esel[0:32, 0:4096], in_=esel_i[:]))
        c_bc = q.dma(lambda e: e.dma_start(out=bcb[0:32, 0:4096], in_=bc_i[:]))
        for ct in range(4):
            q.dma(lambda e, ct=ct: e.dma_start(out=alog[ct], in_=alog_i[128 * ct:128 * (ct + 1), :]))
        c_al = q.n
        q.wait(q, c_al)
        v.wait(q, c_al)
        c_bcb = v.op(lambda e: e.memset(ybuf[0:1, 0:1], 0.0))
        s.wait(q, c_al)
        for ct in range(4):
            s.quiet(lambda e, ct=ct: e.activation(aneg[ct], alog[ct], AF.Exp))
        c_an = s.op(lambda e: e.activation(aneg[0], alog[0], AF.Exp))
        v.wait(s, c_an)
        for ct in range(4):
            v.quiet(lambda e, ct=ct: e.tensor_scalar_mul(aneg[ct], aneg[ct], -1.0))
        c_a = v.op(lambda e: e.memset(ybuf[0:1, 0:1], 0.0))

        c_dA_prev = 0
        c_scan_prev = 0
        c_ydma_prev = 0
        c_dl_used = (0, 0)
        for ct in range(4):
            q.wait(s, c_dl_used[0])
            q.wait(v, c_dl_used[1])
            c_dl = q.dma(lambda e, ct=ct: e.dma_start(out=dl, in_=dl_i[128 * ct:128 * (ct + 1), :]))
            c_xd = q.dma(lambda e, ct=ct: e.dma_start(out=xd, in_=xd_i[128 * ct:128 * (ct + 1), :]))
            for n in range(16):
                s.wait(q, c_xd)
                if c_scan_prev:
                    s.wait(v, c_scan_prev)
                c_dA = s.op(lambda e, ct=ct, n=n: e.activation(dab, dl, AF.Exp, scale=aneg[ct][:, n:n + 1]))
                tok_b = {}
                for jj in range(8):
                    bank = pb[jj % 2]
                    if jj >= 2:
                        t.wait(v, tok_b[jj - 2])
                    elif c_scan_prev:
                        t.wait(v, c_scan_prev)
                    if ct == 0 and n == 0 and jj == 0:
                        t.wait(v, c_bcb)
                    c_mb = t.op(lambda e, n=n, jj=jj, bank=bank: e.matmul(
                        bank[:, 0:512], esel[0:32, 128 * n:128 * (n + 1)], bcb[0:32, 512 * jj:512 * (jj + 1)],
                        start=True, stop=True))
                    v.wait(t, c_mb)
                    v.wait(q, c_xd)
                    tok_b[jj] = v.op(lambda e, jj=jj, bank=bank: e.tensor_tensor(
                        dbx[:, 512 * jj:512 * (jj + 1)], xd[:, 512 * jj:512 * (jj + 1)], bank[:, 0:512], ALU.mult))
                v.wait(s, c_dA)
                c_scan = v.op(lambda e: e.tensor_tensor_scan(hbuf, dab, dbx, 0.0, ALU.mult, ALU.add))
                tok_c = {}
                for jj in range(8):
                    bank = pb[2 + jj % 2]
                    if jj >= 2:
                        t.wait(v, tok_c[jj - 2])
                    c_mc = t.op(lambda e, n=n, jj=jj, bank=bank: e.matmul(
                        bank[:, 0:512], esel[0:32, 128 * (16 + n):128 * (17 + n)], bcb[0:32, 512 * jj:512 * (jj + 1)],
                        start=True, stop=True))
                    v.wait(t, c_mc)
                    if n == 0 and ct >= 0:
                        if c_ydma_prev:
                            v.wait(q, c_ydma_prev)
                        dst = ybuf[:, 512 * jj:512 * (jj + 1)]
                    else:
                        dst = tmp[:, 512 * jj:512 * (jj + 1)]
                    tok_c[jj] = v.op(lambda e, jj=jj, bank=bank, dst=dst: e.tensor_tensor(
                        dst, hbuf[:, 512 * jj:512 * (jj + 1)], bank[:, 0:512], ALU.mult))
                if n > 0:
                    c_y = v.op(lambda e: e.tensor_tensor(ybuf, ybuf, tmp, ALU.add))
                c_scan_prev = c_scan
                c_dA_prev = c_dA
            q.wait(v, v.n)
            c_ydma_prev = q.dma(lambda e, ct=ct: e.dma_start(out=y_o[128 * ct:128 * (ct + 1), :], in_=ybuf))
            q.wait(q, c_ydma_prev)
            c_dl_used = (c_dA_prev, v.n)

        g.wait(q, q.n)
        g.op(lambda e: e.memset(A32[0:1, 0:1], 0.0))
        S.finalize(nc, stack)
    return nc


def prep_scan1(pre, inp):
    alog = _f32(inp["L1_attn_A_log"])
    esel = np.zeros((32, 4096), dtype=np.float32)
    for n in range(32):
        esel[n, 128 * n:128 * (n + 1)] = 1
    maps = []
    for c in range(NCORE):
        b, chalf = c // 2, c % 2
        rows = slice(512 * chalf, 512 * (chalf + 1))
        maps.append({
            "dl_i": np.ascontiguousarray(pre["dl"][b][rows]),
            "xd_i": np.ascontiguousarray(pre["xd"][b][rows]),
            "bc_i": np.ascontiguousarray(pre["bc"][b]),
            "alog_i": np.ascontiguousarray(alog[rows]),
            "esel_i": esel,
        })
    return maps


def run_scan1(pre, inp):
    nc = build_scan1()
    res = run_bass_kernel_spmd(nc, prep_scan1(pre, inp), core_ids=list(range(NCORE)))
    y = np.empty((BD, CM, SQ), dtype=np.float32)
    for c in range(NCORE):
        b, chalf = c // 2, c % 2
        y[b][512 * chalf:512 * (chalf + 1)] = np.asarray(res.results[c]["y_o"])
    return y



def build_post1(dbg=False):
    nc = bass.Bass("TRN2", target_bir_lowering=False)
    xn1_i = nc.dram_tensor("xn1_i", [CM, TC], BF16, kind="ExternalInput").ap()
    zn_i = nc.dram_tensor("zn_i", [CM, TC], BF16, kind="ExternalInput").ap()
    y_i = nc.dram_tensor("y_i", [CM, TC], F32, kind="ExternalInput").ap()
    dd_i = nc.dram_tensor("dd_i", [CM, 1], F32, kind="ExternalInput").ap()
    gb_i = nc.dram_tensor("gb_i", [CM, 2], F32, kind="ExternalInput").ap()
    b1_i = nc.dram_tensor("b1_i", [128, 32], F32, kind="ExternalInput").ap()
    b2_i = nc.dram_tensor("b2_i", [128, 8], F32, kind="ExternalInput").ap()
    w1s = nc.dram_tensor("w1s", [32, 128, 1024], F32, kind="ExternalInput").ap()
    w2s = nc.dram_tensor("w2s", [8, 128, 4096], F32, kind="ExternalInput").ap()
    onesall = nc.dram_tensor("onesall", [128, 128], BF16, kind="ExternalInput").ap()
    s1_o = nc.dram_tensor("s1_o", [CM, TC], BF16, kind="ExternalOutput").ap()
    ssm32_h = nc.dram_tensor("ssm32_h", [CM, TC], F32, kind="Internal").ap()

    S = Sch()
    q, t, v, s, g = (S.q[e] for e in ENGS)
    G = 1024

    with ExitStack() as stack:
        A16 = stack.enter_context(nc.sbuf_tensor("a16", [128, 80600], BF16))
        A32 = stack.enter_context(nc.sbuf_tensor("a32", [128, 7000], F32))
        pb = [stack.enter_context(nc.psum_tensor(f"pb{i}", [128, 512], F32)) for i in range(8)]

        def b16(off, n):
            return A16[:, off:off + n]

        xn1 = [b16(2048 * i, 2048) for i in range(8)]
        zn = [b16(16384 + 2048 * i, 2048) for i in range(8)]
        hff = [b16(1024 * j, 1024) for j in range(32)]
        ssm = [b16(32768 + 2048 * i, 2048) for i in range(8)]
        nrm = [b16(49152 + 2048 * i, 2048) for i in range(8)]
        wcast = [b16(65536 + 4096 * i, 4096) for i in range(2)]
        abc = b16(73728, 2048)
        cbc = b16(75776, 2048)
        xsq = b16(77824, 2048)
        ones_sb = b16(79872, 128)
        sst = b16(80000, 512)
        a_rowb = A16[0:1, 49152:51200]
        c_rowb = A16[0:1, 51200:53248]

        wraw = A32[:, 0:4096]
        yscr = A32[:, 4096:6144]
        dcol = A32[:, 6144:6145]
        gbc = A32[:, 6148:6164]
        b1c = A32[:, 6164:6196]
        b2c = A32[:, 6196:6204]
        c_lneps = A32[:, 6204:6205]
        c_mhalf = A32[:, 6205:6206]
        srow = A32[0:1, 0:2048]
        qrow = A32[0:1, 2048:4096]
        tmp_row = A32[0:1, 4096:6144]

        q.dma(lambda e: e.dma_start(out=ones_sb, in_=onesall[:]))
        q.dma(lambda e: e.dma_start(out=b1c, in_=b1_i[:]))
        q.dma(lambda e: e.dma_start(out=b2c, in_=b2_i[:]))
        for ct in range(8):
            q.dma(lambda e, ct=ct: e.dma_start(out=gbc[:, 2 * ct:2 * ct + 2], in_=gb_i[128 * ct:128 * (ct + 1), :]))
        for ct in range(8):
            q.dma(lambda e, ct=ct: e.dma_start(out=xn1[ct], in_=xn1_i[128 * ct:128 * (ct + 1), :]))
        for ct in range(8):
            q.dma(lambda e, ct=ct: e.dma_start(out=zn[ct], in_=zn_i[128 * ct:128 * (ct + 1), :]))
        c_in = q.n
        q.wait(q, c_in)

        v.wait(q, c_in)
        v.quiet(lambda e: e.memset(c_lneps, LN_EPS))
        c_init = v.op(lambda e: e.memset(c_mhalf, -0.5))

        ddc = A32[:, 6208:6216]
        for ct in range(8):
            q.dma(lambda e, ct=ct: e.dma_start(out=ddc[:, ct:ct + 1], in_=dd_i[128 * ct:128 * (ct + 1), :]))
        c_dd = q.n
        c_y_used = 0
        q.wait(q, c_dd)
        for ct in range(8):
            q.wait(v, c_y_used)
            c_y = q.dma(lambda e, ct=ct: e.dma_start(out=yscr, in_=y_i[128 * ct:128 * (ct + 1), :]))
            v.wait(q, c_y)
            if ct == 0:
                v.wait(q, c_dd)
            v.quiet(lambda e, ct=ct: e.scalar_tensor_tensor(yscr, zn[ct], ddc[:, ct:ct + 1], yscr, ALU.mult, ALU.add))
            c_f32 = v.op(lambda e, ct=ct: e.tensor_tensor(yscr, yscr, xn1[ct], ALU.add))
            v.wait(v, c_f32)
            c_y_used = v.op(lambda e, ct=ct: e.tensor_copy(ssm[ct], yscr))
        c_ssm = c_y_used

        v.quiet(lambda e: e.memset(srow, 0.0))
        c_z0 = v.op(lambda e: e.memset(qrow, 0.0))
        ev_tok = {}
        c_sq_prev = 0
        for ct in range(8):
            if ct == 0:
                s.wait(v, c_z0)
            s.wait(t, c_sq_prev)
            c_sqr = s.op(lambda e, ct=ct: e.activation(xsq, ssm[ct], AF.Square))
            t.wait(s, c_sqr)
            for j in range(4):
                m = ct * 4 + j
                bs_, bq_ = pb[j % 2], pb[2 + j % 2]
                if m >= 2:
                    t.wait(v, ev_tok[m - 2])
                t.quiet(lambda e, j=j, ct=ct, bs_=bs_: e.matmul(
                    bs_[:, 0:512], ones_sb, ssm[ct][:, 512 * j:512 * (j + 1)], start=True, stop=True))
                c_mm = t.op(lambda e, j=j, bq_=bq_: e.matmul(
                    bq_[:, 0:512], ones_sb, xsq[:, 512 * j:512 * (j + 1)], start=True, stop=True))
                v.wait(t, c_mm)
                v.quiet(lambda e, j=j, bs_=bs_: e.tensor_add(srow[:, 512 * j:512 * (j + 1)], srow[:, 512 * j:512 * (j + 1)], bs_[0:1, 0:512]))
                ev_tok[m] = v.op(lambda e, j=j, bq_=bq_: e.tensor_add(qrow[:, 512 * j:512 * (j + 1)], qrow[:, 512 * j:512 * (j + 1)], bq_[0:1, 0:512]))
            c_sq_prev = t.n
        v.quiet(lambda e: e.tensor_scalar_mul(srow, srow, 1.0 / CM))
        v.quiet(lambda e: e.tensor_scalar_mul(qrow, qrow, 1.0 / CM))
        v.quiet(lambda e: e.tensor_tensor(tmp_row, srow, srow, ALU.mult))
        c_var = v.op(lambda e: e.tensor_tensor(qrow, qrow, tmp_row, ALU.subtract))
        s.wait(v, c_var)
        s.quiet(lambda e: e.activation(qrow, qrow, AF.Ln, bias=c_lneps[0:1, 0:1]))
        c_rstd = s.op(lambda e: e.activation(qrow, qrow, AF.Exp, scale=c_mhalf[0:1, 0:1]))
        v.wait(s, c_rstd)
        v.quiet(lambda e: e.tensor_tensor(tmp_row, srow, qrow, ALU.mult))
        v.quiet(lambda e: e.tensor_copy(a_rowb, qrow))
        c_rowsb = v.op(lambda e: e.tensor_copy(c_rowb, tmp_row))
        t.wait(v, c_rowsb)
        for j in range(4):
            for src, dst in ((a_rowb, abc), (c_rowb, cbc)):
                c_b1 = t.op(lambda e, j=j, src=src: e.matmul(
                    pb[4][:, 0:512], ones_sb[0:1, 0:128], src[0:1, 512 * j:512 * (j + 1)], start=True, stop=True))
                s.wait(t, c_b1)
                c_e1 = s.op(lambda e, j=j, dst=dst: e.copy(dst[:, 512 * j:512 * (j + 1)], pb[4][:, 0:512]))
                t.wait(s, c_e1)
        c_bcast = s.n
        v.wait(s, c_bcast)
        c_nrm = 0
        for ct in range(8):
            v.quiet(lambda e, ct=ct: e.tensor_tensor(nrm[ct], ssm[ct], abc, ALU.mult))
            v.quiet(lambda e, ct=ct: e.tensor_tensor(nrm[ct], nrm[ct], cbc, ALU.subtract))
            c_nrm = v.op(lambda e, ct=ct: e.tensor_scalar(
                nrm[ct], nrm[ct], gbc[:, 2 * ct:2 * ct + 1], gbc[:, 2 * ct + 1:2 * ct + 2], ALU.mult, ALU.add))

        c_cast = 0
        G2 = 512
        hff2 = [b16(32768 + 512 * j, 512) for j in range(32)]
        c_store_prev = 0
        c_s1_prev = 0
        c_prev_phase_mm = 0
        for gi in range(4):
            gs = G2 * gi
            toks, evs = [], []
            for j in range(32):
                par = j % 2
                q.wait(v, c_cast)
                if gi == 0 and j == 0:
                    q.wait(v, c_rowsb)
                c_w = q.dma(lambda e, j=j: e.dma_start(out=wraw[:, 0:1024], in_=w1s[j]))
                v.wait(q, c_w)
                if len(toks) >= 2:
                    v.wait(t, toks[-2])
                elif j == 0:
                    v.wait(t, c_prev_phase_mm)
                c_cast = v.op(lambda e, par=par: e.tensor_copy(wcast[par][:, 0:1024], wraw[:, 0:1024]))
                t.wait(v, c_cast)
                if gi == 0 and j == 0:
                    t.wait(v, c_nrm)
                bank = pb[4 + par]
                if len(evs) >= 2:
                    t.wait(s, evs[-2])
                elif gi > 0:
                    t.wait(v, c_s1_prev)
                for k in range(8):
                    fn = lambda e, k=k, par=par, gs=gs, bank=bank: e.matmul(
                        bank[:, 0:512], wcast[par][:, 128 * k:128 * (k + 1)],
                        nrm[k][:, gs:gs + G2], start=(k == 0), stop=(k == 7))
                    toks.append(t.op(fn)) if k == 7 else t.quiet(fn)
                s.wait(t, toks[-1])
                if gi == 0 and j == 0:
                    s.wait(v, c_nrm)
                evs.append(s.op(lambda e, j=j, bank=bank: e.activation(
                    hff2[j], bank[:, 0:512], AF.Gelu, bias=b1c[:, j:j + 1])))

            toks2 = []
            for hoc in range(8):
                par = hoc % 2
                q.wait(v, c_cast)
                c_w = q.dma(lambda e, hoc=hoc: e.dma_start(out=wraw[:, 0:4096], in_=w2s[hoc]))
                v.wait(q, c_w)
                if len(toks2) >= 2:
                    v.wait(t, toks2[-2])
                elif hoc == 0:
                    v.wait(t, toks[-1])
                c_cast = v.op(lambda e, par=par: e.tensor_copy(wcast[par][:, 0:4096], wraw[:, 0:4096]))
                t.wait(v, c_cast)
                if hoc == 0:
                    t.wait(s, evs[-1])
                bank = pb[4 + par]
                if hoc >= 2:
                    t.wait(v, c_s1_prev)
                for k in range(32):
                    fn = lambda e, k=k, par=par, bank=bank: e.matmul(
                        bank[:, 0:512], wcast[par][:, 128 * k:128 * (k + 1)],
                        hff2[k], start=(k == 0), stop=(k == 31))
                    toks2.append(t.op(fn)) if k == 31 else t.quiet(fn)
                g.wait(v, c_s1_prev if c_s1_prev else c_nrm)
                c_yld = g.dma(lambda e, hoc=hoc, gs=gs: e.dma_start(
                    out=yscr[:, 0:512], in_=y_i[128 * hoc:128 * (hoc + 1), gs:gs + G2]))
                v.wait(g, c_yld)
                v.quiet(lambda e, hoc=hoc, gs=gs: e.scalar_tensor_tensor(
                    yscr[:, 0:512], zn[hoc][:, gs:gs + G2], ddc[:, hoc:hoc + 1], yscr[:, 0:512], ALU.mult, ALU.add))
                c_sx = v.op(lambda e, hoc=hoc, gs=gs: e.tensor_tensor(
                    yscr[:, 0:512], yscr[:, 0:512], xn1[hoc][:, gs:gs + G2], ALU.add))
                v.wait(t, toks2[-1])
                v.wait(g, c_store_prev)
                c_s1 = v.op(lambda e, hoc=hoc, gs=gs, bank=bank: e.scalar_tensor_tensor(
                    sst, bank[:, 0:512], b2c[:, hoc:hoc + 1], yscr[:, 0:512], ALU.add, ALU.add))
                c_s1_prev = c_s1
                g.wait(v, c_s1)
                c_store_prev = g.dma(lambda e, hoc=hoc, gs=gs: e.dma_start(
                    out=s1_o[128 * hoc:128 * (hoc + 1), gs:gs + G2], in_=sst))
            c_prev_phase_mm = toks2[-1]

        g.wait(q, q.n)
        g.wait(g, g.n)
        g.op(lambda e: e.memset(A32[0:1, 0:1], 0.0))
        S.finalize(nc, stack)
    return nc


def prep_post1(inp, pre, y_full):
    gb = np.stack([_f32(inp["L1_attn_ln2_g"]), _f32(inp["L1_attn_ln2_b"])], axis=1)
    w1p = pack_lhsT(_f32(inp["L1_attn_W1"]).T)
    w2p = pack_lhsT(_f32(inp["L1_attn_W2"]).T)
    b1 = _f32(inp["L1_attn_b1"]).reshape(32, 128).T
    b2 = _f32(inp["L1_attn_b2"]).reshape(8, 128).T
    dd = _f32(inp["L1_attn_D"]).reshape(CM, 1)
    ones = np.ones((128, 128), dtype=BF)
    maps = []
    for c in range(NCORE):
        b, th = c // 2, c % 2
        sl = slice(TC * th, TC * (th + 1))
        maps.append({
            "xn1_i": np.ascontiguousarray(pre["xn1"][b][:, sl]),
            "zn_i": np.ascontiguousarray(pre["zn"][b][:, sl]),
            "y_i": np.ascontiguousarray(y_full[b][:, sl]),
            "dd_i": dd, "gb_i": gb, "b1_i": b1, "b2_i": b2,
            "w1s": w1p, "w2s": w2p, "onesall": ones,
        })
    return maps


def run_post1(inp, pre, y_full, dbg=False):
    nc = build_post1(dbg)
    res = run_bass_kernel_spmd(nc, prep_post1(inp, pre, y_full), core_ids=list(range(NCORE)))
    s1 = np.empty((BD, CM, SQ), dtype=BF)
    for c in range(NCORE):
        b, th = c // 2, c % 2
        s1[b][:, TC * th:TC * (th + 1)] = np.asarray(res.results[c]["s1_o"])
    if dbg:
        return s1, res
    return s1



def build_moe1():
    nc = bass.Bass("TRN2", target_bir_lowering=False)
    xn1_i = nc.dram_tensor("xn1_i", [CM, TC], BF16, kind="ExternalInput").ap()
    s1_i = nc.dram_tensor("s1_i", [CM, TC], BF16, kind="ExternalInput").ap()
    x1_i = nc.dram_tensor("x1_i", [CM, TC], F32, kind="ExternalInput").ap()
    nw_i = nc.dram_tensor("nw_i", [CM, 1], F32, kind="ExternalInput").ap()
    gw_i = nc.dram_tensor("gw_i", [CM, 8], BF16, kind="ExternalInput").ap()
    gb_i = nc.dram_tensor("gb_i", [1, 8], BF16, kind="ExternalInput").ap()
    m1s = nc.dram_tensor("m1s", [8, 16, 128, 1024], BF16, kind="ExternalInput").ap()
    b1s = nc.dram_tensor("b1s", [8, 128, 16], F32, kind="ExternalInput").ap()
    m2s = nc.dram_tensor("m2s", [8, 8, 128, 1024], BF16, kind="ExternalInput").ap()
    b2lh = nc.dram_tensor("b2lh", [8, 8, 128], BF16, kind="ExternalInput").ap()
    eye = nc.dram_tensor("eye", [128, 128], BF16, kind="ExternalInput").ap()
    esel8_i = nc.dram_tensor("esel8_i", [8, 1024], BF16, kind="ExternalInput").ap()
    onesall = nc.dram_tensor("onesall", [128, 128], BF16, kind="ExternalInput").ap()
    out_o = nc.dram_tensor("out_o", [CM, TC], F32, kind="ExternalOutput").ap()

    S = Sch()
    q, t, v, s, g = (S.q[e] for e in ENGS)
    NCH, CH = 4, 512

    with ExitStack() as stack:
        A16 = stack.enter_context(nc.sbuf_tensor("a16", [128, 69500], BF16))
        A32 = stack.enter_context(nc.sbuf_tensor("a32", [128, 17100], F32))
        pb = [stack.enter_context(nc.psum_tensor(f"pb{i}", [128, 512], F32)) for i in range(7)]
        pbt = stack.enter_context(nc.psum_tensor("pbt", [128, 1024], BF16))

        def b16(off, n):
            return A16[:, off:off + n]

        xn1 = [b16(2048 * i, 2048) for i in range(8)]
        nf = [b16(16384 + 2048 * i, 2048) for i in range(8)]
        s1 = [b16(32768 + 2048 * i, 2048) for i in range(8)]
        act = [b16(49152 + 512 * i, 512) for i in range(8)]
        wg_b = [b16(53248 + 1024 * i, 1024) for i in range(2)]
        wl_b = [b16(55296 + 1024 * i, 1024) for i in range(2)]
        w2_b = [b16(57344 + 1024 * i, 1024) for i in range(2)]
        gwc = b16(59392, 64)
        comb = b16(59456, 2048)
        b2a = b16(61504, 1024)
        lgt = b16(62528, 8)
        combT = b16(62536, 8)
        sndT = b16(62544, 8)
        arow = A16[0:1, 62552:64600]
        gbrow = b16(64600, 8)
        ones_sb = b16(64608, 128)
        eye_sb = b16(64736, 128)
        sgt = b16(64864, 512)
        lint = b16(65376, 512)
        sst = b16(65888, 512)
        mot = b16(66400, 2048)
        esel8 = b16(68448, 1024)

        macc = [A32[:, 2048 * i:2048 * (i + 1)] for i in range(8)]
        b1all = A32[:, 16384:16512]
        nwcs = A32[:, 16512:16520]
        c_rmseps = A32[:, 16520:16521]
        c_mhalf = A32[:, 16521:16522]
        c_1702 = A32[:, 16522:16523]
        m1col = A32[:, 16524:16525]
        m2col = A32[:, 16525:16526]
        fnd = A32[:, 16526:16527]
        eqc = A32[:, 16527:16528]
        w1col = A32[:, 16528:16529]
        w2col = A32[:, 16529:16530]
        sc1 = A32[:, 16530:16531]
        xst = A32[:, 16536:17048]
        srow = A32[0:1, 0:2048]

        q.dma(lambda e: e.dma_start(out=ones_sb, in_=onesall[:]))
        q.dma(lambda e: e.dma_start(out=eye_sb, in_=eye[:]))
        q.dma(lambda e: e.dma_start(out=esel8[0:8, :], in_=esel8_i[:]))
        q.dma(lambda e: e.dma_start(out=gbrow[0:1, 0:8], in_=gb_i[:]))
        for kc in range(8):
            q.dma(lambda e, kc=kc: e.dma_start(out=gwc[:, 8 * kc:8 * (kc + 1)], in_=gw_i[128 * kc:128 * (kc + 1), :]))
        for ct in range(8):
            q.dma(lambda e, ct=ct: e.dma_start(out=nwcs[:, ct:ct + 1], in_=nw_i[128 * ct:128 * (ct + 1), :]))
        for ex in range(8):
            q.dma(lambda e, ex=ex: e.dma_start(out=b1all[:, 16 * ex:16 * (ex + 1)], in_=b1s[ex]))
        for hoc in range(8):
            q.dma(lambda e, hoc=hoc: e.dma_start(out=b2a[0:8, 128 * hoc:128 * (hoc + 1)], in_=b2lh[hoc]))
        for ct in range(8):
            q.dma(lambda e, ct=ct: e.dma_start(out=xn1[ct], in_=xn1_i[128 * ct:128 * (ct + 1), :]))
        c_in = q.n
        q.wait(q, c_in)

        def b1c(ex, jj):
            return b1all[:, 16 * ex + jj:16 * ex + jj + 1]

        v.wait(q, c_in)
        v.quiet(lambda e: e.memset(c_rmseps, LN_EPS))
        v.quiet(lambda e: e.memset(c_mhalf, -0.5))
        v.quiet(lambda e: e.memset(c_1702, 1.702))
        v.quiet(lambda e: e.memset(srow, 0.0))
        c_init = v.op(lambda e: e.memset(fnd, 0.0))

        xsq16 = b16(66400, 2048)
        ev_tok = {}
        c_sq_prev = 0
        for ct in range(8):
            if ct == 0:
                s.wait(v, c_init)
            s.wait(t, c_sq_prev)
            c_sqr = s.op(lambda e, ct=ct: e.activation(xsq16, xn1[ct], AF.Square))
            t.wait(s, c_sqr)
            for j in range(4):
                m = ct * 4 + j
                bq_ = pb[m % 2]
                if m >= 2:
                    t.wait(v, ev_tok[m - 2])
                c_mm = t.op(lambda e, j=j, bq_=bq_: e.matmul(
                    bq_[:, 0:512], ones_sb, xsq16[:, 512 * j:512 * (j + 1)], start=True, stop=True))
                v.wait(t, c_mm)
                ev_tok[m] = v.op(lambda e, j=j, bq_=bq_: e.tensor_add(srow[:, 512 * j:512 * (j + 1)], srow[:, 512 * j:512 * (j + 1)], bq_[0:1, 0:512]))
            c_sq_prev = t.n
        ones32m = A32[:, 2048:2176]
        c_sr = v.op(lambda e: e.tensor_scalar_mul(srow, srow, 1.0 / CM))
        c_o32 = v.op(lambda e: e.memset(ones32m, 1.0))
        s.wait(v, c_sr)
        s.quiet(lambda e: e.activation(srow, srow, AF.Ln, bias=c_rmseps[0:1, 0:1]))
        c_rstd = s.op(lambda e: e.activation(srow, srow, AF.Exp, scale=c_mhalf[0:1, 0:1]))
        t.wait(s, c_rstd)
        t.wait(v, c_o32)
        c_nf = 0
        for j in range(4):
            c_b1 = t.op(lambda e, j=j: e.matmul(
                pb[4][:, 0:512], ones32m[0:1, 0:128], srow[0:1, 512 * j:512 * (j + 1)], start=True, stop=True))
            v.wait(t, c_b1)
            for ct in range(8):
                c_nf = v.op(lambda e, ct=ct, j=j: e.scalar_tensor_tensor(
                    nf[ct][:, 512 * j:512 * (j + 1)], xn1[ct][:, 512 * j:512 * (j + 1)],
                    nwcs[:, ct:ct + 1], pb[4][:, 0:512], ALU.mult, ALU.mult))
            t.wait(v, c_nf)

        def vs(fn):
            tok = v.op(fn)
            v.wait(v, tok)
            return tok

        c_combT = 0
        for tt in range(16):
            if tt == 0:
                t.wait(v, c_nf)
            for kc in range(8):
                fn = lambda e, kc=kc, tt=tt: e.matmul(
                    pb[5][:, 0:8], nf[kc][:, 128 * tt:128 * (tt + 1)], gwc[:, 8 * kc:8 * (kc + 1)],
                    start=(kc == 0), stop=False)
                t.quiet(fn)
            c_lg = t.op(lambda e: e.matmul(pb[5][:, 0:8], ones_sb[0:1, 0:128], gbrow[0:1, 0:8], start=False, stop=True))
            v.wait(t, c_lg)
            vs(lambda e: e.tensor_copy(lgt[:, 0:8], pb[5][:, 0:8]))
            c_lgdone = v.n
            vs(lambda e: e.tensor_reduce(m1col, lgt[:, 0:8], AX.X, ALU.max))
            vs(lambda e: e.memset(fnd, 0.0))
            for ee in range(8):
                vs(lambda e, ee=ee: e.tensor_tensor(eqc, lgt[:, ee:ee + 1], m1col, ALU.is_ge))
                vs(lambda e, ee=ee: e.scalar_tensor_tensor(eqc, fnd, -1.0, eqc, ALU.mult, ALU.add))
                vs(lambda e, ee=ee: e.tensor_scalar_max(eqc, eqc, 0.0))
                vs(lambda e, ee=ee: e.tensor_copy(combT[:, ee:ee + 1], eqc))
                vs(lambda e, ee=ee: e.tensor_add(fnd, fnd, eqc))
            vs(lambda e: e.scalar_tensor_tensor(sgt[:, 0:8], combT[:, 0:8], -60000.0, lgt[:, 0:8], ALU.mult, ALU.add))
            vs(lambda e: e.tensor_reduce(m2col, sgt[:, 0:8], AX.X, ALU.max))
            vs(lambda e: e.memset(fnd, 0.0))
            for ee in range(8):
                vs(lambda e, ee=ee: e.tensor_tensor(eqc, sgt[:, ee:ee + 1], m2col, ALU.is_ge))
                vs(lambda e, ee=ee: e.scalar_tensor_tensor(eqc, fnd, -1.0, eqc, ALU.mult, ALU.add))
                vs(lambda e, ee=ee: e.tensor_scalar_max(eqc, eqc, 0.0))
                vs(lambda e, ee=ee: e.tensor_copy(sndT[:, ee:ee + 1], eqc))
                vs(lambda e, ee=ee: e.tensor_add(fnd, fnd, eqc))
            c_d12 = vs(lambda e: e.tensor_tensor(sc1, m1col, m2col, ALU.subtract))
            s.wait(v, c_d12)
            c_w1 = s.op(lambda e: e.activation(w1col, sc1, AF.Sigmoid))
            v.wait(s, c_w1)
            vs(lambda e: e.tensor_scalar(w2col, w1col, -1.0, 1.0, ALU.mult, ALU.add))
            for ee in range(8):
                vs(lambda e, ee=ee: e.tensor_scalar_mul(eqc, sndT[:, ee:ee + 1], w2col[:, 0:1]))
                vs(lambda e, ee=ee: e.scalar_tensor_tensor(combT[:, ee:ee + 1], combT[:, ee:ee + 1], w1col[:, 0:1], eqc, ALU.mult, ALU.add))
            c_cT = v.n
            t.wait(v, c_cT)
            c_tp = t.op(lambda e, tt=tt: e.transpose(pbt[0:8, 0:128], combT[:, 0:8], eye_sb))
            s.wait(t, c_tp)
            c_combT = s.op(lambda e, tt=tt: e.copy(comb[0:8, 128 * tt:128 * (tt + 1)], pbt[0:8, 0:128]))
            t.wait(s, c_combT)
        c_comb = c_combT

        act2 = [b16(32768 + 2048 * i, 2048) for i in range(8)]
        c_z = 0
        for ct in range(8):
            c_z = v.op(lambda e, ct=ct: e.memset(macc[ct], 0.0))
        wtoks = {"g": {}, "l": {}, "2": {}}
        c_acc_tok = c_z
        c_wprev = 0
        c_awlast = c_z
        for ex in range(8):
            c_ep = 0
            for j in range(8):
                par = j % 2
                q.wait(t, wtoks["g"].get(par, 0))
                q.wait(q, c_wprev)
                c_wg = q.dma(lambda e, ex=ex, j=j, par=par: e.dma_start(out=wg_b[par], in_=m1s[ex, j]))
                q.wait(t, wtoks["l"].get(par, 0))
                q.wait(q, c_wg)
                c_wl = q.dma(lambda e, ex=ex, j=j, par=par: e.dma_start(out=wl_b[par], in_=m1s[ex, 8 + j]))
                c_wprev = c_wl
                t.wait(q, c_wl)
                for chx in range(NCH):
                    cs_ = CH * chx
                    t.wait(v, c_ep if c_ep else c_acc_tok)
                    for k in range(8):
                        fn = lambda e, k=k, par=par, cs_=cs_: e.matmul(
                            pb[4][:, 0:512], wg_b[par][:, 128 * k:128 * (k + 1)], nf[k][:, cs_:cs_ + 512],
                            start=(k == 0), stop=(k == 7))
                        wtoks["g"][par] = t.op(fn) if k == 7 else t.quiet(fn)
                    for k in range(8):
                        fn = lambda e, k=k, par=par, cs_=cs_: e.matmul(
                            pb[5][:, 0:512], wl_b[par][:, 128 * k:128 * (k + 1)], nf[k][:, cs_:cs_ + 512],
                            start=(k == 0), stop=(k == 7))
                        wtoks["l"][par] = t.op(fn) if k == 7 else t.quiet(fn)
                    c_p1 = wtoks["l"][par]
                    v.wait(t, c_p1)
                    c_gm = v.op(lambda e, ex=ex, j=j: e.tensor_scalar(
                        sgt, pb[4][:, 0:512], b1c(ex, j), 7.0, ALU.add, ALU.min))
                    s.wait(v, c_gm)
                    c_sg = s.op(lambda e: e.activation(lint, sgt, AF.Sigmoid, scale=c_1702))
                    c_lin = v.op(lambda e, ex=ex, j=j: e.tensor_scalar(
                        sst, pb[5][:, 0:512], b1c(ex, 8 + j), 7.0, ALU.add, ALU.min))
                    v.wait(v, c_lin)
                    v.quiet(lambda e: e.tensor_scalar(sst, sst, -7.0, 1.0, ALU.max, ALU.add))
                    v.wait(s, c_sg)
                    v.quiet(lambda e: e.tensor_tensor(sgt, sgt, lint, ALU.mult))
                    c_ep = v.op(lambda e, j=j, cs_=cs_: e.tensor_tensor(
                        act2[j][:, cs_:cs_ + 512], sgt, sst, ALU.mult))
            c_aw = c_ep
            for chx in range(NCH):
                cs_ = CH * chx
                if chx == 0 and ex == 0:
                    t.wait(s, c_comb)
                t.wait(v, c_aw)
                c_cb = t.op(lambda e, ex=ex, cs_=cs_: e.matmul(
                    pb[6][:, 0:512], esel8[0:8, 128 * ex:128 * (ex + 1)], comb[0:8, cs_:cs_ + 512],
                    start=True, stop=True))
                v.wait(t, c_cb)
                for j in range(8):
                    c_aw = v.op(lambda e, j=j, cs_=cs_: e.tensor_tensor(
                        act2[j][:, cs_:cs_ + 512], act2[j][:, cs_:cs_ + 512], pb[6][:, 0:512], ALU.mult))
            c_awlast = c_aw
            for hoc in range(8):
                par = hoc % 2
                q.wait(t, wtoks["2"].get(par, 0))
                q.wait(q, c_wprev)
                c_w2d = q.dma(lambda e, ex=ex, hoc=hoc, par=par: e.dma_start(out=w2_b[par], in_=m2s[ex, hoc]))
                c_wprev = c_w2d
                t.wait(q, c_w2d)
                for chx in range(NCH):
                    cs_ = CH * chx
                    t.wait(v, c_awlast if (hoc == 0 and chx == 0) else c_acc_tok)
                    for k in range(8):
                        fn = lambda e, k=k, par=par, cs_=cs_: e.matmul(
                            pb[4][:, 0:512], w2_b[par][:, 128 * k:128 * (k + 1)], act2[k][:, cs_:cs_ + 512],
                            start=(k == 0), stop=(k == 7))
                        wtoks["2"][par] = t.op(fn) if k == 7 else t.quiet(fn)
                    c_p2 = wtoks["2"][par]
                    v.wait(t, c_p2)
                    c_acc_tok = v.op(lambda e, hoc=hoc, cs_=cs_: e.tensor_tensor(
                        macc[hoc][:, cs_:cs_ + 512], macc[hoc][:, cs_:cs_ + 512], pb[4][:, 0:512], ALU.add))
        for chx in range(NCH):
            cs_ = CH * chx
            for hoc in range(8):
                t.wait(v, c_acc_tok)
                c_bt = t.op(lambda e, hoc=hoc, cs_=cs_: e.matmul(
                    pb[4][:, 0:512], b2a[0:8, 128 * hoc:128 * (hoc + 1)], comb[0:8, cs_:cs_ + 512],
                    start=True, stop=True))
                v.wait(t, c_bt)
                c_acc_tok = v.op(lambda e, hoc=hoc, cs_=cs_: e.tensor_tensor(
                    macc[hoc][:, cs_:cs_ + 512], macc[hoc][:, cs_:cs_ + 512], pb[4][:, 0:512], ALU.add))

        s1t = act2[0]
        c_xst_used = 0
        c_ostore = 0
        c_s1ld_used = c_acc_tok
        for ct in range(8):
            g.wait(v, c_s1ld_used)
            c_s1ld = g.dma(lambda e, ct=ct: e.dma_start(out=s1t, in_=s1_i[128 * ct:128 * (ct + 1), :]))
            v.wait(v, c_acc_tok)
            c_mo = v.op(lambda e, ct=ct: e.tensor_tensor(mot, macc[ct], xn1[ct], ALU.add))
            s.wait(v, c_mo)
            c_g1 = s.op(lambda e: e.activation(mot, mot, AF.Silu))
            v.wait(s, c_g1)
            v.wait(g, c_s1ld)
            c_s1ld_used = v.op(lambda e, ct=ct: e.tensor_tensor(mot, mot, s1t, ALU.mult))
            c_sg1 = c_s1ld_used
            for ch in range(4):
                q.wait(v, c_xst_used)
                q.wait(q, c_ostore)
                c_x1 = q.dma(lambda e, ct=ct, ch=ch: e.dma_start(
                    out=xst, in_=x1_i[128 * ct:128 * (ct + 1), 512 * ch:512 * (ch + 1)]))
                v.wait(q, c_x1)
                v.wait(v, c_sg1)
                if c_ostore:
                    v.wait(q, c_ostore)
                c_xst_used = v.op(lambda e, ch=ch: e.tensor_tensor(xst, xst, mot[:, 512 * ch:512 * (ch + 1)], ALU.add))
                q.wait(v, c_xst_used)
                c_ostore = q.dma(lambda e, ct=ct, ch=ch: e.dma_start(
                    out=out_o[128 * ct:128 * (ct + 1), 512 * ch:512 * (ch + 1)], in_=xst))

        g.wait(q, q.n)
        g.wait(g, g.n)
        g.op(lambda e: e.memset(A32[0:1, 0:1], 0.0))
        S.finalize(nc, stack)
    return nc


def prep_moe1(inp, pre, s1_full, x1_full):
    gwT = np.ascontiguousarray(np.asarray(inp["L1_ffn_gate_weight"]).T)
    gb = np.ascontiguousarray(np.asarray(inp["L1_ffn_gate_bias"]).reshape(1, 8))
    m1w = np.asarray(inp["L1_ffn_mlp1_weight"])
    m2w = np.asarray(inp["L1_ffn_mlp2_weight"])
    b1w = np.asarray(inp["L1_ffn_mlp1_bias"]).astype(np.float32)
    b2w = np.asarray(inp["L1_ffn_mlp2_bias"])
    m1p = np.empty((8, 16, 128, 1024), dtype=BF)
    b1p = np.empty((8, 128, 16), dtype=np.float32)
    m2p = np.empty((8, 8, 128, 1024), dtype=BF)
    for ex in range(8):
        gT = np.ascontiguousarray(m1w[ex, 0::2, :].T)
        lT = np.ascontiguousarray(m1w[ex, 1::2, :].T)
        m1p[ex, 0:8] = pack_lhsT(gT)
        m1p[ex, 8:16] = pack_lhsT(lT)
        b1p[ex, :, 0:8] = b1w[ex, 0::2].reshape(8, 128).T
        b1p[ex, :, 8:16] = b1w[ex, 1::2].reshape(8, 128).T
        m2p[ex] = pack_lhsT(np.ascontiguousarray(m2w[ex].T))
    b2p = np.ascontiguousarray(np.transpose(b2w.reshape(8, 8, 128), (1, 0, 2)))
    nw = _f32(inp["L1_ffn_norm_weight"]).reshape(CM, 1)
    eyeb = np.eye(128, dtype=np.float32).astype(BF)
    ones = np.ones((128, 128), dtype=BF)
    esel8 = np.zeros((8, 1024), dtype=BF)
    for ee in range(8):
        esel8[ee, 128 * ee:128 * (ee + 1)] = 1
    maps = []
    for c in range(NCORE):
        b, th = c // 2, c % 2
        sl = slice(TC * th, TC * (th + 1))
        maps.append({
            "xn1_i": np.ascontiguousarray(pre["xn1"][b][:, sl]),
            "s1_i": np.ascontiguousarray(s1_full[b][:, sl]),
            "x1_i": np.ascontiguousarray(x1_full[b][:, sl]),
            "nw_i": nw, "gw_i": gwT, "gb_i": gb,
            "m1s": m1p, "b1s": b1p, "m2s": m2p, "b2lh": b2p,
            "eye": eyeb, "onesall": ones, "esel8_i": esel8,
        })
    return maps


def run_moe1(inp, pre, s1_full, x1_full):
    nc = build_moe1()
    res = run_bass_kernel_spmd(nc, prep_moe1(inp, pre, s1_full, x1_full), core_ids=list(range(NCORE)))
    out = np.empty((BD, CM, SQ), dtype=np.float32)
    for c in range(NCORE):
        b, th = c // 2, c % 2
        out[b][:, TC * th:TC * (th + 1)] = np.asarray(res.results[c]["out_o"])
    return out



def kernel(**inputs):
    xn, sp, _ = run_attn0(inputs)
    x1 = run_ffn0(inputs, xn, sp)
    pre = run_pre1(inputs, x1)
    y = run_scan1(pre, inputs)
    s1 = run_post1(inputs, pre, y)
    out = run_moe1(inputs, pre, s1, x1)
    return np.ascontiguousarray(np.transpose(out, (0, 2, 1)))
